# version 03
# baseline (speedup 1.0000x reference)
import contextlib
import math
from bisect import bisect_left

import numpy as np
import ml_dtypes

import concourse.bass as bass
import concourse.mybir as mybir
from concourse.bass_utils import run_bass_kernel_spmd

F32 = mybir.dt.float32
BF16 = mybir.dt.bfloat16
U32 = mybir.dt.uint32
I32 = mybir.dt.int32
AF = mybir.ActivationFunctionType
OP = mybir.AluOpType
AX = mybir.AxisListType

BQ = 4
T = 2048
D = 1024
NCHUNK = D // 128
NTILE = T // 128



class _Op:
    __slots__ = ("fn", "waits", "incs", "is_dma")

    def __init__(self, fn, is_dma=False):
        self.fn = fn
        self.is_dma = is_dma
        self.waits = []
        self.incs = []


class _Dom:

    __slots__ = ("name", "sem", "ops", "inc_idx", "inc_cum", "step")

    def __init__(self, name, sem, step):
        self.name = name
        self.sem = sem
        self.ops = []
        self.inc_idx = []
        self.inc_cum = []
        self.step = step


class Buf:
    __slots__ = ("ap_", "writers", "readers", "name", "is_psum")

    def __init__(self, handle, name, is_psum=False):
        self.ap_ = handle.ap() if hasattr(handle, "ap") else handle
        self.name = name
        self.is_psum = is_psum
        self.writers = {}
        self.readers = {}

    def __getitem__(self, key):
        return self.ap_[key]

    @property
    def ap(self):
        return self.ap_[:]


ENGINES = ("pe", "act", "dve", "pool", "sp")
_ENG_ATTR = {"pe": "tensor", "act": "scalar", "dve": "vector",
             "pool": "gpsimd", "sp": "sync"}
_QUEUE_OF = {"sp": "q_sp", "act": "q_act", "pool": "q_pool", "dve": "q_dve"}


class Prog:
    def __init__(self, nc):
        self.nc = nc
        self.stack = contextlib.ExitStack()
        self.dom = {}
        for e in ENGINES:
            sem = self.stack.enter_context(nc.semaphore(f"s_{e}"))
            self.dom[e] = _Dom(e, sem, 1)
        for q in set(_QUEUE_OF.values()):
            sem = self.stack.enter_context(nc.semaphore(f"s_{q}"))
            self.dom[q] = _Dom(q, sem, 16)
        self.waited = {e: {} for e in ENGINES}
        self.bufs = []
        self._phase_stack = None
        self._pending_barrier = {}
        self._names = {}

    def _uname(self, name):
        if name in self._names:
            self._names[name] += 1
            return f"{name}__{self._names[name]}"
        self._names[name] = 0
        return name

    def sbuf(self, name, shape, dtype):
        st = self._phase_stack if self._phase_stack is not None else self.stack
        h = st.enter_context(self.nc.sbuf_tensor(self._uname(name), shape, dtype))
        b = Buf(h, name)
        self.bufs.append(b)
        return b

    def psum(self, name, shape, dtype):
        st = self._phase_stack if self._phase_stack is not None else self.stack
        h = st.enter_context(self.nc.psum_tensor(self._uname(name), shape, dtype))
        b = Buf(h, name, is_psum=True)
        self.bufs.append(b)
        return b

    def barrier(self):
        for dn, d in self.dom.items():
            if not d.ops:
                continue
            idx = len(d.ops)
            if dn in ENGINES:
                while idx > 0 and d.ops[idx - 1].is_dma:
                    idx -= 1
                if idx == 0:
                    continue
            v = self._ensure_inc(dn, idx)
            self._pending_barrier[dn] = v

    @contextlib.contextmanager
    def phase(self, name=""):
        self._phase_stack = contextlib.ExitStack()
        try:
            yield
        finally:
            self.barrier()
            self._phase_stack.close()
            self._phase_stack = None

    def dram(self, name, shape, dtype, kind="Internal"):
        h = self.nc.dram_tensor(name, shape, dtype, kind=kind)
        b = Buf(h, name)
        self.bufs.append(b)
        return b

    def _ensure_inc(self, dname, idx):
        d = self.dom[dname]
        if d.inc_idx and d.inc_idx[-1] >= idx:
            i = bisect_left(d.inc_idx, idx)
            return d.inc_cum[i]
        op = d.ops[idx - 1]
        cum = (d.inc_cum[-1] if d.inc_cum else 0) + d.step
        op.incs.append((d.sem, d.step))
        d.inc_idx.append(idx)
        d.inc_cum.append(cum)
        return cum

    def emit(self, eng, fn, reads=(), writes=(), dma_q=None, no_waw=False):
        op = _Op(fn, is_dma=bool(dma_q))
        if self._pending_barrier:
            w = self.waited[eng]
            for dn, v in self._pending_barrier.items():
                if w.get(dn, 0) < v:
                    op.waits.append((self.dom[dn].sem, v))
                    w[dn] = v
        deps = {}
        for b in reads:
            for dn, (idx, _o) in b.writers.items():
                if deps.get(dn, 0) < idx:
                    deps[dn] = idx
            if b.is_psum:
                for dn, (idx, _o) in b.readers.items():
                    if dn != eng and deps.get(dn, 0) < idx:
                        deps[dn] = idx
        for b in writes:
            for dn, (idx, _o) in b.writers.items():
                if not no_waw and deps.get(dn, 0) < idx:
                    deps[dn] = idx
            for dn, (idx, _o) in b.readers.items():
                if deps.get(dn, 0) < idx:
                    deps[dn] = idx
        mydom = _QUEUE_OF[eng] if dma_q else eng
        for dn, idx in deps.items():
            if dn == "pe" and eng == "pe":
                continue
            if dn == mydom and dn == eng:
                pass
            v = self._ensure_inc(dn, idx)
            w = self.waited[eng]
            if w.get(dn, 0) < v:
                op.waits.append((self.dom[dn].sem, v))
                w[dn] = v
        d = self.dom[mydom]
        d.ops.append(op)
        idx = len(d.ops)
        if mydom != eng:
            self.dom[eng].ops.append(op)
            self._ensure_inc(mydom, idx)
        for b in writes:
            b.writers = {mydom: (idx, op)}
            b.readers = {}
        for b in reads:
            b.readers[mydom] = (idx, op)
        return op

    def dma(self, eng, out, in_, out_buf=None, in_buf=None, no_waw=False, **kw):
        reads = [in_buf] if in_buf is not None else []
        writes = [out_buf] if out_buf is not None else []
        return self.emit(eng, lambda e: e.dma_start(out=out, in_=in_, **kw),
                         reads=reads, writes=writes, dma_q=eng, no_waw=no_waw)

    def replay(self):
        nc = self.nc
        with nc.Block() as block:
            for e in ENGINES:
                ops = self.dom[e].ops
                if not ops and e != "sp":
                    continue

                def run(engobj, ops=ops, e=e):
                    for op in ops:
                        for sem, v in op.waits:
                            engobj.wait_ge(sem, v)
                        ins = op.fn(engobj)
                        for sem, amt in op.incs:
                            ins.then_inc(sem, amt)
                    if e == "sp":
                        for q in set(_QUEUE_OF.values()):
                            dq = self.dom[q]
                            if dq.inc_cum:
                                engobj.wait_ge(dq.sem, dq.inc_cum[-1])
                getattr(block, _ENG_ATTR[e])(run)

    def close(self):
        self.stack.close()

    def finish(self):
        for q in set(_QUEUE_OF.values()):
            d = self.dom[q]
            if d.ops:
                self._ensure_inc(q, len(d.ops))
        self._round_dma_waits_to_bursts()
        self.replay()
        self.close()

    def _round_dma_waits_to_bursts(self):
        qnames = set(_QUEUE_OF.values())
        opq = {}
        for q in qnames:
            for i, op in enumerate(self.dom[q].ops):
                opq[id(op)] = (q, i + 1)
        burst_start = {}
        for e in ENGINES:
            boundary = True
            cur_start = {}
            for op in self.dom[e].ops:
                if op.waits:
                    boundary = True
                ent = opq.get(id(op))
                if ent is not None:
                    q, i = ent
                    if boundary or q not in cur_start:
                        cur_start[q] = i
                        boundary = False
                    burst_start[(q, i)] = cur_start[q]
        burst_end = {}
        for (q, i), s in burst_start.items():
            key = (q, s)
            if burst_end.get(key, 0) < i:
                burst_end[key] = i
        fix = {}
        for (q, i), s in burst_start.items():
            fix[(id(self.dom[q].sem), 16 * i)] = 16 * burst_end[(q, s)]
        for e in ENGINES:
            for op in self.dom[e].ops:
                op.waits = [(s, fix.get((id(s), v), v)) for (s, v) in op.waits]
        for (q, i), s in burst_start.items():
            if i == s and s > 1:
                self.dom[q].ops[i - 1].waits.append(
                    (self.dom[q].sem, 16 * (s - 1)))




def _make_io(nc, specs):
    io = {}
    for name, shape, dt, kind in specs:
        io[name] = Buf(nc.dram_tensor(name, shape, dt, kind=kind), name)
    return io


def _host_consts():
    c = {}
    c["ident_f32"] = np.eye(128, dtype=np.float32)
    c["ident_bf16"] = np.eye(128).astype(ml_dtypes.bfloat16)
    return c



_IO_P1 = [
    ("x", [BQ, T, D], F32, "ExternalInput"),
    ("Wq", [D, D], F32, "ExternalInput"),
    ("Wk", [D, D], F32, "ExternalInput"),
    ("Wv", [D, D], F32, "ExternalInput"),
    ("beta_w", [D, 1], F32, "ExternalInput"),
    ("beta_b", [1], F32, "ExternalInput"),
    ("norm1_w", [D], F32, "ExternalInput"),
    ("alpha_logit", [D], F32, "ExternalInput"),
    ("ident_f32", [128, 128], F32, "ExternalInput"),
    ("ident_bf16", [128, 128], BF16, "ExternalInput"),
    ("XN_FM", [BQ, NCHUNK, 128, T], F32, "ExternalOutput"),
    ("Q_FM", [BQ, NCHUNK, 128, T], BF16, "ExternalOutput"),
    ("K_FM", [BQ, NCHUNK, 128, T], BF16, "ExternalOutput"),
    ("K_TM", [BQ, NTILE, 128, D], BF16, "ExternalOutput"),
    ("V_TM", [BQ, NTILE, 128, D], F32, "ExternalOutput"),
    ("BETA", [BQ, T], F32, "ExternalOutput"),
    ("RSTD2", [BQ, T], F32, "ExternalOutput"),
    ("RSTD2i", [BQ, T], F32, "Internal"),
]

def emit_p1(p, io, nbatch=BQ, sections=('a','T','r','k','q','v','e')):
    nc = p.nc
    x_in = io["x"]
    wq_in = io["Wq"]
    wk_in = io["Wk"]
    wv_in = io["Wv"]
    bw_in = io["beta_w"]
    bb_in = io["beta_b"]
    n1_in = io["norm1_w"]
    al_in = io["alpha_logit"]
    id32_in = io["ident_f32"]
    idbf_in = io["ident_bf16"]
    XN_FM = io["XN_FM"]
    Q_FM = io["Q_FM"]
    K_FM = io["K_FM"]
    K_TM = io["K_TM"]
    V_TM = io["V_TM"]
    BETA = io["BETA"]
    RSTD2 = io["RSTD2"]
    RSTD2i = io["RSTD2i"]
    emit, dma = p.emit, p.dma

    idf = p.sbuf("idf", [128, 128], F32)
    idb = p.sbuf("idb", [128, 128], BF16)
    n1c = p.sbuf("n1c", [128, NCHUNK], F32)
    alc = p.sbuf("alc", [128, NCHUNK], F32)
    bbc = p.sbuf("bbc", [128, 1], F32)
    ones1 = p.sbuf("ones1", [1, 128], F32)
    onesT = p.sbuf("onesT", [128, 2048], F32)
    eps5 = p.sbuf("eps5", [128, 1], F32)
    eps6 = p.sbuf("eps6", [128, 1], F32)

    dma("sp", idf.ap, id32_in.ap, out_buf=idf, in_buf=id32_in)
    dma("sp", idb.ap, idbf_in.ap, out_buf=idb, in_buf=idbf_in)
    vrows = p.sbuf("vrows", [8, 128], F32)
    bwc = p.sbuf("bwc", [128, NCHUNK], F32)
    ps_t = [p.psum(f"ps_t{i}", [128, 128], F32) for i in range(2)]
    ps_v8 = ps_t[0]
    for src_in, view, dst in (
        (n1_in, n1_in.ap_.rearrange("(c p) -> c p", p=128), n1c),
        (al_in, al_in.ap_.rearrange("(c p) -> c p", p=128), alc),
        (bw_in, bw_in.ap_.rearrange("(c p) one -> c (p one)", p=128), bwc),
    ):
        dma("sp", vrows.ap, view, out_buf=vrows, in_buf=src_in)
        emit("pe", lambda e: e.transpose(ps_v8[:, 0:8], vrows.ap, idf[0:8, 0:8]),
             reads=[vrows, idf], writes=[ps_v8])
        emit("dve", lambda e, dst=dst: e.tensor_copy(out=dst.ap, in_=ps_v8[:, 0:8]),
             reads=[ps_v8], writes=[dst])
    emit("dve", lambda e: e.memset(ones1.ap, 1.0), writes=[ones1])
    emit("dve", lambda e: e.memset(eps5.ap, 1e-5), writes=[eps5])
    emit("dve", lambda e: e.memset(eps6.ap, 1e-6), writes=[eps6])
    emit("dve", lambda e: e.memset(onesT.ap, 1.0), writes=[onesT])
    emit("act", lambda e: e.activation(out=alc.ap, in_=alc.ap, func=AF.Sigmoid),
         reads=[alc], writes=[alc])

    bb1 = p.sbuf("bb1", [1, 1], F32)
    ps_bb = ps_t[0]
    dma("sp", bb1.ap, bb_in.ap_[:, None], out_buf=bb1, in_buf=bb_in)
    emit("pe", lambda e: e.matmul(ps_bb[:, 0:1], ones1.ap, bb1.ap, start=True, stop=True),
         reads=[ones1, bb1], writes=[ps_bb])
    emit("dve", lambda e: e.tensor_copy(out=bbc.ap, in_=ps_bb[:, 0:1]),
         reads=[ps_bb], writes=[bbc])

    wq = p.sbuf("wq", [128, NCHUNK * D], BF16)
    wk = p.sbuf("wk", [128, NCHUNK * D], BF16)
    wv = p.sbuf("wv", [128, NCHUNK * D], BF16)
    wb = p.sbuf("wb", [128, NCHUNK], BF16)
    wstg = p.sbuf("wstg", [128, D], F32)
    for name, win, wout in (("q", wq_in, wq), ("k", wk_in, wk), ("v", wv_in, wv)):
        for c in range(NCHUNK):
            dma("sp", wstg.ap, win.ap_[c * 128:(c + 1) * 128, :], out_buf=wstg, in_buf=win)
            emit("dve", lambda e, c=c, wout=wout: e.tensor_scalar(
                out=wout[:, c * D:(c + 1) * D], in0=wstg.ap,
                scalar1=n1c[:, c:c + 1], scalar2=None, op0=OP.mult),
                reads=[wstg, n1c], writes=[wout], no_waw=True)
    emit("dve", lambda e: e.tensor_tensor(out=wb.ap, in0=bwc.ap, in1=n1c.ap,
         op=OP.mult), reads=[bwc, n1c], writes=[wb])

    xt = p.sbuf("xt", [128, D], F32)
    xsq = p.sbuf("xsq", [128, D], F32)
    s1 = p.sbuf("s1", [128, 1], F32)
    s2 = p.sbuf("s2", [128, 1], F32)
    mcol = p.sbuf("mcol", [128, 1], F32)
    vcol = p.sbuf("vcol", [128, 1], F32)
    r1col = p.sbuf("r1col", [128, 1], F32)
    xn_tm = p.sbuf("xn_tm", [128, D], F32)
    r2grp = p.sbuf("r2grp", [128, 4], F32)
    XNF = p.sbuf("XNF", [128, NCHUNK * 512], F32)
    XNFB = p.sbuf("XNFB", [128, NCHUNK * 512], BF16)
    ps_a = p.psum("ps_a", [128, 512], F32)
    ps_b = p.psum("ps_b", [128, 512], F32)
    ps_c = p.psum("ps_c", [128, 512], F32)
    ps_s = ps_t[1]
    kraw = p.sbuf("kraw", [128, D], F32)
    ksq = p.sbuf("ksq", [128, D], F32)
    kn2 = p.sbuf("kn2", [128, 1], F32)
    ksc = p.sbuf("ksc", [128, 1], F32)
    ktmb = p.sbuf("ktmb", [128, D], BF16)
    kfmb = p.sbuf("kfmb", [128, 128], BF16)
    ps_kt = p.psum("ps_kt", [128, 128], BF16)
    qfmb = p.sbuf("qfmb", [128, 512], BF16)
    rrow = p.sbuf("rrow", [1, 512], F32)
    rbc = p.sbuf("rbc", [128, 512], F32)
    VRAW = p.sbuf("VRAW", [128, NCHUNK * T], F32)
    a0bc = p.sbuf("a0bc", [128, T], F32)
    d1t = p.sbuf("d1t", [128, T], F32)
    vema = p.sbuf("vema", [128, T], F32)
    vstg = p.sbuf("vstg", [128, 128], F32)
    bcol = p.sbuf("bcol", [128, 1], F32)

    inv_d = 1.0 / D

    for b in range(nbatch):
        for g in range(4):
            for ti in range(4):
                tile = g * 4 + ti
                t0 = tile * 128
                dma("sp", xt.ap, x_in.ap_[b, t0:t0 + 128, :], out_buf=xt, in_buf=x_in)
                emit("act", lambda e: e.square(out=xsq.ap, in_=xt.ap),
                     reads=[xt], writes=[xsq])
                emit("dve", lambda e: e.reduce_sum(out=s1.ap, in_=xt.ap, axis=AX.X),
                     reads=[xt], writes=[s1])
                emit("dve", lambda e: e.reduce_sum(out=s2.ap, in_=xsq.ap, axis=AX.X),
                     reads=[xsq], writes=[s2])
                emit("dve", lambda e: e.tensor_scalar(out=mcol.ap, in0=s1.ap,
                     scalar1=inv_d, scalar2=None, op0=OP.mult),
                     reads=[s1], writes=[mcol])
                emit("dve", lambda e: e.tensor_scalar(out=vcol.ap, in0=s2.ap,
                     scalar1=inv_d, scalar2=None, op0=OP.mult),
                     reads=[s2], writes=[vcol])
                emit("dve", lambda e: e.tensor_tensor(out=r1col.ap, in0=mcol.ap,
                     in1=mcol.ap, op=OP.mult), reads=[mcol], writes=[r1col])
                emit("dve", lambda e: e.tensor_tensor(out=vcol.ap, in0=vcol.ap,
                     in1=r1col.ap, op=OP.subtract), reads=[vcol, r1col], writes=[vcol])
                emit("act", lambda e: e.activation(out=r1col.ap, in_=vcol.ap,
                     func=AF.Sqrt, bias=eps5.ap), reads=[vcol, eps5], writes=[r1col])
                emit("dve", lambda e: e.reciprocal(out=r1col.ap, in_=r1col.ap),
                     reads=[r1col], writes=[r1col])
                emit("dve", lambda e: e.tensor_scalar(out=xn_tm.ap, in0=xt.ap,
                     scalar1=mcol.ap, scalar2=r1col.ap, op0=OP.subtract, op1=OP.mult),
                     reads=[xt, mcol, r1col], writes=[xn_tm])
                emit("act", lambda e: e.square(out=xsq.ap, in_=xn_tm.ap),
                     reads=[xn_tm], writes=[xsq])
                emit("dve", lambda e: e.reduce_sum(out=s2.ap, in_=xsq.ap, axis=AX.X),
                     reads=[xsq], writes=[s2])
                emit("dve", lambda e, ti=ti: e.tensor_scalar(out=r2grp[:, ti:ti + 1],
                     in0=s2.ap, scalar1=inv_d, scalar2=None, op0=OP.mult),
                     reads=[s2], writes=[r2grp], no_waw=True)
                emit("act", lambda e, ti=ti: e.activation(out=r2grp[:, ti:ti + 1],
                     in_=r2grp[:, ti:ti + 1], func=AF.Sqrt, bias=eps6.ap),
                     reads=[r2grp, eps6], writes=[r2grp], no_waw=True)
                emit("dve", lambda e, ti=ti: e.reciprocal(out=r2grp[:, ti:ti + 1],
                     in_=r2grp[:, ti:ti + 1]), reads=[r2grp], writes=[r2grp], no_waw=True)
                if 'r' in sections:
                    dma("act", RSTD2.ap_[b, t0:t0 + 128, None], r2grp[:, ti:ti + 1],
                        out_buf=RSTD2, in_buf=r2grp)
                    dma("act", RSTD2i.ap_[b, t0:t0 + 128, None], r2grp[:, ti:ti + 1],
                        out_buf=RSTD2i, in_buf=r2grp)
                for c in (range(NCHUNK) if 'T' in sections else []):
                    pst = ps_t[c % 2]
                    emit("pe", lambda e, c=c, pst=pst: e.transpose(
                        pst.ap, xn_tm[:, c * 128:(c + 1) * 128], idf.ap),
                        reads=[xn_tm, idf], writes=[pst])
                    emit("dve", lambda e, c=c, ti=ti, pst=pst: e.tensor_copy(
                        out=XNF[:, c * 512 + ti * 128: c * 512 + (ti + 1) * 128],
                        in_=pst.ap), reads=[pst], writes=[XNF], no_waw=True)
                    emit("act", lambda e, c=c, ti=ti, pst=pst: e.copy(
                        out=XNFB[:, c * 512 + ti * 128: c * 512 + (ti + 1) * 128],
                        in_=pst.ap), reads=[pst], writes=[XNFB], no_waw=True)

            for c in (range(NCHUNK) if 'T' in sections else []):
                dma("act", XN_FM.ap_[b, c, :, g * 512:(g + 1) * 512],
                    XNF[:, c * 512:(c + 1) * 512], out_buf=XN_FM, in_buf=XNF)
            if 'r' in sections:
                dma("sp", rrow.ap, RSTD2i.ap_[b, None, g * 512:(g + 1) * 512],
                    out_buf=rrow, in_buf=RSTD2i)
                emit("pe", lambda e: e.matmul(ps_c.ap, ones1.ap, rrow.ap, start=True, stop=True),
                     reads=[ones1, rrow], writes=[ps_c])
                emit("dve", lambda e: e.tensor_copy(out=rbc.ap, in_=ps_c.ap),
                     reads=[ps_c], writes=[rbc])

            for ti in (range(4) if 'k' in sections else []):
                tile = g * 4 + ti
                for h in range(2):
                    ps = ps_a if h == 0 else ps_b
                    for c in range(NCHUNK):
                        emit("pe", lambda e, c=c, ti=ti, h=h, ps=ps: e.matmul(
                            ps.ap,
                            XNFB[:, c * 512 + ti * 128: c * 512 + (ti + 1) * 128],
                            wk[:, c * D + h * 512: c * D + (h + 1) * 512],
                            start=(c == 0), stop=(c == NCHUNK - 1)),
                            reads=[XNFB, wk], writes=[ps])
                    emit("dve", lambda e, h=h, ps=ps: e.tensor_copy(
                        out=kraw[:, h * 512:(h + 1) * 512], in_=ps.ap),
                        reads=[ps], writes=[kraw], no_waw=True)
                emit("act", lambda e: e.square(out=ksq.ap, in_=kraw.ap),
                     reads=[kraw], writes=[ksq])
                emit("dve", lambda e: e.reduce_sum(out=kn2.ap, in_=ksq.ap, axis=AX.X),
                     reads=[ksq], writes=[kn2])
                emit("act", lambda e: e.sqrt(out=kn2.ap, in_=kn2.ap),
                     reads=[kn2], writes=[kn2])
                emit("dve", lambda e, ti=ti: e.tensor_tensor(out=kn2.ap, in0=kn2.ap,
                     in1=r2grp[:, ti:ti + 1], op=OP.mult),
                     reads=[kn2, r2grp], writes=[kn2])
                emit("dve", lambda e: e.tensor_scalar(out=kn2.ap, in0=kn2.ap,
                     scalar1=1e-12, scalar2=None, op0=OP.max),
                     reads=[kn2], writes=[kn2])
                emit("dve", lambda e: e.reciprocal(out=kn2.ap, in_=kn2.ap),
                     reads=[kn2], writes=[kn2])
                emit("dve", lambda e, ti=ti: e.tensor_tensor(out=ksc.ap,
                     in0=r2grp[:, ti:ti + 1], in1=kn2.ap, op=OP.mult),
                     reads=[r2grp, kn2], writes=[ksc])
                emit("dve", lambda e: e.tensor_scalar(out=ktmb.ap, in0=kraw.ap,
                     scalar1=ksc.ap, scalar2=None, op0=OP.mult),
                     reads=[kraw, ksc], writes=[ktmb])
                dma("act", K_TM.ap_[b, tile, :, :], ktmb.ap, out_buf=K_TM, in_buf=ktmb)
                for c in range(NCHUNK):
                    emit("pe", lambda e, c=c: e.transpose(
                        ps_kt.ap, ktmb[:, c * 128:(c + 1) * 128], idb.ap),
                        reads=[ktmb, idb], writes=[ps_kt])
                    emit("dve", lambda e: e.tensor_copy(out=kfmb.ap, in_=ps_kt.ap),
                         reads=[ps_kt], writes=[kfmb])
                    dma("act", K_FM.ap_[b, c, :, tile * 128:(tile + 1) * 128],
                        kfmb.ap, out_buf=K_FM, in_buf=kfmb)
                for c in range(NCHUNK):
                    emit("pe", lambda e, c=c, ti=ti: e.matmul(
                        ps_s[:, 0:1],
                        XNFB[:, c * 512 + ti * 128: c * 512 + (ti + 1) * 128],
                        wb[:, c:c + 1], start=(c == 0), stop=(c == NCHUNK - 1)),
                        reads=[XNFB, wb], writes=[ps_s])
                emit("dve", lambda e, ti=ti: e.tensor_scalar(out=bcol.ap,
                     in0=ps_s[:, 0:1], scalar1=r2grp[:, ti:ti + 1], scalar2=None,
                     op0=OP.mult), reads=[ps_s, r2grp], writes=[bcol])
                emit("act", lambda e: e.activation(out=bcol.ap, in_=bcol.ap,
                     func=AF.Sigmoid, bias=bbc.ap), reads=[bcol, bbc], writes=[bcol])
                dma("act", BETA.ap_[b, tile * 128:(tile + 1) * 128, None], bcol.ap,
                    out_buf=BETA, in_buf=bcol)

            for c in (range(NCHUNK) if 'q' in sections else []):
                ps = ps_a if c % 2 == 0 else ps_b
                for k in range(NCHUNK):
                    emit("pe", lambda e, c=c, k=k, ps=ps: e.matmul(
                        ps.ap, wq[:, k * D + c * 128: k * D + (c + 1) * 128],
                        XNFB[:, k * 512:(k + 1) * 512],
                        start=(k == 0), stop=(k == NCHUNK - 1)),
                        reads=[wq, XNFB], writes=[ps])
                emit("dve", lambda e, ps=ps: e.tensor_copy(out=qfmb.ap, in_=ps.ap),
                     reads=[ps], writes=[qfmb])
                dma("act", Q_FM.ap_[b, c, :, g * 512:(g + 1) * 512], qfmb.ap,
                    out_buf=Q_FM, in_buf=qfmb)

            for c in (range(NCHUNK) if 'v' in sections else []):
                ps = ps_a if c % 2 == 0 else ps_b
                for k in range(NCHUNK):
                    emit("pe", lambda e, c=c, k=k, ps=ps: e.matmul(
                        ps.ap, wv[:, k * D + c * 128: k * D + (c + 1) * 128],
                        XNFB[:, k * 512:(k + 1) * 512],
                        start=(k == 0), stop=(k == NCHUNK - 1)),
                        reads=[wv, XNFB], writes=[ps])
                emit("dve", lambda e, c=c, g=g, ps=ps: e.tensor_tensor(
                    out=VRAW[:, c * T + g * 512: c * T + (g + 1) * 512],
                    in0=ps.ap, in1=rbc.ap, op=OP.mult),
                    reads=[ps, rbc], writes=[VRAW], no_waw=True)

        for c in (range(NCHUNK) if 'e' in sections else []):
            emit("dve", lambda e, c=c: e.tensor_scalar(out=a0bc.ap, in0=onesT.ap,
                 scalar1=alc[:, c:c + 1], scalar2=None, op0=OP.subtract),
                 reads=[onesT, alc], writes=[a0bc])
            emit("dve", lambda e, c=c: e.tensor_scalar(out=d1t.ap,
                 in0=VRAW[:, c * T:(c + 1) * T], scalar1=alc[:, c:c + 1],
                 scalar2=None, op0=OP.mult), reads=[VRAW, alc], writes=[d1t])
            emit("dve", lambda e: e.tensor_tensor_scan(out=vema.ap, data0=a0bc.ap,
                 data1=d1t.ap, initial=0.0, op0=OP.mult, op1=OP.add),
                 reads=[a0bc, d1t], writes=[vema])
            for tile in range(NTILE):
                pst = ps_t[tile % 2]
                emit("pe", lambda e, tile=tile, pst=pst: e.transpose(
                    pst.ap, vema[:, tile * 128:(tile + 1) * 128], idf.ap),
                    reads=[vema, idf], writes=[pst])
                emit("dve", lambda e, pst=pst: e.tensor_copy(out=vstg.ap, in_=pst.ap),
                     reads=[pst], writes=[vstg])
                dma("act", V_TM.ap_[b, tile, :, c * 128:(c + 1) * 128], vstg.ap,
                    out_buf=V_TM, in_buf=vstg)


def build_p1(nbatch=BQ, sections=('a','T','r','k','q','v','e')):
    nc = bass.Bass("TRN2", target_bir_lowering=False, debug=False)
    io = _make_io(nc, _IO_P1)
    p = Prog(nc)
    with p.phase():
        emit_p1(p, io, nbatch, sections)
    p.finish()
    return nc



_IO_PDELTA = [
    ("Q_FM", [BQ, NCHUNK, 128, T], BF16, "ExternalInput"),
    ("K_FM", [BQ, NCHUNK, 128, T], BF16, "ExternalInput"),
    ("K_TM", [BQ, NTILE, 128, D], BF16, "ExternalInput"),
    ("V_TM", [BQ, NTILE, 128, D], F32, "ExternalInput"),
    ("BETA", [BQ, T], F32, "ExternalInput"),
    ("RSTD2", [BQ, T], F32, "ExternalInput"),
    ("ident_bf16", [128, 128], BF16, "ExternalInput"),
    ("mask_su", [128, 128], F32, "ExternalInput"),
    ("mask_sl", [128, 128], F32, "ExternalInput"),
    ("mask_ui", [128, 128], F32, "ExternalInput"),
    ("O_FM", [BQ, NCHUNK, 128, T], BF16, "ExternalOutput"),
]

def emit_pdelta(p, io, nbatch=BQ, nchunks=NTILE):
    nc = p.nc
    QF = io["Q_FM"]
    KF = io["K_FM"]
    KT = io["K_TM"]
    VT = io["V_TM"]
    BETA = io["BETA"]
    RSTD2 = io["RSTD2"]
    idbf_in = io["ident_bf16"]
    msu_in = io["mask_su"]
    msl_in = io["mask_sl"]
    mui_in = io["mask_ui"]
    OF = io["O_FM"]
    emit, dma = p.emit, p.dma

    idb = p.sbuf("idb", [128, 128], BF16)
    msu = p.sbuf("msu", [128, 128], F32)
    msl = p.sbuf("msl", [128, 128], F32)
    mui = p.sbuf("mui", [128, 128], F32)
    ones1 = p.sbuf("ones1", [1, 128], F32)
    dma("sp", idb.ap, idbf_in.ap, out_buf=idb, in_buf=idbf_in)
    dma("sp", msu.ap, msu_in.ap, out_buf=msu, in_buf=msu_in)
    dma("sp", msl.ap, msl_in.ap, out_buf=msl, in_buf=msl_in)
    dma("sp", mui.ap, mui_in.ap, out_buf=mui, in_buf=mui_in)
    emit("dve", lambda e: e.memset(ones1.ap, 1.0), writes=[ones1])

    STB = [p.sbuf(f"STB{b}", [128, NCHUNK * D], F32) for b in range(nbatch)]
    for b in range(nbatch):
        emit("dve", lambda e, b=b: e.memset(STB[b].ap, 0.0), writes=[STB[b]])

    qf = p.sbuf("qf", [128, 1024], BF16)
    kf = p.sbuf("kf", [128, 1024], BF16)
    kt = p.sbuf("kt", [128, 1024], BF16)
    vt = p.sbuf("vt", [128, 1024], F32)
    bcol = p.sbuf("bcol", [128, 1], F32)
    nbcol = p.sbuf("nbcol", [128, 1], F32)
    brow = p.sbuf("brow", [1, 128], F32)
    rcol = p.sbuf("rcol", [128, 1], F32)
    nbc = p.sbuf("nbc", [128, 128], F32)
    Sbf = p.sbuf("Sbf", [128, 1024], BF16)
    Wt_ = p.sbuf("Wt_", [128, 128], BF16)
    W_ = p.sbuf("W_", [128, 128], BF16)
    Pt = p.sbuf("Pt", [128, 128], BF16)
    MqT = p.sbuf("MqT", [128, 128], BF16)
    Bb = p.sbuf("Bb", [128, 1024], BF16)
    Ub = p.sbuf("Ub", [128, 1024], BF16)
    otm = p.sbuf("otm", [128, 1024], BF16)
    ofm = p.sbuf("ofm", [128, 128], BF16)
    tmp = p.sbuf("tmpd", [128, 128], F32)

    psG = p.psum("psG", [128, 128], F32)
    psGq = p.psum("psGq", [128, 128], F32)
    psM = p.psum("psM", [128, 128], F32)
    psX = p.psum("psX", [128, 128], BF16)
    psW = [p.psum(f"psW{h}", [128, 512], F32) for h in range(2)]
    psO = [p.psum(f"psO{h}", [128, 512], F32) for h in range(2)]

    for ci in range(nchunks):
        for b in range(nbatch):
            S = STB[b]
            t0 = ci * 128
            for c in range(NCHUNK):
                dma("sp", qf[:, c * 128:(c + 1) * 128],
                    QF.ap_[b, c, :, t0:t0 + 128], out_buf=qf, in_buf=QF,
                    no_waw=(c > 0))
                dma("sp", kf[:, c * 128:(c + 1) * 128],
                    KF.ap_[b, c, :, t0:t0 + 128], out_buf=kf, in_buf=KF,
                    no_waw=(c > 0))
            dma("act", kt.ap, KT.ap_[b, ci, :, :], out_buf=kt, in_buf=KT)
            dma("act", vt.ap, VT.ap_[b, ci, :, :], out_buf=vt, in_buf=VT)
            dma("pool", bcol.ap, BETA.ap_[b, t0:t0 + 128, None], out_buf=bcol, in_buf=BETA)
            dma("pool", brow.ap, BETA.ap_[b, None, t0:t0 + 128], out_buf=brow, in_buf=BETA)
            dma("pool", rcol.ap, RSTD2.ap_[b, t0:t0 + 128, None], out_buf=rcol, in_buf=RSTD2)

            for c in range(NCHUNK):
                emit("pe", lambda e, c=c: e.matmul(psG.ap, kf[:, c*128:(c+1)*128],
                     kf[:, c*128:(c+1)*128], start=(c == 0), stop=(c == NCHUNK-1)),
                     reads=[kf], writes=[psG])
            for c in range(NCHUNK):
                emit("pe", lambda e, c=c: e.matmul(psGq.ap, kf[:, c*128:(c+1)*128],
                     qf[:, c*128:(c+1)*128], start=(c == 0), stop=(c == NCHUNK-1)),
                     reads=[kf, qf], writes=[psGq])
            emit("pe", lambda e: e.matmul(psM.ap, ones1.ap, brow.ap, start=True, stop=True),
                 reads=[ones1, brow], writes=[psM])
            emit("dve", lambda e: e.tensor_scalar(out=nbc.ap, in0=psM.ap,
                 scalar1=-1.0, scalar2=None, op0=OP.mult), reads=[psM], writes=[nbc])
            emit("dve", lambda e: e.tensor_scalar(out=nbcol.ap, in0=bcol.ap,
                 scalar1=-1.0, scalar2=None, op0=OP.mult), reads=[bcol], writes=[nbcol])

            emit("dve", lambda e: e.tensor_tensor(out=MqT.ap, in0=psGq.ap,
                 in1=mui.ap, op=OP.mult), reads=[psGq, mui], writes=[MqT])
            emit("dve", lambda e: e.tensor_tensor(out=tmp.ap, in0=psG.ap,
                 in1=msu.ap, op=OP.mult), reads=[psG, msu], writes=[tmp])
            emit("dve", lambda e: e.tensor_tensor(out=W_.ap, in0=tmp.ap,
                 in1=nbc.ap, op=OP.mult), reads=[tmp, nbc], writes=[W_])
            emit("dve", lambda e: e.tensor_tensor(out=tmp.ap, in0=psG.ap,
                 in1=msl.ap, op=OP.mult), reads=[psG, msl], writes=[tmp])
            emit("dve", lambda e: e.tensor_scalar(out=Wt_.ap, in0=tmp.ap,
                 scalar1=nbcol.ap, scalar2=None, op0=OP.mult),
                 reads=[tmp, nbcol], writes=[Wt_])
            emit("dve", lambda e: e.tensor_tensor(out=Pt.ap, in0=W_.ap,
                 in1=idb.ap, op=OP.add), reads=[W_, idb], writes=[Pt])

            for c in range(NCHUNK):
                emit("dve" if c % 2 == 0 else "act",
                     (lambda e, c=c, S=S: e.tensor_copy(out=Sbf.ap, in_=S[:, c*1024:(c+1)*1024]))
                     if c % 2 == 0 else
                     (lambda e, c=c, S=S: e.copy(out=Sbf.ap, in_=S[:, c*1024:(c+1)*1024])),
                     reads=[S], writes=[Sbf])
                for h in range(2):
                    emit("pe", lambda e, c=c, h=h: e.matmul(psW[h].ap,
                         kf[:, c*128:(c+1)*128], Sbf[:, h*512:(h+1)*512],
                         start=(c == 0), stop=(c == NCHUNK-1)),
                         reads=[kf, Sbf], writes=[psW[h]])
                    emit("pe", lambda e, c=c, h=h: e.matmul(psO[h].ap,
                         qf[:, c*128:(c+1)*128], Sbf[:, h*512:(h+1)*512],
                         start=(c == 0), stop=False, skip_group_check=True),
                         reads=[qf, Sbf], writes=[psO[h]])

            for k in range(1, 7):
                emit("pe", lambda e: e.matmul(psG.ap, Wt_.ap, W_.ap, start=True, stop=True),
                     reads=[Wt_, W_], writes=[psG])
                emit("pe", lambda e: e.matmul(psGq.ap, W_.ap, Wt_.ap, start=True, stop=True),
                     reads=[W_, Wt_], writes=[psGq])
                emit("dve", lambda e: e.tensor_copy(out=W_.ap, in_=psG.ap),
                     reads=[psG], writes=[W_])
                emit("dve", lambda e: e.tensor_copy(out=Wt_.ap, in_=psGq.ap),
                     reads=[psGq], writes=[Wt_])
                emit("pe", lambda e: e.matmul(psM.ap, idb.ap, Pt.ap, start=True, stop=False),
                     reads=[idb, Pt], writes=[psM])
                emit("pe", lambda e: e.matmul(psM.ap, Wt_.ap, Pt.ap, start=False, stop=True),
                     reads=[Wt_, Pt], writes=[psM])
                emit("dve", lambda e: e.tensor_copy(out=Pt.ap, in_=psM.ap),
                     reads=[psM], writes=[Pt])

            for h in range(2):
                emit("dve", lambda e, h=h: e.scalar_tensor_tensor(
                     out=vt[:, h*512:(h+1)*512], in0=psW[h].ap, scalar=-1.0,
                     in1=vt[:, h*512:(h+1)*512], op0=OP.mult, op1=OP.add),
                     reads=[psW[h], vt], writes=[vt], no_waw=True)
            emit("dve", lambda e: e.tensor_scalar(out=Bb.ap, in0=vt.ap,
                 scalar1=bcol.ap, scalar2=None, op0=OP.mult),
                 reads=[vt, bcol], writes=[Bb])

            for h in range(2):
                emit("pe", lambda e, h=h: e.matmul(psW[h].ap, Pt.ap,
                     Bb[:, h*512:(h+1)*512], start=True, stop=True),
                     reads=[Pt, Bb], writes=[psW[h]])
                emit("dve", lambda e, h=h: e.tensor_copy(
                     out=Ub[:, h*512:(h+1)*512], in_=psW[h].ap),
                     reads=[psW[h]], writes=[Ub], no_waw=True)

            for h in range(2):
                emit("pe", lambda e, h=h: e.matmul(psO[h].ap, MqT.ap,
                     Ub[:, h*512:(h+1)*512], start=False, stop=True,
                     skip_group_check=True),
                     reads=[MqT, Ub], writes=[psO[h]])
                emit("dve", lambda e, h=h: e.tensor_scalar(
                     out=otm[:, h*512:(h+1)*512], in0=psO[h].ap,
                     scalar1=rcol.ap, scalar2=None, op0=OP.mult),
                     reads=[psO[h], rcol], writes=[otm], no_waw=True)

            for c in range(NCHUNK):
                for h in range(2):
                    emit("pe", lambda e, c=c, h=h: e.matmul(psW[h].ap,
                         kt[:, c*128:(c+1)*128], Ub[:, h*512:(h+1)*512],
                         start=True, stop=True),
                         reads=[kt, Ub], writes=[psW[h]])
                    emit("dve", lambda e, c=c, h=h, S=S: e.tensor_tensor(
                         out=S[:, c*1024 + h*512: c*1024 + (h+1)*512],
                         in0=psW[h].ap,
                         in1=S[:, c*1024 + h*512: c*1024 + (h+1)*512],
                         op=OP.add),
                         reads=[psW[h], S], writes=[S], no_waw=True)

            for c in range(NCHUNK):
                emit("pe", lambda e, c=c: e.transpose(psX.ap,
                     otm[:, c*128:(c+1)*128], idb.ap),
                     reads=[otm, idb], writes=[psX])
                emit("dve", lambda e: e.tensor_copy(out=ofm.ap, in_=psX.ap),
                     reads=[psX], writes=[ofm])
                dma("act", OF.ap_[b, c, :, t0:t0 + 128], ofm.ap,
                    out_buf=OF, in_buf=ofm)


def build_pdelta(nbatch=BQ, nchunks=NTILE):
    nc = bass.Bass("TRN2", target_bir_lowering=False, debug=False)
    io = _make_io(nc, _IO_PDELTA)
    p = Prog(nc)
    with p.phase():
        emit_pdelta(p, io, nbatch, nchunks)
    p.finish()
    return nc



_IO_PC = [
    ("XN_FM", [BQ, NCHUNK, 128, T], F32, "ExternalInput"),
    ("O_FM", [BQ, NCHUNK, 128, T], BF16, "ExternalInput"),
    ("Wo", [D, D], F32, "ExternalInput"),
    ("norm2_w", [D], F32, "ExternalInput"),
    ("w_gate", [D, 2 * D], F32, "ExternalInput"),
    ("w_up", [D, 2 * D], F32, "ExternalInput"),
    ("w_down", [2 * D, D], F32, "ExternalInput"),
    ("ident_f32", [128, 128], F32, "ExternalInput"),
    ("S_FM", [BQ, NCHUNK, 128, T], F32, "ExternalOutput"),
]

def emit_pc(p, io, nbatch=BQ):
    nc = p.nc
    XN = io["XN_FM"]
    OF = io["O_FM"]
    WO = io["Wo"]
    N2 = io["norm2_w"]
    WG = io["w_gate"]
    WU = io["w_up"]
    WD = io["w_down"]
    idf_in = io["ident_f32"]
    SF = io["S_FM"]
    emit, dma = p.emit, p.dma

    idf = p.sbuf("idf", [128, 128], F32)
    dma("sp", idf.ap, idf_in.ap, out_buf=idf, in_buf=idf_in)
    onesc = p.sbuf("onesc", [128, 1], F32)
    ones1 = p.sbuf("ones1", [1, 128], F32)
    eps6 = p.sbuf("eps6", [128, 1], F32)
    emit("dve", lambda e: e.memset(onesc.ap, 1.0), writes=[onesc])
    emit("dve", lambda e: e.memset(ones1.ap, 1.0), writes=[ones1])
    emit("dve", lambda e: e.memset(eps6.ap, 1e-6), writes=[eps6])

    vrows = p.sbuf("vrows", [8, 128], F32)
    n2c = p.sbuf("n2c", [128, NCHUNK], F32)
    ps_t = [p.psum(f"ps_t{i}", [128, 128], F32) for i in range(2)]
    dma("sp", vrows.ap, N2.ap_.rearrange("(c p) -> c p", p=128), out_buf=vrows, in_buf=N2)
    emit("pe", lambda e: e.transpose(ps_t[0][:, 0:8], vrows.ap, idf[0:8, 0:8]),
         reads=[vrows, idf], writes=[ps_t[0]])
    emit("dve", lambda e: e.tensor_copy(out=n2c.ap, in_=ps_t[0][:, 0:8]),
         reads=[ps_t[0]], writes=[n2c])

    wstg = p.sbuf("wstg", [128, D], F32)
    wo = p.sbuf("wo", [128, NCHUNK * D], BF16)
    wg = p.sbuf("wg", [128, NCHUNK * 2 * D], BF16)
    wu = p.sbuf("wu", [128, NCHUNK * 2 * D], BF16)
    wd = p.sbuf("wd", [128, 2 * NCHUNK * D], BF16)
    for c in range(NCHUNK):
        dma("sp", wstg.ap, WO.ap_[c * 128:(c + 1) * 128, :], out_buf=wstg, in_buf=WO)
        emit("dve", lambda e, c=c: e.tensor_copy(out=wo[:, c * D:(c + 1) * D],
             in_=wstg.ap), reads=[wstg], writes=[wo], no_waw=True)
    for win, wout in ((WG, wg), (WU, wu)):
        for c in range(NCHUNK):
            for hh in range(2):
                dma("sp", wstg.ap, win.ap_[c * 128:(c + 1) * 128, hh * D:(hh + 1) * D],
                    out_buf=wstg, in_buf=win)
                emit("dve", lambda e, c=c, hh=hh, wout=wout: e.tensor_scalar(
                     out=wout[:, c * 2 * D + hh * D: c * 2 * D + (hh + 1) * D],
                     in0=wstg.ap, scalar1=n2c[:, c:c + 1], scalar2=None, op0=OP.mult),
                     reads=[wstg, n2c], writes=[wout], no_waw=True)
    for c in range(2 * NCHUNK):
        dma("sp", wstg.ap, WD.ap_[c * 128:(c + 1) * 128, :], out_buf=wstg, in_buf=WD)
        emit("dve", lambda e, c=c: e.tensor_copy(out=wd[:, c * D:(c + 1) * D],
             in_=wstg.ap), reads=[wstg], writes=[wd], no_waw=True)

    xnf = p.sbuf("xnf", [128, NCHUNK * 512], F32)
    of = p.sbuf("of", [128, NCHUNK * 512], BF16)
    yf = p.sbuf("yf", [128, NCHUNK * 512], F32)
    ysq = p.sbuf("ysq", [128, 512], F32)
    h2b = p.sbuf("h2b", [128, NCHUNK * 512], BF16)
    gatef = p.sbuf("gatef", [128, 4 * 512], F32)
    mlpin = p.sbuf("mlpin", [128, 16 * 512], BF16)
    sfo = p.sbuf("sfo", [128, 512], F32)
    rrow = p.sbuf("rrow", [1, 512], F32)
    rbc = p.sbuf("rbc", [128, 512], F32)
    psA = p.psum("psA", [128, 512], F32)
    psB = p.psum("psB", [128, 512], F32)
    psR = p.psum("psR", [1, 512], F32)
    psC = p.psum("psC", [128, 512], F32)

    for b in range(nbatch):
        for g in range(4):
            sl = slice(g * 512, (g + 1) * 512)
            for c in range(NCHUNK):
                dma("sp", xnf[:, c * 512:(c + 1) * 512], XN.ap_[b, c, :, sl],
                    out_buf=xnf, in_buf=XN, no_waw=(c > 0))
                dma("act", of[:, c * 512:(c + 1) * 512], OF.ap_[b, c, :, sl],
                    out_buf=of, in_buf=OF, no_waw=(c > 0))
            for c in range(NCHUNK):
                ps = psA if c % 2 == 0 else psB
                for k in range(NCHUNK):
                    emit("pe", lambda e, c=c, k=k, ps=ps: e.matmul(ps.ap,
                         wo[:, k * D + c * 128: k * D + (c + 1) * 128],
                         of[:, k * 512:(k + 1) * 512],
                         start=(k == 0), stop=(k == NCHUNK - 1)),
                         reads=[wo, of], writes=[ps])
                emit("dve", lambda e, c=c, ps=ps: e.tensor_tensor(
                     out=yf[:, c * 512:(c + 1) * 512], in0=ps.ap,
                     in1=xnf[:, c * 512:(c + 1) * 512], op=OP.add),
                     reads=[ps, xnf], writes=[yf], no_waw=True)
            for c in range(NCHUNK):
                emit("act", lambda e, c=c: e.square(out=ysq.ap,
                     in_=yf[:, c * 512:(c + 1) * 512]), reads=[yf], writes=[ysq])
                emit("pe", lambda e, c=c: e.matmul(psR.ap, onesc.ap, ysq.ap,
                     start=(c == 0), stop=(c == NCHUNK - 1)),
                     reads=[onesc, ysq], writes=[psR])
            emit("dve", lambda e: e.tensor_scalar(out=rrow.ap, in0=psR.ap,
                 scalar1=1.0 / D, scalar2=None, op0=OP.mult),
                 reads=[psR], writes=[rrow])
            emit("act", lambda e: e.activation(out=rrow.ap, in_=rrow.ap,
                 func=AF.Sqrt, bias=eps6[0:1, :]), reads=[rrow, eps6], writes=[rrow])
            emit("dve", lambda e: e.reciprocal(out=rrow.ap, in_=rrow.ap),
                 reads=[rrow], writes=[rrow])
            emit("pe", lambda e: e.matmul(psC.ap, ones1.ap, rrow.ap,
                 start=True, stop=True), reads=[ones1, rrow], writes=[psC])
            emit("dve", lambda e: e.tensor_copy(out=rbc.ap, in_=psC.ap),
                 reads=[psC], writes=[rbc])
            for c in range(NCHUNK):
                emit("dve", lambda e, c=c: e.tensor_tensor(
                     out=h2b[:, c * 512:(c + 1) * 512],
                     in0=yf[:, c * 512:(c + 1) * 512], in1=rbc.ap, op=OP.mult),
                     reads=[yf, rbc], writes=[h2b], no_waw=True)
            for fh in range(4):
                for fc in range(4):
                    f = fh * 4 + fc
                    ps = psA if fc % 2 == 0 else psB
                    for k in range(NCHUNK):
                        emit("pe", lambda e, f=f, k=k, ps=ps: e.matmul(ps.ap,
                             wg[:, k * 2 * D + f * 128: k * 2 * D + (f + 1) * 128],
                             h2b[:, k * 512:(k + 1) * 512],
                             start=(k == 0), stop=(k == NCHUNK - 1)),
                             reads=[wg, h2b], writes=[ps])
                    emit("act", lambda e, fc=fc, ps=ps: e.activation(
                         out=gatef[:, fc * 512:(fc + 1) * 512], in_=ps.ap,
                         func=AF.Silu), reads=[ps], writes=[gatef], no_waw=True)
                    for k in range(NCHUNK):
                        emit("pe", lambda e, f=f, k=k, ps=ps: e.matmul(ps.ap,
                             wu[:, k * 2 * D + f * 128: k * 2 * D + (f + 1) * 128],
                             h2b[:, k * 512:(k + 1) * 512],
                             start=(k == 0), stop=(k == NCHUNK - 1)),
                             reads=[wu, h2b], writes=[ps])
                    emit("dve", lambda e, f=f, fc=fc, ps=ps: e.tensor_tensor(
                         out=mlpin[:, f * 512:(f + 1) * 512], in0=ps.ap,
                         in1=gatef[:, fc * 512:(fc + 1) * 512], op=OP.mult),
                         reads=[ps, gatef], writes=[mlpin], no_waw=True)
            for c in range(NCHUNK):
                ps = psA if c % 2 == 0 else psB
                for k in range(16):
                    emit("pe", lambda e, c=c, k=k, ps=ps: e.matmul(ps.ap,
                         wd[:, k * D + c * 128: k * D + (c + 1) * 128],
                         mlpin[:, k * 512:(k + 1) * 512],
                         start=(k == 0), stop=(k == 15)),
                         reads=[wd, mlpin], writes=[ps])
                emit("dve", lambda e, c=c, ps=ps: e.tensor_tensor(
                     out=sfo.ap, in0=ps.ap,
                     in1=yf[:, c * 512:(c + 1) * 512], op=OP.add),
                     reads=[ps, yf], writes=[sfo])
                dma("act", SF.ap_[b, c, :, sl], sfo.ap, out_buf=SF, in_buf=sfo)


def build_pc(nbatch=BQ):
    nc = bass.Bass("TRN2", target_bir_lowering=False, debug=False)
    io = _make_io(nc, _IO_PC)
    p = Prog(nc)
    with p.phase():
        emit_pc(p, io, nbatch)
    p.finish()
    return nc



_P0_SPECS = [
    ("w1", [14 * 1024, 2048]), ("w2", [14 * 2048, 1024]),
    ("gug", [8 * 1024, 1024]), ("guu", [8 * 1024, 1024]),
    ("dn", [8 * 1024, 1024]),
]


def emit_p0(p, io):
    nc = p.nc
    stg = [p.sbuf(f"stg{i}", [128, 2048], F32) for i in range(3)]
    stb = [p.sbuf(f"stb{i}", [128, 2048], BF16) for i in range(3)]
    qs = ["sp", "act", "pool"]
    i = 0
    for name, (R, C) in _P0_SPECS:
        fin = io[name]
        fout = io[name + "b"]
        for r0 in range(0, R, 128):
            j = i % 3
            i += 1
            p.dma(qs[j], stg[j].ap[:, 0:C] if C < 2048 else stg[j].ap,
                  fin.ap_[r0:r0 + 128, :], out_buf=stg[j], in_buf=fin)
            eng = "dve" if j % 2 == 0 else "act"
            if eng == "dve":
                p.emit(eng, lambda e, j=j, C=C: e.tensor_copy(
                    out=stb[j][:, 0:C], in_=stg[j][:, 0:C]),
                    reads=[stg[j]], writes=[stb[j]])
            else:
                p.emit(eng, lambda e, j=j, C=C: e.copy(
                    out=stb[j][:, 0:C], in_=stg[j][:, 0:C]),
                    reads=[stg[j]], writes=[stb[j]])
            p.dma(qs[(j + 1) % 3], fout.ap_[r0:r0 + 128, :], stb[j][:, 0:C],
                  out_buf=fout, in_buf=stb[j])


def _p0_io(nc, kind_in="ExternalInput", kind_out="ExternalOutput"):
    io = {}
    for name, (R, C) in _P0_SPECS:
        io[name] = Buf(nc.dram_tensor(name, [R, C], F32, kind=kind_in), name)
        io[name + "b"] = Buf(nc.dram_tensor(name + "b", [R, C], BF16, kind=kind_out), name + "b")
    return io


def build_p0():
    nc = bass.Bass("TRN2", target_bir_lowering=False, debug=False)
    io = _p0_io(nc)
    p = Prog(nc)
    with p.phase():
        emit_p0(p, io)
    p.finish()
    return nc


def _old_build_p0_unused():
    nc = bass.Bass("TRN2", target_bir_lowering=False, debug=False)
    specs = [
        ("w1", [14 * 1024, 2048]), ("w2", [14 * 2048, 1024]),
        ("gug", [8 * 1024, 1024]), ("guu", [8 * 1024, 1024]),
        ("dn", [8 * 1024, 1024]),
    ]
    p = Prog(nc)
    stg = [p.sbuf(f"stg{i}", [128, 2048], F32) for i in range(3)]
    stb = [p.sbuf(f"stb{i}", [128, 2048], BF16) for i in range(3)]
    qs = ["sp", "act", "pool"]
    i = 0
    for name, (R, C) in specs:
        fin = Buf(nc.dram_tensor(name, [R, C], F32, kind="ExternalInput"), name)
        fout = Buf(nc.dram_tensor(name + "b", [R, C], BF16, kind="ExternalOutput"), name + "b")
        for r0 in range(0, R, 128):
            j = i % 3
            i += 1
            p.dma(qs[j], stg[j].ap[:, 0:C] if C < 2048 else stg[j].ap,
                  fin.ap_[r0:r0 + 128, :], out_buf=stg[j], in_buf=fin)
            eng = "dve" if j % 2 == 0 else "act"
            if eng == "dve":
                p.emit(eng, lambda e, j=j, C=C: e.tensor_copy(
                    out=stb[j][:, 0:C], in_=stg[j][:, 0:C]),
                    reads=[stg[j]], writes=[stb[j]])
            else:
                p.emit(eng, lambda e, j=j, C=C: e.copy(
                    out=stb[j][:, 0:C], in_=stg[j][:, 0:C]),
                    reads=[stg[j]], writes=[stb[j]])
            p.dma(qs[(j + 1) % 3], fout.ap_[r0:r0 + 128, :], stb[j][:, 0:C],
                  out_buf=fout, in_buf=stb[j])
    p.finish()
    return nc



_IO_PD = [
    ("XN_FM", [BQ, NCHUNK, 128, T], F32, "ExternalInput"),
    ("S_FM", [BQ, NCHUNK, 128, T], F32, "ExternalInput"),
    ("x", [BQ, T, D], F32, "ExternalInput"),
    ("w1b", [14 * 1024, 2048], BF16, "ExternalInput"),
    ("w2b", [14 * 2048, 1024], BF16, "ExternalInput"),
    ("b1", [14, 2048], F32, "ExternalInput"),
    ("b2", [14, 1024], F32, "ExternalInput"),
    ("router_w", [D, 14], F32, "ExternalInput"),
    ("ident_f32", [128, 128], F32, "ExternalInput"),
    ("iota14", [128, 14], F32, "ExternalInput"),
    ("X1", [BQ, T, D], F32, "ExternalOutput"),
]

def emit_pd(p, io, nbatch=BQ, nexp=14, nsg=None):
    nc = p.nc
    E = 14
    XN = io["XN_FM"]
    SFM = io["S_FM"]
    X0 = io["x"]
    W1B = io["w1b"]
    W2B = io["w2b"]
    B1 = io["b1"]
    B2 = io["b2"]
    RW = io["router_w"]
    idf_in = io["ident_f32"]
    iota_in = io["iota14"]
    X1 = io["X1"]
    emit, dma = p.emit, p.dma

    idf = p.sbuf("idf", [128, 128], F32)
    iot = p.sbuf("iot", [128, E], F32)
    dma("sp", idf.ap, idf_in.ap, out_buf=idf, in_buf=idf_in)
    dma("sp", iot.ap, iota_in.ap, out_buf=iot, in_buf=iota_in)

    rwc = p.sbuf("rwc", [128, NCHUNK * E], F32)
    for c in range(NCHUNK):
        dma("sp", rwc[:, c * E:(c + 1) * E],
            RW.ap_.rearrange("(c p) e -> c p e", p=128)[c], out_buf=rwc,
            in_buf=RW, no_waw=(c > 0))
    xstg = p.sbuf("xstg", [128, 1024], F32)
    b1rows = p.sbuf("b1rows", [E, 1024], F32)
    b2rows = p.sbuf("b2rows", [E, 1024], BF16)
    b1c = p.sbuf("b1c", [128, 16 * E], F32)
    ps_t = p.psum("ps_t", [128, 128], F32)
    dma("sp", xstg[0:E, :], B2.ap, out_buf=xstg, in_buf=B2)
    emit("dve", lambda e: e.tensor_copy(out=b2rows.ap, in_=xstg[0:E, :]),
         reads=[xstg], writes=[b2rows])
    for bh in range(2):
        dma("sp", b1rows.ap, B1.ap_[:, bh * 1024:(bh + 1) * 1024],
            out_buf=b1rows, in_buf=B1)
        for fc8 in range(8):
            fc = bh * 8 + fc8
            emit("pe", lambda e, fc8=fc8: e.transpose(ps_t[:, 0:E],
                 b1rows[:, fc8 * 128:(fc8 + 1) * 128], idf[0:E, 0:E]),
                 reads=[b1rows, idf], writes=[ps_t])
            emit("dve", lambda e, fc=fc: e.tensor_copy(
                 out=b1c[:, fc * E:(fc + 1) * E], in_=ps_t[:, 0:E]),
                 reads=[ps_t], writes=[b1c], no_waw=True)

    xnb = p.sbuf("xnb", [128, NCHUNK * 1024], BF16)
    wbuf = [p.sbuf(f"wbuf{i}", [128, 16384], BF16) for i in range(3)]
    hid = p.sbuf("hid", [128, 16 * 1024], BF16)
    acc = p.sbuf("acc", [128, 8 * 1024], F32)
    lg = p.sbuf("lg", [128, E], F32)
    mx8 = p.sbuf("mx8", [128, 8], F32)
    idxu = p.sbuf("idxu", [128, 8], U32)
    idxf = p.sbuf("idxf", [128, 1], F32)
    nmx = p.sbuf("nmx", [128, 1], F32)
    ssum = p.sbuf("ssum", [128, 1], F32)
    meq = p.sbuf("meq", [128, E], F32)
    CC = p.sbuf("CC", [128, 8 * E], F32)
    CTT = p.sbuf("CTT", [E, 8 * 128], BF16)
    gt = p.sbuf("gt", [128, 1024], F32)
    stm = p.sbuf("stm", [128, 1024], F32)
    x0t = p.sbuf("x0t", [128, 1024], F32)
    sstg = p.sbuf("sstg", [128, 128], F32)
    psA = p.psum("psA", [128, 512], F32)
    psB = p.psum("psB", [128, 512], F32)
    psC = p.psum("psC", [128, 512], F32)
    psD = p.psum("psD", [128, 512], F32)
    psR = p.psum("psR", [128, E], F32)

    NSG = 2 * nbatch if nsg is None else nsg
    for sg in range(NSG):
        b, hh = sg // 2, sg % 2
        tok0 = hh * 1024
        for ti in range(8):
            t0 = tok0 + ti * 128
            for c in range(NCHUNK):
                dma("sp", xstg[:, 0:128], XN.ap_[b, c, :, t0:t0 + 128],
                    out_buf=xstg, in_buf=XN)
                emit("pe", lambda e, c=c: e.matmul(psR.ap, xstg[:, 0:128],
                     rwc[:, c * E:(c + 1) * E], start=(c == 0),
                     stop=(c == NCHUNK - 1)), reads=[xstg, rwc], writes=[psR])
            emit("dve", lambda e: e.tensor_copy(out=lg.ap, in_=psR.ap),
                 reads=[psR], writes=[lg])
            emit("dve", lambda e: e.max(out=mx8.ap, in_=lg.ap),
                 reads=[lg], writes=[mx8])
            emit("dve", lambda e: e.max_index(out=idxu.ap, in_max=mx8.ap,
                 in_values=lg.ap), reads=[mx8, lg], writes=[idxu])
            emit("dve", lambda e: e.tensor_copy(out=idxf.ap, in_=idxu[:, 0:1]),
                 reads=[idxu], writes=[idxf])
            emit("dve", lambda e: e.reduce_max(out=nmx.ap, in_=lg.ap, axis=AX.X,
                 negate=True), reads=[lg], writes=[nmx])
            emit("act", lambda e: e.activation(out=meq.ap, in_=lg.ap, func=AF.Exp,
                 bias=nmx.ap), reads=[lg, nmx], writes=[meq])
            emit("dve", lambda e: e.reduce_sum(out=ssum.ap, in_=meq.ap, axis=AX.X),
                 reads=[meq], writes=[ssum])
            emit("dve", lambda e: e.reciprocal(out=ssum.ap, in_=ssum.ap),
                 reads=[ssum], writes=[ssum])
            emit("dve", lambda e: e.tensor_scalar(out=meq.ap, in0=iot.ap,
                 scalar1=idxf.ap, scalar2=None, op0=OP.is_equal),
                 reads=[iot, idxf], writes=[meq])
            emit("dve", lambda e, ti=ti: e.tensor_scalar(
                 out=CC[:, ti * E:(ti + 1) * E], in0=meq.ap, scalar1=ssum.ap,
                 scalar2=None, op0=OP.mult), reads=[meq, ssum], writes=[CC],
                 no_waw=True)
            emit("pe", lambda e, ti=ti: e.transpose(ps_t[0:E, 0:128],
                 CC[:, ti * E:(ti + 1) * E], idf.ap),
                 reads=[CC, idf], writes=[ps_t])
            emit("dve", lambda e, ti=ti: e.tensor_copy(
                 out=CTT[:, ti * 128:(ti + 1) * 128], in_=ps_t[0:E, 0:128]),
                 reads=[ps_t], writes=[CTT], no_waw=True)
        for c in range(NCHUNK):
            dma("act", xstg.ap, XN.ap_[b, c, :, tok0:tok0 + 1024],
                out_buf=xstg, in_buf=XN)
            emit("dve", lambda e, c=c: e.tensor_copy(
                 out=xnb[:, c * 1024:(c + 1) * 1024], in_=xstg.ap),
                 reads=[xstg], writes=[xnb], no_waw=(c > 0))
        emit("dve", lambda e: e.memset(acc.ap, 0.0), writes=[acc])
        for ei in range(nexp):
            w1t = wbuf[(2 * ei) % 3]
            w2t = wbuf[(2 * ei + 1) % 3]
            for c in range(NCHUNK):
                dma("sp", w1t[:, c * 2048:(c + 1) * 2048],
                    W1B.ap_[ei * 1024 + c * 128: ei * 1024 + (c + 1) * 128, :],
                    out_buf=w1t, in_buf=W1B, no_waw=(c > 0))
            for fc in range(16):
                dma("pool", w2t[:, fc * 1024:(fc + 1) * 1024],
                    W2B.ap_[ei * 2048 + fc * 128: ei * 2048 + (fc + 1) * 128, :],
                    out_buf=w2t, in_buf=W2B, no_waw=(fc > 0))
            for fc in range(16):
                for th in range(2):
                    ps = psA if (fc * 2 + th) % 2 == 0 else psB
                    for k in range(NCHUNK):
                        emit("pe", lambda e, fc=fc, th=th, k=k, ps=ps, w1t=w1t:
                             e.matmul(ps.ap,
                             w1t[:, k * 2048 + fc * 128: k * 2048 + (fc + 1) * 128],
                             xnb[:, k * 1024 + th * 512: k * 1024 + (th + 1) * 512],
                             start=(k == 0), stop=(k == NCHUNK - 1)),
                             reads=[w1t, xnb], writes=[ps])
                    emit("act", lambda e, fc=fc, th=th, ps=ps, ei=ei:
                         e.activation(out=hid[:, fc * 1024 + th * 512:
                                              fc * 1024 + (th + 1) * 512],
                         in_=ps.ap, func=AF.Gelu,
                         bias=b1c[:, fc * E + ei: fc * E + ei + 1]),
                         reads=[ps, b1c], writes=[hid], no_waw=True)
            for ti in range(8):
                for h in range(2):
                    ps = psC if (ti * 2 + h) % 2 == 0 else psD
                    for fc in range(16):
                        emit("pe", lambda e, ti=ti, h=h, fc=fc, ps=ps, w2t=w2t:
                             e.matmul(ps.ap,
                             hid[:, fc * 1024 + ti * 128: fc * 1024 + (ti + 1) * 128],
                             w2t[:, fc * 1024 + h * 512: fc * 1024 + (h + 1) * 512],
                             start=(fc == 0), stop=(fc == 15)),
                             reads=[hid, w2t], writes=[ps])
                    emit("dve", lambda e, ti=ti, h=h, ps=ps, ei=ei:
                         e.scalar_tensor_tensor(
                         out=acc[:, ti * 1024 + h * 512: ti * 1024 + (h + 1) * 512],
                         in0=ps.ap, scalar=CC[:, ti * E + ei: ti * E + ei + 1],
                         in1=acc[:, ti * 1024 + h * 512: ti * 1024 + (h + 1) * 512],
                         op0=OP.mult, op1=OP.add),
                         reads=[ps, CC, acc], writes=[acc], no_waw=True)
        for ti in range(8):
            t0 = tok0 + ti * 128
            for h in range(2):
                ps = psA if h == 0 else psB
                emit("pe", lambda e, ti=ti, h=h, ps=ps: e.matmul(ps.ap,
                     CTT[:, ti * 128:(ti + 1) * 128],
                     b2rows[:, h * 512:(h + 1) * 512], start=True, stop=True),
                     reads=[CTT, b2rows], writes=[ps])
                emit("dve", lambda e, ti=ti, h=h, ps=ps: e.tensor_tensor(
                     out=acc[:, ti * 1024 + h * 512: ti * 1024 + (h + 1) * 512],
                     in0=ps.ap,
                     in1=acc[:, ti * 1024 + h * 512: ti * 1024 + (h + 1) * 512],
                     op=OP.add), reads=[ps, acc], writes=[acc], no_waw=True)
            emit("act", lambda e, ti=ti: e.activation(out=gt.ap,
                 in_=acc[:, ti * 1024:(ti + 1) * 1024], func=AF.Silu),
                 reads=[acc], writes=[gt])
            for c in range(NCHUNK):
                dma("pool", sstg.ap, SFM.ap_[b, c, :, t0:t0 + 128],
                    out_buf=sstg, in_buf=SFM)
                emit("pe", lambda e, c=c: e.transpose(ps_t.ap, sstg.ap, idf.ap),
                     reads=[sstg, idf], writes=[ps_t])
                emit("dve", lambda e, c=c: e.tensor_copy(
                     out=stm[:, c * 128:(c + 1) * 128], in_=ps_t.ap),
                     reads=[ps_t], writes=[stm], no_waw=True)
            dma("sp", x0t.ap, X0.ap_[b, t0:t0 + 128, :], out_buf=x0t, in_buf=X0)
            emit("dve", lambda e: e.tensor_tensor(out=gt.ap, in0=gt.ap,
                 in1=stm.ap, op=OP.mult), reads=[gt, stm], writes=[gt])
            emit("dve", lambda e: e.tensor_tensor(out=gt.ap, in0=gt.ap,
                 in1=x0t.ap, op=OP.add), reads=[gt, x0t], writes=[gt])
            dma("act", X1.ap_[b, t0:t0 + 128, :], gt.ap, out_buf=X1, in_buf=gt)


def build_pd(nbatch=BQ, nexp=14, nsg=None):
    nc = bass.Bass("TRN2", target_bir_lowering=False, debug=False)
    io = _make_io(nc, _IO_PD)
    p = Prog(nc)
    with p.phase():
        emit_pd(p, io, nbatch, nexp, nsg)
    p.finish()
    return nc



_IO_PE = [
    ("X1", [BQ, T, D], F32, "ExternalInput"),
    ("gugb", [8 * 1024, 1024], BF16, "ExternalInput"),
    ("guub", [8 * 1024, 1024], BF16, "ExternalInput"),
    ("dnb", [8 * 1024, 1024], BF16, "ExternalInput"),
    ("gb_g", [8, 1024], F32, "ExternalInput"),
    ("gb_u", [8, 1024], F32, "ExternalInput"),
    ("db", [8, 1024], F32, "ExternalInput"),
    ("router_indices", [T, 2], I32, "ExternalInput"),
    ("routing_weights", [T, 8], F32, "ExternalInput"),
    ("flagcol", [128, 1], F32, "ExternalInput"),
    ("ident_f32", [128, 128], F32, "ExternalInput"),
    ("ident_bf16", [128, 128], BF16, "ExternalInput"),
    ("mask_li", [128, 128], F32, "ExternalInput"),
    ("OUT", [BQ, T, D], F32, "ExternalOutput"),
]

def emit_pe(p, io, nquarter=4, nexp=8, nheads=16):
    nc = p.nc
    E = 8
    X1 = io["X1"]
    GG = io["gugb"]
    GU = io["guub"]
    DN = io["dnb"]
    GBG = io["gb_g"]
    GBU = io["gb_u"]
    DB = io["db"]
    RI = io["router_indices"]
    RWT = io["routing_weights"]
    FLAG = io["flagcol"]
    idf_in = io["ident_f32"]
    idb_in = io["ident_bf16"]
    mli_in = io["mask_li"]
    OUT = io["OUT"]
    emit, dma = p.emit, p.dma

    idf = p.sbuf("idf", [128, 128], F32)
    idb = p.sbuf("idb", [128, 128], BF16)
    mli = p.sbuf("mli", [128, 128], F32)
    flagc = p.sbuf("flagc", [128, 1], F32)
    eighth = p.sbuf("eighth", [128, 1], F32)
    c1702 = p.sbuf("c1702", [128, 1], F32)
    eps5 = p.sbuf("eps5", [128, 1], F32)
    dma("sp", idf.ap, idf_in.ap, out_buf=idf, in_buf=idf_in)
    dma("sp", idb.ap, idb_in.ap, out_buf=idb, in_buf=idb_in)
    dma("sp", mli.ap, mli_in.ap, out_buf=mli, in_buf=mli_in)
    dma("sp", flagc.ap, FLAG.ap, out_buf=flagc, in_buf=FLAG)
    emit("dve", lambda e: e.memset(eighth.ap, 0.125), writes=[eighth])
    emit("dve", lambda e: e.memset(c1702.ap, 1.702), writes=[c1702])
    emit("dve", lambda e: e.memset(eps5.ap, 1e-5), writes=[eps5])

    xstg = p.sbuf("xstg", [128, D], F32)
    for b in range(1, BQ):
        for tile in range(NTILE):
            t0 = tile * 128
            dma("pool", xstg.ap, X1.ap_[b, t0:t0 + 128, :], out_buf=xstg, in_buf=X1)
            dma("pool", OUT.ap_[b, t0:t0 + 128, :], xstg.ap, out_buf=OUT, in_buf=xstg)

    gbgc = p.sbuf("gbgc", [128, 8 * E], F32)
    gbuc = p.sbuf("gbuc", [128, 8 * E], F32)
    dbr = p.sbuf("dbr", [E, 1024], BF16)
    ps_t = p.psum("ps_t", [128, 128], F32)
    for src_b, dst in ((GBG, gbgc), (GBU, gbuc)):
        dma("sp", xstg[0:E, :], src_b.ap, out_buf=xstg, in_buf=src_b)
        for fc in range(8):
            emit("pe", lambda e, fc=fc: e.transpose(ps_t[:, 0:E],
                 xstg[0:E, fc * 128:(fc + 1) * 128], idf[0:E, 0:E]),
                 reads=[xstg, idf], writes=[ps_t])
            emit("dve", lambda e, fc=fc, dst=dst: e.tensor_copy(
                 out=dst[:, fc * E:(fc + 1) * E], in_=ps_t[:, 0:E]),
                 reads=[ps_t], writes=[dst], no_waw=True)
    dma("sp", xstg[0:E, :], DB.ap, out_buf=xstg, in_buf=DB)
    emit("dve", lambda e: e.tensor_copy(out=dbr.ap, in_=xstg[0:E, :]),
         reads=[xstg], writes=[dbr])

    ri2 = p.sbuf("ri2", [128, 2], I32)
    rif = p.sbuf("rif", [128, 2], F32)
    rwt = p.sbuf("rwt", [128, E], F32)
    eqt = p.sbuf("eqt", [128, E], F32)
    C1 = p.sbuf("C1", [128, NTILE * E], F32)
    C1T = p.sbuf("C1T", [E, NTILE * 128], BF16)
    iota8 = p.sbuf("iota8", [128, E], F32)
    for e8 in range(E):
        emit("dve", lambda e, e8=e8: e.memset(iota8[:, e8:e8 + 1], float(e8)),
             writes=[iota8], no_waw=(e8 > 0))
    for tile in range(NTILE):
        t0 = tile * 128
        dma("sp", ri2.ap, RI.ap_[t0:t0 + 128, :], out_buf=ri2, in_buf=RI)
        dma("sp", rwt.ap, RWT.ap_[t0:t0 + 128, :], out_buf=rwt, in_buf=RWT)
        emit("dve", lambda e: e.tensor_copy(out=rif.ap, in_=ri2.ap),
             reads=[ri2], writes=[rif])
        emit("dve", lambda e: e.tensor_scalar(out=eqt.ap, in0=iota8.ap,
             scalar1=rif[:, 0:1], scalar2=None, op0=OP.is_equal),
             reads=[iota8, rif], writes=[eqt])
        emit("dve", lambda e: e.tensor_scalar(out=rwt.ap, in0=iota8.ap,
             scalar1=rif[:, 1:2], scalar2=rwt.ap, op0=OP.is_equal, op1=OP.bypass),
             reads=[iota8, rif], writes=[]) if False else None
        emit("dve", lambda e: e.tensor_scalar(out=rif.ap, in0=iota8[:, 0:2],
             scalar1=rif[:, 1:2], scalar2=None, op0=OP.is_equal),
             reads=[iota8, rif], writes=[rif]) if False else None
        emit("dve", lambda e, tile=tile: e.tensor_scalar(
             out=C1[:, tile * E:(tile + 1) * E],
             in0=iota8.ap, scalar1=rif[:, 1:2], scalar2=None, op0=OP.is_equal),
             reads=[iota8, rif], writes=[C1], no_waw=True)
        emit("dve", lambda e, tile=tile: e.tensor_tensor(
             out=eqt.ap, in0=eqt.ap, in1=C1[:, tile * E:(tile + 1) * E],
             op=OP.add), reads=[eqt, C1], writes=[eqt])
        emit("dve", lambda e, tile=tile: e.tensor_tensor(
             out=C1[:, tile * E:(tile + 1) * E], in0=eqt.ap, in1=rwt.ap,
             op=OP.mult), reads=[eqt, rwt], writes=[C1], no_waw=True)
        emit("pe", lambda e, tile=tile: e.transpose(ps_t[0:E, 0:128],
             C1[:, tile * E:(tile + 1) * E], idf.ap),
             reads=[C1, idf], writes=[ps_t])
        emit("dve", lambda e, tile=tile: e.tensor_copy(
             out=C1T[:, tile * 128:(tile + 1) * 128], in_=ps_t[0:E, 0:128]),
             reads=[ps_t], writes=[C1T], no_waw=True)

    XN1B = p.sbuf("XN1B", [128, NTILE * D], BF16)
    XN1FM = p.sbuf("XN1FM", [128, NCHUNK * T], BF16)
    xsq = p.sbuf("xsq", [128, D], F32)
    s1c = p.sbuf("s1c", [128, 1], F32)
    s2c = p.sbuf("s2c", [128, 1], F32)
    mcol = p.sbuf("mcol", [128, 1], F32)
    vcol = p.sbuf("vcol", [128, 1], F32)
    rcol = p.sbuf("rcol", [128, 1], F32)
    xnb_t = p.sbuf("xnb_t", [128, D], BF16)
    ps_tb = p.psum("ps_tb", [128, 128], BF16)
    tstg = p.sbuf("tstg", [128, 128], BF16)
    inv_d = 1.0 / D
    for tile in range(NTILE):
        t0 = tile * 128
        dma("sp", xstg.ap, X1.ap_[0, t0:t0 + 128, :], out_buf=xstg, in_buf=X1)
        emit("act", lambda e: e.square(out=xsq.ap, in_=xstg.ap),
             reads=[xstg], writes=[xsq])
        emit("dve", lambda e: e.reduce_sum(out=s1c.ap, in_=xstg.ap, axis=AX.X),
             reads=[xstg], writes=[s1c])
        emit("dve", lambda e: e.reduce_sum(out=s2c.ap, in_=xsq.ap, axis=AX.X),
             reads=[xsq], writes=[s2c])
        emit("dve", lambda e: e.tensor_scalar(out=mcol.ap, in0=s1c.ap,
             scalar1=inv_d, scalar2=None, op0=OP.mult), reads=[s1c], writes=[mcol])
        emit("dve", lambda e: e.tensor_scalar(out=vcol.ap, in0=s2c.ap,
             scalar1=inv_d, scalar2=None, op0=OP.mult), reads=[s2c], writes=[vcol])
        emit("dve", lambda e: e.tensor_tensor(out=rcol.ap, in0=mcol.ap,
             in1=mcol.ap, op=OP.mult), reads=[mcol], writes=[rcol])
        emit("dve", lambda e: e.tensor_tensor(out=vcol.ap, in0=vcol.ap,
             in1=rcol.ap, op=OP.subtract), reads=[vcol, rcol], writes=[vcol])
        emit("act", lambda e: e.activation(out=rcol.ap, in_=vcol.ap,
             func=AF.Sqrt, bias=eps5.ap), reads=[vcol, eps5], writes=[rcol])
        emit("dve", lambda e: e.reciprocal(out=rcol.ap, in_=rcol.ap),
             reads=[rcol], writes=[rcol])
        emit("dve", lambda e, tile=tile: e.tensor_scalar(
             out=XN1B[:, tile * D:(tile + 1) * D], in0=xstg.ap,
             scalar1=mcol.ap, scalar2=rcol.ap, op0=OP.subtract, op1=OP.mult),
             reads=[xstg, mcol, rcol], writes=[XN1B], no_waw=True)
        emit("dve", lambda e, tile=tile: e.tensor_copy(out=xnb_t.ap,
             in_=XN1B[:, tile * D:(tile + 1) * D]), reads=[XN1B], writes=[xnb_t])
        for c in range(NCHUNK):
            emit("pe", lambda e, c=c: e.transpose(ps_tb.ap,
                 xnb_t[:, c * 128:(c + 1) * 128], idb.ap),
                 reads=[xnb_t, idb], writes=[ps_tb])
            emit("dve", lambda e: e.tensor_copy(out=tstg.ap, in_=ps_tb.ap),
                 reads=[ps_tb], writes=[tstg])
            emit("act", lambda e, c=c, tile=tile: e.copy(
                 out=XN1FM[:, c * T + tile * 128: c * T + (tile + 1) * 128],
                 in_=tstg.ap), reads=[tstg], writes=[XN1FM], no_waw=True)

    sc = p.sbuf("sc", [128, T], F32)
    pexp = p.sbuf("pexp", [128, T], F32)
    rmx = p.sbuf("rmx", [128, 1], F32)
    nm8 = p.sbuf("nm8", [128, 1], F32)
    rsm = p.sbuf("rsm", [128, 1], F32)
    pb = p.sbuf("pb", [128, 128], BF16)
    pT = p.sbuf("pT", [128, 128], BF16)
    OATT = p.sbuf("OATT", [128, 4 * D], F32)
    acc1 = p.sbuf("acc1", [128, 4 * D], F32)
    wL = [p.sbuf(f"wL{i}", [128, 8 * 1024], BF16) for i in range(3)]
    ggt = p.sbuf("ggt", [128, 512], F32)
    sgt = p.sbuf("sgt", [128, 512], F32)
    gut = p.sbuf("gut", [128, 512], F32)
    inter = p.sbuf("inter", [128, 8 * 512], BF16)
    gt1 = p.sbuf("gt1", [128, D], F32)
    x1t = p.sbuf("x1t", [128, D], F32)
    psS = p.psum("psS", [128, 512], F32)
    psS2 = p.psum("psS2", [128, 512], F32)
    psV = p.psum("psV", [128, 64], F32)
    psM1 = p.psum("psM1", [128, 512], F32)
    psM2 = p.psum("psM2", [128, 512], F32)

    for q4 in range(nquarter):
        for h in range(nheads):
            ch, pb0 = h // 2, (h % 2) * 64
            for qi in range(4):
                qb = q4 * 4 + qi
                nk = (qb + 1) * 128
                nkg = (nk + 511) // 512
                for kg in range(nkg):
                    wdt = min(512, nk - kg * 512)
                    ps = psS if kg % 2 == 0 else psS2
                    emit("pe", lambda e, ch=ch, pb0=pb0, qb=qb, kg=kg, wdt=wdt, ps=ps:
                         e.matmul(ps[:, 0:wdt],
                         XN1FM[pb0:pb0 + 64, ch * T + qb * 128: ch * T + (qb + 1) * 128],
                         XN1FM[pb0:pb0 + 64, ch * T + kg * 512: ch * T + kg * 512 + wdt],
                         start=True, stop=True),
                         reads=[XN1FM], writes=[ps])
                    emit("dve", lambda e, kg=kg, wdt=wdt, ps=ps: e.tensor_copy(
                         out=sc[:, kg * 512: kg * 512 + wdt], in_=ps[:, 0:wdt]),
                         reads=[ps], writes=[sc], no_waw=(kg > 0))
                emit("dve", lambda e, nk=nk: e.reduce_max(out=rmx.ap,
                     in_=sc[:, 0:nk], axis=AX.X), reads=[sc], writes=[rmx])
                emit("dve", lambda e: e.tensor_scalar(out=nm8.ap, in0=rmx.ap,
                     scalar1=-0.125, scalar2=None, op0=OP.mult),
                     reads=[rmx], writes=[nm8])
                emit("act", lambda e, nk=nk: e.activation(out=pexp[:, 0:nk],
                     in_=sc[:, 0:nk], func=AF.Exp, bias=nm8.ap, scale=eighth.ap),
                     reads=[sc, nm8, eighth], writes=[pexp])
                emit("dve", lambda e, qb=qb: e.tensor_tensor(
                     out=pexp[:, qb * 128:(qb + 1) * 128],
                     in0=pexp[:, qb * 128:(qb + 1) * 128], in1=mli.ap, op=OP.mult),
                     reads=[pexp, mli], writes=[pexp], no_waw=True)
                emit("dve", lambda e, nk=nk: e.reduce_sum(out=rsm.ap,
                     in_=pexp[:, 0:nk], axis=AX.X), reads=[pexp], writes=[rsm])
                emit("dve", lambda e: e.reciprocal(out=rsm.ap, in_=rsm.ap),
                     reads=[rsm], writes=[rsm])
                for kb in range(qb + 1):
                    emit("act", lambda e, kb=kb: e.copy(out=pb.ap,
                         in_=pexp[:, kb * 128:(kb + 1) * 128]),
                         reads=[pexp], writes=[pb])
                    emit("pe", lambda e: e.transpose(ps_tb.ap, pb.ap, idb.ap),
                         reads=[pb, idb], writes=[ps_tb])
                    emit("dve", lambda e: e.tensor_copy(out=pT.ap, in_=ps_tb.ap),
                         reads=[ps_tb], writes=[pT])
                    emit("pe", lambda e, kb=kb, h=h, qb=qb: e.matmul(psV.ap,
                         pT.ap, XN1B[:, kb * D + h * 64: kb * D + (h + 1) * 64],
                         start=(kb == 0), stop=(kb == qb)),
                         reads=[pT, XN1B], writes=[psV])
                emit("dve", lambda e, qi=qi, h=h: e.tensor_scalar(
                     out=OATT[:, qi * D + h * 64: qi * D + (h + 1) * 64],
                     in0=psV.ap, scalar1=rsm.ap, scalar2=None, op0=OP.mult),
                     reads=[psV, rsm], writes=[OATT], no_waw=True)

        emit("dve", lambda e: e.memset(acc1.ap, 0.0), writes=[acc1])
        for ei in range(nexp):
            bgg = wL[(3 * ei) % 3]
            bgu = wL[(3 * ei + 1) % 3]
            bdn = wL[(3 * ei + 2) % 3]
            for c in range(NCHUNK):
                dma("sp", bgg[:, c * 1024:(c + 1) * 1024],
                    GG.ap_[ei * 1024 + c * 128: ei * 1024 + (c + 1) * 128, :],
                    out_buf=bgg, in_buf=GG, no_waw=(c > 0))
                dma("act", bgu[:, c * 1024:(c + 1) * 1024],
                    GU.ap_[ei * 1024 + c * 128: ei * 1024 + (c + 1) * 128, :],
                    out_buf=bgu, in_buf=GU, no_waw=(c > 0))
                dma("pool", bdn[:, c * 1024:(c + 1) * 1024],
                    DN.ap_[ei * 1024 + c * 128: ei * 1024 + (c + 1) * 128, :],
                    out_buf=bdn, in_buf=DN, no_waw=(c > 0))
            for fc in range(8):
                for k in range(NCHUNK):
                    emit("pe", lambda e, fc=fc, k=k, q4=q4, bgg=bgg: e.matmul(
                         psM1.ap, bgg[:, k * 1024 + fc * 128: k * 1024 + (fc + 1) * 128],
                         XN1FM[:, k * T + q4 * 512: k * T + (q4 + 1) * 512],
                         start=(k == 0), stop=(k == NCHUNK - 1)),
                         reads=[bgg, XN1FM], writes=[psM1])
                emit("dve", lambda e, fc=fc, ei=ei: e.tensor_scalar(out=ggt.ap,
                     in0=psM1.ap, scalar1=gbgc[:, fc * E + ei: fc * E + ei + 1],
                     scalar2=None, op0=OP.add),
                     reads=[psM1, gbgc], writes=[ggt])
                emit("dve", lambda e: e.tensor_scalar(out=ggt.ap, in0=ggt.ap,
                     scalar1=7.0, scalar2=None, op0=OP.min),
                     reads=[ggt], writes=[ggt])
                emit("act", lambda e: e.activation(out=sgt.ap, in_=ggt.ap,
                     func=AF.Sigmoid, scale=c1702.ap), reads=[ggt, c1702], writes=[sgt])
                emit("dve", lambda e: e.tensor_tensor(out=ggt.ap, in0=ggt.ap,
                     in1=sgt.ap, op=OP.mult), reads=[ggt, sgt], writes=[ggt])
                for k in range(NCHUNK):
                    emit("pe", lambda e, fc=fc, k=k, q4=q4, bgu=bgu: e.matmul(
                         psM2.ap, bgu[:, k * 1024 + fc * 128: k * 1024 + (fc + 1) * 128],
                         XN1FM[:, k * T + q4 * 512: k * T + (q4 + 1) * 512],
                         start=(k == 0), stop=(k == NCHUNK - 1)),
                         reads=[bgu, XN1FM], writes=[psM2])
                emit("dve", lambda e, fc=fc, ei=ei: e.tensor_scalar(out=gut.ap,
                     in0=psM2.ap, scalar1=gbuc[:, fc * E + ei: fc * E + ei + 1],
                     scalar2=None, op0=OP.add),
                     reads=[psM2, gbuc], writes=[gut])
                emit("dve", lambda e: e.tensor_scalar(out=gut.ap, in0=gut.ap,
                     scalar1=-7.0, scalar2=7.0, op0=OP.max, op1=OP.min),
                     reads=[gut], writes=[gut])
                emit("dve", lambda e: e.tensor_scalar(out=gut.ap, in0=gut.ap,
                     scalar1=1.0, scalar2=None, op0=OP.add),
                     reads=[gut], writes=[gut])
                emit("dve", lambda e, fc=fc: e.tensor_tensor(
                     out=inter[:, fc * 512:(fc + 1) * 512], in0=gut.ap,
                     in1=ggt.ap, op=OP.mult), reads=[gut, ggt], writes=[inter],
                     no_waw=True)
            for ti in range(4):
                for hh2 in range(2):
                    ps = psM1 if hh2 == 0 else psM2
                    for fc in range(8):
                        emit("pe", lambda e, ti=ti, hh2=hh2, fc=fc, ps=ps, bdn=bdn:
                             e.matmul(ps.ap,
                             inter[:, fc * 512 + ti * 128: fc * 512 + (ti + 1) * 128],
                             bdn[:, fc * 1024 + hh2 * 512: fc * 1024 + (hh2 + 1) * 512],
                             start=(fc == 0), stop=(fc == 7)),
                             reads=[inter, bdn], writes=[ps])
                    emit("dve", lambda e, ti=ti, hh2=hh2, ps=ps, ei=ei, q4=q4:
                         e.scalar_tensor_tensor(
                         out=acc1[:, ti * D + hh2 * 512: ti * D + (hh2 + 1) * 512],
                         in0=ps.ap,
                         scalar=C1[:, (q4 * 4 + ti) * E + ei: (q4 * 4 + ti) * E + ei + 1],
                         in1=acc1[:, ti * D + hh2 * 512: ti * D + (hh2 + 1) * 512],
                         op0=OP.mult, op1=OP.add),
                         reads=[ps, C1, acc1], writes=[acc1], no_waw=True)
        for ti in range(4):
            tile = q4 * 4 + ti
            t0 = tile * 128
            for hh2 in range(2):
                ps = psM1 if hh2 == 0 else psM2
                emit("pe", lambda e, tile=tile, hh2=hh2, ps=ps: e.matmul(ps.ap,
                     C1T[:, tile * 128:(tile + 1) * 128],
                     dbr[:, hh2 * 512:(hh2 + 1) * 512], start=True, stop=True),
                     reads=[C1T, dbr], writes=[ps])
                emit("dve", lambda e, ti=ti, hh2=hh2, ps=ps: e.tensor_tensor(
                     out=acc1[:, ti * D + hh2 * 512: ti * D + (hh2 + 1) * 512],
                     in0=ps.ap,
                     in1=acc1[:, ti * D + hh2 * 512: ti * D + (hh2 + 1) * 512],
                     op=OP.add), reads=[ps, acc1], writes=[acc1], no_waw=True)
            emit("act", lambda e, ti=ti: e.activation(out=gt1.ap,
                 in_=acc1[:, ti * D:(ti + 1) * D], func=AF.Silu),
                 reads=[acc1], writes=[gt1])
            dma("sp", x1t.ap, X1.ap_[0, t0:t0 + 128, :], out_buf=x1t, in_buf=X1)
            emit("dve", lambda e, ti=ti: e.tensor_tensor(out=gt1.ap, in0=gt1.ap,
                 in1=OATT[:, ti * D:(ti + 1) * D], op=OP.mult),
                 reads=[gt1, OATT], writes=[gt1])
            emit("dve", lambda e: e.tensor_scalar(out=gt1.ap, in0=gt1.ap,
                 scalar1=flagc.ap, scalar2=None, op0=OP.mult),
                 reads=[gt1, flagc], writes=[gt1])
            emit("dve", lambda e: e.tensor_tensor(out=gt1.ap, in0=gt1.ap,
                 in1=x1t.ap, op=OP.add), reads=[gt1, x1t], writes=[gt1])
            dma("act", OUT.ap_[0, t0:t0 + 128, :], gt1.ap, out_buf=OUT, in_buf=gt1)


def build_pe(nquarter=4, nexp=8, nheads=16):
    nc = bass.Bass("TRN2", target_bir_lowering=False, debug=False)
    io = _make_io(nc, _IO_PE)
    p = Prog(nc)
    with p.phase():
        emit_pe(p, io, nquarter, nexp, nheads)
    p.finish()
    return nc



_FUSED_EXTERNAL_IN = {
    "x", "Wq", "Wk", "Wv", "beta_w", "beta_b", "norm1_w", "alpha_logit",
    "ident_f32", "ident_bf16", "mask_su", "mask_sl", "mask_ui", "mask_li",
    "Wo", "norm2_w", "w_gate", "w_up", "w_down", "b1", "b2", "router_w",
    "iota14", "gb_g", "gb_u", "db", "router_indices", "routing_weights",
    "flagcol", "w1", "w2", "gug", "guu", "dn",
}


def build_fused():
    nc = bass.Bass("TRN2", target_bir_lowering=False, debug=False)
    specs = {}
    for name, (R, C) in _P0_SPECS:
        specs[name] = ([R, C], F32)
        specs[name + "b"] = ([R, C], BF16)
    for spec in (_IO_P1, _IO_PDELTA, _IO_PC, _IO_PD, _IO_PE):
        for name, shape, dt, kind in spec:
            specs.setdefault(name, (shape, dt))
    io = {}
    for name, (shape, dt) in specs.items():
        if name == "OUT":
            kind = "ExternalOutput"
        elif name in _FUSED_EXTERNAL_IN:
            kind = "ExternalInput"
        else:
            kind = "Internal"
        io[name] = Buf(nc.dram_tensor(name, shape, dt, kind=kind), name)
    p = Prog(nc)
    with p.phase():
        emit_p0(p, io)
    with p.phase():
        emit_p1(p, io)
    with p.phase():
        emit_pdelta(p, io)
    with p.phase():
        emit_pc(p, io)
    with p.phase():
        emit_pd(p, io)
    with p.phase():
        emit_pe(p, io)
    p.finish()
    return nc


def kernel(**inputs):
    inp = {k: np.asarray(v) for k, v in inputs.items()}
    consts = _host_consts()
    asc = np.ascontiguousarray
    cores = list(range(_NCORES))
    nc = build_fused()
    maps = []
    for c in cores:
        maps.append({
            "x": inp["L0_attn_x"][4 * c:4 * c + 4],
            "Wq": inp["L0_attn_Wq"], "Wk": inp["L0_attn_Wk"], "Wv": inp["L0_attn_Wv"],
            "beta_w": inp["L0_attn_beta_w"], "beta_b": inp["L0_attn_beta_b"],
            "norm1_w": inp["L0_attn_norm1_w"], "alpha_logit": inp["L0_attn_alpha_logit"],
            "Wo": inp["L0_attn_Wo"], "norm2_w": inp["L0_attn_norm2_w"],
            "w_gate": inp["L0_attn_w_gate"], "w_up": inp["L0_attn_w_up"],
            "w_down": inp["L0_attn_w_down"],
            "w1": inp["L0_ffn_w1"].reshape(14 * 1024, 2048),
            "w2": inp["L0_ffn_w2"].reshape(14 * 2048, 1024),
            "b1": inp["L0_ffn_b1"], "b2": inp["L0_ffn_b2"],
            "router_w": inp["L0_ffn_router_w"],
            "gug": asc(inp["L1_ffn_gate_up_proj"][:, :, 0::2]).reshape(8 * 1024, 1024),
            "guu": asc(inp["L1_ffn_gate_up_proj"][:, :, 1::2]).reshape(8 * 1024, 1024),
            "dn": inp["L1_ffn_down_proj"].reshape(8 * 1024, 1024),
            "gb_g": asc(inp["L1_ffn_gate_up_proj_bias"][:, 0::2]),
            "gb_u": asc(inp["L1_ffn_gate_up_proj_bias"][:, 1::2]),
            "db": inp["L1_ffn_down_proj_bias"],
            "router_indices": inp["L1_ffn_router_indices"],
            "routing_weights": inp["L1_ffn_routing_weights"],
            "flagcol": np.full((128, 1), 1.0 if c == 0 else 0.0, np.float32),
            "ident_f32": consts["ident_f32"], "ident_bf16": consts["ident_bf16"],
            "mask_su": np.triu(np.ones((128, 128), np.float32), 1),
            "mask_sl": np.tril(np.ones((128, 128), np.float32), -1),
            "mask_ui": np.triu(np.ones((128, 128), np.float32), 0),
            "mask_li": np.tril(np.ones((128, 128), np.float32)),
            "iota14": np.broadcast_to(np.arange(14, dtype=np.float32), (128, 14)).copy(),
        })
    res = run_bass_kernel_spmd(nc, maps, core_ids=cores)
    return np.concatenate([res.results[c]["OUT"] for c in cores], axis=0)




_NCORES = 8


def kernel_unfused(**inputs):
    inp = {k: np.asarray(v) for k, v in inputs.items()}
    consts = _host_consts()
    msu = np.triu(np.ones((128, 128), np.float32), 1)
    msl = np.tril(np.ones((128, 128), np.float32), -1)
    mui = np.triu(np.ones((128, 128), np.float32), 0)
    mli = np.tril(np.ones((128, 128), np.float32))
    iota14 = np.broadcast_to(np.arange(14, dtype=np.float32), (128, 14)).copy()
    cores = list(range(_NCORES))
    asc = np.ascontiguousarray

    nc = build_p0()
    m0 = {
        "w1": inp["L0_ffn_w1"].reshape(14 * 1024, 2048),
        "w2": inp["L0_ffn_w2"].reshape(14 * 2048, 1024),
        "gug": asc(inp["L1_ffn_gate_up_proj"][:, :, 0::2]).reshape(8 * 1024, 1024),
        "guu": asc(inp["L1_ffn_gate_up_proj"][:, :, 1::2]).reshape(8 * 1024, 1024),
        "dn": inp["L1_ffn_down_proj"].reshape(8 * 1024, 1024),
    }
    r0 = run_bass_kernel_spmd(nc, [m0] * _NCORES, core_ids=cores).results

    nc = build_p1()
    maps = []
    for c in cores:
        maps.append({
            "x": inp["L0_attn_x"][4 * c:4 * c + 4],
            "Wq": inp["L0_attn_Wq"], "Wk": inp["L0_attn_Wk"], "Wv": inp["L0_attn_Wv"],
            "beta_w": inp["L0_attn_beta_w"], "beta_b": inp["L0_attn_beta_b"],
            "norm1_w": inp["L0_attn_norm1_w"], "alpha_logit": inp["L0_attn_alpha_logit"],
            "ident_f32": consts["ident_f32"], "ident_bf16": consts["ident_bf16"],
        })
    r1 = run_bass_kernel_spmd(nc, maps, core_ids=cores).results

    nc = build_pdelta()
    maps = []
    for c in cores:
        maps.append({
            "Q_FM": r1[c]["Q_FM"], "K_FM": r1[c]["K_FM"], "K_TM": r1[c]["K_TM"],
            "V_TM": r1[c]["V_TM"], "BETA": r1[c]["BETA"], "RSTD2": r1[c]["RSTD2"],
            "ident_bf16": consts["ident_bf16"],
            "mask_su": msu, "mask_sl": msl, "mask_ui": mui,
        })
    r2 = run_bass_kernel_spmd(nc, maps, core_ids=cores).results

    nc = build_pc()
    maps = []
    for c in cores:
        maps.append({
            "XN_FM": r1[c]["XN_FM"], "O_FM": r2[c]["O_FM"],
            "Wo": inp["L0_attn_Wo"], "norm2_w": inp["L0_attn_norm2_w"],
            "w_gate": inp["L0_attn_w_gate"], "w_up": inp["L0_attn_w_up"],
            "w_down": inp["L0_attn_w_down"], "ident_f32": consts["ident_f32"],
        })
    r3 = run_bass_kernel_spmd(nc, maps, core_ids=cores).results

    nc = build_pd()
    maps = []
    for c in cores:
        maps.append({
            "XN_FM": r1[c]["XN_FM"], "S_FM": r3[c]["S_FM"],
            "x": inp["L0_attn_x"][4 * c:4 * c + 4],
            "w1b": r0[c]["w1b"], "w2b": r0[c]["w2b"],
            "b1": inp["L0_ffn_b1"], "b2": inp["L0_ffn_b2"],
            "router_w": inp["L0_ffn_router_w"],
            "ident_f32": consts["ident_f32"], "iota14": iota14,
        })
    r4 = run_bass_kernel_spmd(nc, maps, core_ids=cores).results

    nc = build_pe()
    maps = []
    for c in cores:
        flag = np.full((128, 1), 1.0 if c == 0 else 0.0, np.float32)
        maps.append({
            "X1": r4[c]["X1"],
            "gugb": r0[c]["gugb"], "guub": r0[c]["guub"], "dnb": r0[c]["dnb"],
            "gb_g": asc(inp["L1_ffn_gate_up_proj_bias"][:, 0::2]),
            "gb_u": asc(inp["L1_ffn_gate_up_proj_bias"][:, 1::2]),
            "db": inp["L1_ffn_down_proj_bias"],
            "router_indices": inp["L1_ffn_router_indices"],
            "routing_weights": inp["L1_ffn_routing_weights"],
            "flagcol": flag,
            "ident_f32": consts["ident_f32"], "ident_bf16": consts["ident_bf16"],
            "mask_li": mli,
        })
    r5 = run_bass_kernel_spmd(nc, maps, core_ids=cores).results

    return np.concatenate([r5[c]["OUT"] for c in cores], axis=0)


# version 04
# speedup vs baseline: 1.0706x; 1.0706x over previous
import contextlib
import math
from bisect import bisect_left

import numpy as np
import ml_dtypes

import concourse.bass as bass
import concourse.mybir as mybir
from concourse.bass_utils import run_bass_kernel_spmd

F32 = mybir.dt.float32
BF16 = mybir.dt.bfloat16
U32 = mybir.dt.uint32
I32 = mybir.dt.int32
AF = mybir.ActivationFunctionType
OP = mybir.AluOpType
AX = mybir.AxisListType

BQ = 4
T = 2048
D = 1024
NCHUNK = D // 128
NTILE = T // 128



class _Op:
    __slots__ = ("fn", "waits", "incs", "is_dma")

    def __init__(self, fn, is_dma=False):
        self.fn = fn
        self.is_dma = is_dma
        self.waits = []
        self.incs = []


class _Dom:

    __slots__ = ("name", "sem", "ops", "inc_idx", "inc_cum", "step")

    def __init__(self, name, sem, step):
        self.name = name
        self.sem = sem
        self.ops = []
        self.inc_idx = []
        self.inc_cum = []
        self.step = step


class Buf:
    __slots__ = ("ap_", "writers", "readers", "name", "is_psum")

    def __init__(self, handle, name, is_psum=False):
        self.ap_ = handle.ap() if hasattr(handle, "ap") else handle
        self.name = name
        self.is_psum = is_psum
        self.writers = {}
        self.readers = {}

    def __getitem__(self, key):
        return self.ap_[key]

    @property
    def ap(self):
        return self.ap_[:]


ENGINES = ("pe", "act", "dve", "pool", "sp")
_ENG_ATTR = {"pe": "tensor", "act": "scalar", "dve": "vector",
             "pool": "gpsimd", "sp": "sync"}
_QUEUE_OF = {"sp": "q_sp", "act": "q_act", "pool": "q_pool", "dve": "q_dve"}


class Prog:
    def __init__(self, nc):
        self.nc = nc
        self.stack = contextlib.ExitStack()
        self.dom = {}
        for e in ENGINES:
            sem = self.stack.enter_context(nc.semaphore(f"s_{e}"))
            self.dom[e] = _Dom(e, sem, 1)
        for q in set(_QUEUE_OF.values()):
            sem = self.stack.enter_context(nc.semaphore(f"s_{q}"))
            self.dom[q] = _Dom(q, sem, 16)
        self.waited = {e: {} for e in ENGINES}
        self.bufs = []
        self._phase_stack = None
        self._pending_barrier = {}
        self._names = {}

    def _uname(self, name):
        if name in self._names:
            self._names[name] += 1
            return f"{name}__{self._names[name]}"
        self._names[name] = 0
        return name

    def sbuf(self, name, shape, dtype):
        st = self._phase_stack if self._phase_stack is not None else self.stack
        h = st.enter_context(self.nc.sbuf_tensor(self._uname(name), shape, dtype))
        b = Buf(h, name)
        self.bufs.append(b)
        return b

    def psum(self, name, shape, dtype):
        st = self._phase_stack if self._phase_stack is not None else self.stack
        h = st.enter_context(self.nc.psum_tensor(self._uname(name), shape, dtype))
        b = Buf(h, name, is_psum=True)
        self.bufs.append(b)
        return b

    def barrier(self):
        for dn, d in self.dom.items():
            if not d.ops:
                continue
            idx = len(d.ops)
            if dn in ENGINES:
                while idx > 0 and d.ops[idx - 1].is_dma:
                    idx -= 1
                if idx == 0:
                    continue
            v = self._ensure_inc(dn, idx)
            self._pending_barrier[dn] = v

    @contextlib.contextmanager
    def phase(self, name=""):
        self._phase_stack = contextlib.ExitStack()
        try:
            yield
        finally:
            self.barrier()
            self._phase_stack.close()
            self._phase_stack = None

    def dram(self, name, shape, dtype, kind="Internal"):
        h = self.nc.dram_tensor(name, shape, dtype, kind=kind)
        b = Buf(h, name)
        self.bufs.append(b)
        return b

    def _ensure_inc(self, dname, idx):
        d = self.dom[dname]
        if d.inc_idx and d.inc_idx[-1] >= idx:
            i = bisect_left(d.inc_idx, idx)
            return d.inc_cum[i]
        op = d.ops[idx - 1]
        cum = (d.inc_cum[-1] if d.inc_cum else 0) + d.step
        op.incs.append((d.sem, d.step))
        d.inc_idx.append(idx)
        d.inc_cum.append(cum)
        return cum

    def emit(self, eng, fn, reads=(), writes=(), dma_q=None, no_waw=False):
        op = _Op(fn, is_dma=bool(dma_q))
        if self._pending_barrier:
            w = self.waited[eng]
            for dn, v in self._pending_barrier.items():
                if w.get(dn, 0) < v:
                    op.waits.append((self.dom[dn].sem, v))
                    w[dn] = v
        deps = {}
        for b in reads:
            for dn, (idx, _o) in b.writers.items():
                if deps.get(dn, 0) < idx:
                    deps[dn] = idx
            if b.is_psum:
                for dn, (idx, _o) in b.readers.items():
                    if dn != eng and deps.get(dn, 0) < idx:
                        deps[dn] = idx
        for b in writes:
            for dn, (idx, _o) in b.writers.items():
                if not no_waw and deps.get(dn, 0) < idx:
                    deps[dn] = idx
            for dn, (idx, _o) in b.readers.items():
                if deps.get(dn, 0) < idx:
                    deps[dn] = idx
        mydom = _QUEUE_OF[eng] if dma_q else eng
        for dn, idx in deps.items():
            if dn == "pe" and eng == "pe":
                continue
            if dn == mydom and dn == eng:
                pass
            v = self._ensure_inc(dn, idx)
            w = self.waited[eng]
            if w.get(dn, 0) < v:
                op.waits.append((self.dom[dn].sem, v))
                w[dn] = v
        d = self.dom[mydom]
        d.ops.append(op)
        idx = len(d.ops)
        if mydom != eng:
            self.dom[eng].ops.append(op)
            self._ensure_inc(mydom, idx)
        for b in writes:
            b.writers = {mydom: (idx, op)}
            b.readers = {}
        for b in reads:
            b.readers[mydom] = (idx, op)
        return op

    def dma(self, eng, out, in_, out_buf=None, in_buf=None, no_waw=False, **kw):
        reads = [in_buf] if in_buf is not None else []
        writes = [out_buf] if out_buf is not None else []
        return self.emit(eng, lambda e: e.dma_start(out=out, in_=in_, **kw),
                         reads=reads, writes=writes, dma_q=eng, no_waw=no_waw)

    def replay(self):
        nc = self.nc
        with nc.Block() as block:
            for e in ENGINES:
                ops = self.dom[e].ops
                if not ops and e != "sp":
                    continue

                def run(engobj, ops=ops, e=e):
                    for op in ops:
                        for sem, v in op.waits:
                            engobj.wait_ge(sem, v)
                        ins = op.fn(engobj)
                        for sem, amt in op.incs:
                            ins.then_inc(sem, amt)
                    if e == "sp":
                        for q in set(_QUEUE_OF.values()):
                            dq = self.dom[q]
                            if dq.inc_cum:
                                engobj.wait_ge(dq.sem, dq.inc_cum[-1])
                getattr(block, _ENG_ATTR[e])(run)

    def close(self):
        self.stack.close()

    def finish(self):
        for q in set(_QUEUE_OF.values()):
            d = self.dom[q]
            if d.ops:
                self._ensure_inc(q, len(d.ops))
        self._round_dma_waits_to_bursts()
        self.replay()
        self.close()

    def _round_dma_waits_to_bursts(self):
        qnames = set(_QUEUE_OF.values())
        opq = {}
        for q in qnames:
            for i, op in enumerate(self.dom[q].ops):
                opq[id(op)] = (q, i + 1)
        burst_start = {}
        for e in ENGINES:
            boundary = True
            cur_start = {}
            for op in self.dom[e].ops:
                if op.waits:
                    boundary = True
                ent = opq.get(id(op))
                if ent is not None:
                    q, i = ent
                    if boundary or q not in cur_start:
                        cur_start[q] = i
                        boundary = False
                    burst_start[(q, i)] = cur_start[q]
        burst_end = {}
        for (q, i), s in burst_start.items():
            key = (q, s)
            if burst_end.get(key, 0) < i:
                burst_end[key] = i
        fix = {}
        for (q, i), s in burst_start.items():
            fix[(id(self.dom[q].sem), 16 * i)] = 16 * burst_end[(q, s)]
        for e in ENGINES:
            for op in self.dom[e].ops:
                op.waits = [(s, fix.get((id(s), v), v)) for (s, v) in op.waits]
        for (q, i), s in burst_start.items():
            if i == s and s > 1:
                self.dom[q].ops[i - 1].waits.append(
                    (self.dom[q].sem, 16 * (s - 1)))




def _make_io(nc, specs):
    io = {}
    for name, shape, dt, kind in specs:
        io[name] = Buf(nc.dram_tensor(name, shape, dt, kind=kind), name)
    return io


def _host_consts():
    c = {}
    c["ident_f32"] = np.eye(128, dtype=np.float32)
    c["ident_bf16"] = np.eye(128).astype(ml_dtypes.bfloat16)
    return c



_IO_P1 = [
    ("x", [BQ, T, D], F32, "ExternalInput"),
    ("Wq", [D, D], F32, "ExternalInput"),
    ("Wk", [D, D], F32, "ExternalInput"),
    ("Wv", [D, D], F32, "ExternalInput"),
    ("beta_w", [D, 1], F32, "ExternalInput"),
    ("beta_b", [1], F32, "ExternalInput"),
    ("norm1_w", [D], F32, "ExternalInput"),
    ("alpha_logit", [D], F32, "ExternalInput"),
    ("ident_f32", [128, 128], F32, "ExternalInput"),
    ("ident_bf16", [128, 128], BF16, "ExternalInput"),
    ("XN_FM", [BQ, NCHUNK, 128, T], F32, "ExternalOutput"),
    ("Q_FM", [BQ, NCHUNK, 128, T], BF16, "ExternalOutput"),
    ("K_FM", [BQ, NCHUNK, 128, T], BF16, "ExternalOutput"),
    ("K_TM", [BQ, NTILE, 128, D], BF16, "ExternalOutput"),
    ("V_TM", [BQ, NTILE, 128, D], F32, "ExternalOutput"),
    ("BETA", [BQ, T], F32, "ExternalOutput"),
    ("RSTD2", [BQ, T], F32, "ExternalOutput"),
    ("RSTD2i", [BQ, T], F32, "Internal"),
]

def emit_p1(p, io, nbatch=BQ, sections=('a','T','r','k','q','v','e')):
    nc = p.nc
    x_in = io["x"]
    wq_in = io["Wq"]
    wk_in = io["Wk"]
    wv_in = io["Wv"]
    bw_in = io["beta_w"]
    bb_in = io["beta_b"]
    n1_in = io["norm1_w"]
    al_in = io["alpha_logit"]
    id32_in = io["ident_f32"]
    idbf_in = io["ident_bf16"]
    XN_FM = io["XN_FM"]
    Q_FM = io["Q_FM"]
    K_FM = io["K_FM"]
    K_TM = io["K_TM"]
    V_TM = io["V_TM"]
    BETA = io["BETA"]
    RSTD2 = io["RSTD2"]
    RSTD2i = io["RSTD2i"]
    emit, dma = p.emit, p.dma

    idf = p.sbuf("idf", [128, 128], F32)
    idb = p.sbuf("idb", [128, 128], BF16)
    n1c = p.sbuf("n1c", [128, NCHUNK], F32)
    alc = p.sbuf("alc", [128, NCHUNK], F32)
    bbc = p.sbuf("bbc", [128, 1], F32)
    ones1 = p.sbuf("ones1", [1, 128], F32)
    eps5 = p.sbuf("eps5", [128, 1], F32)
    eps6 = p.sbuf("eps6", [128, 1], F32)

    dma("sp", idf.ap, id32_in.ap, out_buf=idf, in_buf=id32_in)
    dma("sp", idb.ap, idbf_in.ap, out_buf=idb, in_buf=idbf_in)
    vrows = p.sbuf("vrows", [8, 128], F32)
    bwc = p.sbuf("bwc", [128, NCHUNK], F32)
    ps_t = [p.psum(f"ps_t{i}", [128, 128], F32) for i in range(2)]
    ps_v8 = ps_t[0]
    for src_in, view, dst in (
        (n1_in, n1_in.ap_.rearrange("(c p) -> c p", p=128), n1c),
        (al_in, al_in.ap_.rearrange("(c p) -> c p", p=128), alc),
        (bw_in, bw_in.ap_.rearrange("(c p) one -> c (p one)", p=128), bwc),
    ):
        dma("sp", vrows.ap, view, out_buf=vrows, in_buf=src_in)
        emit("pe", lambda e: e.transpose(ps_v8[:, 0:8], vrows.ap, idf[0:8, 0:8]),
             reads=[vrows, idf], writes=[ps_v8])
        emit("dve", lambda e, dst=dst: e.tensor_copy(out=dst.ap, in_=ps_v8[:, 0:8]),
             reads=[ps_v8], writes=[dst])
    emit("dve", lambda e: e.memset(ones1.ap, 1.0), writes=[ones1])
    emit("dve", lambda e: e.memset(eps5.ap, 1e-5), writes=[eps5])
    emit("dve", lambda e: e.memset(eps6.ap, 1e-6), writes=[eps6])
    emit("act", lambda e: e.activation(out=alc.ap, in_=alc.ap, func=AF.Sigmoid),
         reads=[alc], writes=[alc])

    bb1 = p.sbuf("bb1", [1, 1], F32)
    ps_bb = ps_t[0]
    dma("sp", bb1.ap, bb_in.ap_[:, None], out_buf=bb1, in_buf=bb_in)
    emit("pe", lambda e: e.matmul(ps_bb[:, 0:1], ones1.ap, bb1.ap, start=True, stop=True),
         reads=[ones1, bb1], writes=[ps_bb])
    emit("dve", lambda e: e.tensor_copy(out=bbc.ap, in_=ps_bb[:, 0:1]),
         reads=[ps_bb], writes=[bbc])

    wq = p.sbuf("wq", [128, NCHUNK * D], BF16)
    wk = p.sbuf("wk", [128, NCHUNK * D], BF16)
    wv = p.sbuf("wv", [128, NCHUNK * D], BF16)
    wb = p.sbuf("wb", [128, NCHUNK], BF16)
    wstg = p.sbuf("wstg", [128, D], F32)
    for name, win, wout in (("q", wq_in, wq), ("k", wk_in, wk), ("v", wv_in, wv)):
        for c in range(NCHUNK):
            dma("sp", wstg.ap, win.ap_[c * 128:(c + 1) * 128, :], out_buf=wstg, in_buf=win)
            emit("dve", lambda e, c=c, wout=wout: e.tensor_scalar(
                out=wout[:, c * D:(c + 1) * D], in0=wstg.ap,
                scalar1=n1c[:, c:c + 1], scalar2=None, op0=OP.mult),
                reads=[wstg, n1c], writes=[wout], no_waw=True)
    emit("dve", lambda e: e.tensor_tensor(out=wb.ap, in0=bwc.ap, in1=n1c.ap,
         op=OP.mult), reads=[bwc, n1c], writes=[wb])

    xt = p.sbuf("xt", [128, D], F32)
    xsq = p.sbuf("xsq", [128, D], F32)
    s1 = p.sbuf("s1", [128, 1], F32)
    s2 = p.sbuf("s2", [128, 1], F32)
    mcol = p.sbuf("mcol", [128, 1], F32)
    vcol = p.sbuf("vcol", [128, 1], F32)
    r1col = p.sbuf("r1col", [128, 1], F32)
    xn_tm = p.sbuf("xn_tm", [128, D], F32)
    r2grp = p.sbuf("r2grp", [128, 4], F32)
    XNF = p.sbuf("XNF", [128, NCHUNK * 512], F32)
    XNFB = p.sbuf("XNFB", [128, NCHUNK * 512], BF16)
    ps_a = p.psum("ps_a", [128, 512], F32)
    ps_b = p.psum("ps_b", [128, 512], F32)
    ps_c = p.psum("ps_c", [128, 512], F32)
    ps_s = ps_t[1]
    kraw = p.sbuf("kraw", [128, D], F32)
    ksq = xsq
    kn2 = p.sbuf("kn2", [128, 1], F32)
    ksc = p.sbuf("ksc", [128, 1], F32)
    ktmb = p.sbuf("ktmb", [128, D], BF16)
    kfmb = p.sbuf("kfmb", [128, 128], BF16)
    ps_kt = p.psum("ps_kt", [128, 128], BF16)
    qfmb = p.sbuf("qfmb", [128, 512], BF16)
    rrow = p.sbuf("rrow", [1, 512], F32)
    rbc = p.sbuf("rbc", [128, 512], F32)
    VRAW = p.sbuf("VRAW", [128, NCHUNK * T], F32)
    a0bc = p.sbuf("a0bc", [128, T], F32)
    d1t = p.sbuf("d1t", [128, T], F32)
    vema = p.sbuf("vema", [128, T], F32)
    bcol = p.sbuf("bcol", [128, 1], F32)

    inv_d = 1.0 / D

    for b in range(nbatch):
        for g in range(4):
            for ti in range(4):
                tile = g * 4 + ti
                t0 = tile * 128
                dma("sp", xt.ap, x_in.ap_[b, t0:t0 + 128, :], out_buf=xt, in_buf=x_in)
                emit("act", lambda e: e.square(out=xsq.ap, in_=xt.ap),
                     reads=[xt], writes=[xsq])
                emit("dve", lambda e: e.reduce_sum(out=s1.ap, in_=xt.ap, axis=AX.X),
                     reads=[xt], writes=[s1])
                emit("dve", lambda e: e.reduce_sum(out=s2.ap, in_=xsq.ap, axis=AX.X),
                     reads=[xsq], writes=[s2])
                emit("dve", lambda e: e.tensor_scalar(out=mcol.ap, in0=s1.ap,
                     scalar1=inv_d, scalar2=None, op0=OP.mult),
                     reads=[s1], writes=[mcol])
                emit("dve", lambda e: e.tensor_scalar(out=vcol.ap, in0=s2.ap,
                     scalar1=inv_d, scalar2=None, op0=OP.mult),
                     reads=[s2], writes=[vcol])
                emit("dve", lambda e: e.tensor_tensor(out=r1col.ap, in0=mcol.ap,
                     in1=mcol.ap, op=OP.mult), reads=[mcol], writes=[r1col])
                emit("dve", lambda e: e.tensor_tensor(out=vcol.ap, in0=vcol.ap,
                     in1=r1col.ap, op=OP.subtract), reads=[vcol, r1col], writes=[vcol])
                emit("act", lambda e: e.activation(out=r1col.ap, in_=vcol.ap,
                     func=AF.Sqrt, bias=eps5.ap), reads=[vcol, eps5], writes=[r1col])
                emit("dve", lambda e: e.reciprocal(out=r1col.ap, in_=r1col.ap),
                     reads=[r1col], writes=[r1col])
                emit("dve", lambda e: e.tensor_scalar(out=xn_tm.ap, in0=xt.ap,
                     scalar1=mcol.ap, scalar2=r1col.ap, op0=OP.subtract, op1=OP.mult),
                     reads=[xt, mcol, r1col], writes=[xn_tm])
                emit("act", lambda e: e.square(out=xsq.ap, in_=xn_tm.ap),
                     reads=[xn_tm], writes=[xsq])
                emit("dve", lambda e: e.reduce_sum(out=s2.ap, in_=xsq.ap, axis=AX.X),
                     reads=[xsq], writes=[s2])
                emit("dve", lambda e, ti=ti: e.tensor_scalar(out=r2grp[:, ti:ti + 1],
                     in0=s2.ap, scalar1=inv_d, scalar2=None, op0=OP.mult),
                     reads=[s2], writes=[r2grp], no_waw=True)
                emit("act", lambda e, ti=ti: e.activation(out=r2grp[:, ti:ti + 1],
                     in_=r2grp[:, ti:ti + 1], func=AF.Sqrt, bias=eps6.ap),
                     reads=[r2grp, eps6], writes=[r2grp], no_waw=True)
                emit("dve", lambda e, ti=ti: e.reciprocal(out=r2grp[:, ti:ti + 1],
                     in_=r2grp[:, ti:ti + 1]), reads=[r2grp], writes=[r2grp], no_waw=True)
                if 'r' in sections:
                    dma("act", RSTD2.ap_[b, t0:t0 + 128, None], r2grp[:, ti:ti + 1],
                        out_buf=RSTD2, in_buf=r2grp)
                    dma("act", RSTD2i.ap_[b, t0:t0 + 128, None], r2grp[:, ti:ti + 1],
                        out_buf=RSTD2i, in_buf=r2grp)
                for c in (range(NCHUNK) if 'T' in sections else []):
                    pst = ps_t[c % 2]
                    emit("pe", lambda e, c=c, pst=pst: e.transpose(
                        pst.ap, xn_tm[:, c * 128:(c + 1) * 128], idf.ap),
                        reads=[xn_tm, idf], writes=[pst])
                    emit("dve", lambda e, c=c, ti=ti, pst=pst: e.tensor_copy(
                        out=XNF[:, c * 512 + ti * 128: c * 512 + (ti + 1) * 128],
                        in_=pst.ap), reads=[pst], writes=[XNF], no_waw=True)
                    emit("act", lambda e, c=c, ti=ti, pst=pst: e.copy(
                        out=XNFB[:, c * 512 + ti * 128: c * 512 + (ti + 1) * 128],
                        in_=pst.ap), reads=[pst], writes=[XNFB], no_waw=True)

            for c in (range(NCHUNK) if 'T' in sections else []):
                dma("act", XN_FM.ap_[b, c, :, g * 512:(g + 1) * 512],
                    XNF[:, c * 512:(c + 1) * 512], out_buf=XN_FM, in_buf=XNF)
            if 'r' in sections:
                dma("sp", rrow.ap, RSTD2i.ap_[b, None, g * 512:(g + 1) * 512],
                    out_buf=rrow, in_buf=RSTD2i)
                emit("pe", lambda e: e.matmul(ps_c.ap, ones1.ap, rrow.ap, start=True, stop=True),
                     reads=[ones1, rrow], writes=[ps_c])
                emit("dve", lambda e: e.tensor_copy(out=rbc.ap, in_=ps_c.ap),
                     reads=[ps_c], writes=[rbc])

            for ti in (range(4) if 'k' in sections else []):
                tile = g * 4 + ti
                for h in range(2):
                    ps = ps_a if h == 0 else ps_b
                    for c in range(NCHUNK):
                        emit("pe", lambda e, c=c, ti=ti, h=h, ps=ps: e.matmul(
                            ps.ap,
                            XNFB[:, c * 512 + ti * 128: c * 512 + (ti + 1) * 128],
                            wk[:, c * D + h * 512: c * D + (h + 1) * 512],
                            start=(c == 0), stop=(c == NCHUNK - 1)),
                            reads=[XNFB, wk], writes=[ps])
                    emit("dve", lambda e, h=h, ps=ps: e.tensor_copy(
                        out=kraw[:, h * 512:(h + 1) * 512], in_=ps.ap),
                        reads=[ps], writes=[kraw], no_waw=True)
                emit("act", lambda e: e.square(out=ksq.ap, in_=kraw.ap),
                     reads=[kraw], writes=[ksq])
                emit("dve", lambda e: e.reduce_sum(out=kn2.ap, in_=ksq.ap, axis=AX.X),
                     reads=[ksq], writes=[kn2])
                emit("act", lambda e: e.sqrt(out=kn2.ap, in_=kn2.ap),
                     reads=[kn2], writes=[kn2])
                emit("dve", lambda e, ti=ti: e.tensor_tensor(out=kn2.ap, in0=kn2.ap,
                     in1=r2grp[:, ti:ti + 1], op=OP.mult),
                     reads=[kn2, r2grp], writes=[kn2])
                emit("dve", lambda e: e.tensor_scalar(out=kn2.ap, in0=kn2.ap,
                     scalar1=1e-12, scalar2=None, op0=OP.max),
                     reads=[kn2], writes=[kn2])
                emit("dve", lambda e: e.reciprocal(out=kn2.ap, in_=kn2.ap),
                     reads=[kn2], writes=[kn2])
                emit("dve", lambda e, ti=ti: e.tensor_tensor(out=ksc.ap,
                     in0=r2grp[:, ti:ti + 1], in1=kn2.ap, op=OP.mult),
                     reads=[r2grp, kn2], writes=[ksc])
                emit("dve", lambda e: e.tensor_scalar(out=ktmb.ap, in0=kraw.ap,
                     scalar1=ksc.ap, scalar2=None, op0=OP.mult),
                     reads=[kraw, ksc], writes=[ktmb])
                dma("act", K_TM.ap_[b, tile, :, :], ktmb.ap, out_buf=K_TM, in_buf=ktmb)
                for c in range(NCHUNK):
                    emit("pe", lambda e, c=c: e.transpose(
                        ps_kt.ap, ktmb[:, c * 128:(c + 1) * 128], idb.ap),
                        reads=[ktmb, idb], writes=[ps_kt])
                    emit("dve", lambda e: e.tensor_copy(out=kfmb.ap, in_=ps_kt.ap),
                         reads=[ps_kt], writes=[kfmb])
                    dma("act", K_FM.ap_[b, c, :, tile * 128:(tile + 1) * 128],
                        kfmb.ap, out_buf=K_FM, in_buf=kfmb)
                for c in range(NCHUNK):
                    emit("pe", lambda e, c=c, ti=ti: e.matmul(
                        ps_s[:, 0:1],
                        XNFB[:, c * 512 + ti * 128: c * 512 + (ti + 1) * 128],
                        wb[:, c:c + 1], start=(c == 0), stop=(c == NCHUNK - 1)),
                        reads=[XNFB, wb], writes=[ps_s])
                emit("dve", lambda e, ti=ti: e.tensor_scalar(out=bcol.ap,
                     in0=ps_s[:, 0:1], scalar1=r2grp[:, ti:ti + 1], scalar2=None,
                     op0=OP.mult), reads=[ps_s, r2grp], writes=[bcol])
                emit("act", lambda e: e.activation(out=bcol.ap, in_=bcol.ap,
                     func=AF.Sigmoid, bias=bbc.ap), reads=[bcol, bbc], writes=[bcol])
                dma("act", BETA.ap_[b, tile * 128:(tile + 1) * 128, None], bcol.ap,
                    out_buf=BETA, in_buf=bcol)

            for c in (range(NCHUNK) if 'q' in sections else []):
                ps = ps_a if c % 2 == 0 else ps_b
                for k in range(NCHUNK):
                    emit("pe", lambda e, c=c, k=k, ps=ps: e.matmul(
                        ps.ap, wq[:, k * D + c * 128: k * D + (c + 1) * 128],
                        XNFB[:, k * 512:(k + 1) * 512],
                        start=(k == 0), stop=(k == NCHUNK - 1)),
                        reads=[wq, XNFB], writes=[ps])
                emit("dve", lambda e, ps=ps: e.tensor_copy(out=qfmb.ap, in_=ps.ap),
                     reads=[ps], writes=[qfmb])
                dma("act", Q_FM.ap_[b, c, :, g * 512:(g + 1) * 512], qfmb.ap,
                    out_buf=Q_FM, in_buf=qfmb)

            for c in (range(NCHUNK) if 'v' in sections else []):
                ps = ps_a if c % 2 == 0 else ps_b
                for k in range(NCHUNK):
                    emit("pe", lambda e, c=c, k=k, ps=ps: e.matmul(
                        ps.ap, wv[:, k * D + c * 128: k * D + (c + 1) * 128],
                        XNFB[:, k * 512:(k + 1) * 512],
                        start=(k == 0), stop=(k == NCHUNK - 1)),
                        reads=[wv, XNFB], writes=[ps])
                emit("dve", lambda e, c=c, g=g, ps=ps: e.tensor_tensor(
                    out=VRAW[:, c * T + g * 512: c * T + (g + 1) * 512],
                    in0=ps.ap, in1=rbc.ap, op=OP.mult),
                    reads=[ps, rbc], writes=[VRAW], no_waw=True)

        for c in (range(NCHUNK) if 'e' in sections else []):
            emit("dve", lambda e: e.memset(a0bc.ap, 1.0), writes=[a0bc])
            emit("dve", lambda e, c=c: e.tensor_scalar(out=a0bc.ap, in0=a0bc.ap,
                 scalar1=alc[:, c:c + 1], scalar2=None, op0=OP.subtract),
                 reads=[a0bc, alc], writes=[a0bc])
            emit("dve", lambda e, c=c: e.tensor_scalar(out=d1t.ap,
                 in0=VRAW[:, c * T:(c + 1) * T], scalar1=alc[:, c:c + 1],
                 scalar2=None, op0=OP.mult), reads=[VRAW, alc], writes=[d1t])
            emit("dve", lambda e: e.tensor_tensor_scan(out=vema.ap, data0=a0bc.ap,
                 data1=d1t.ap, initial=0.0, op0=OP.mult, op1=OP.add),
                 reads=[a0bc, d1t], writes=[vema])
            for tile in range(NTILE):
                pst = ps_t[tile % 2]
                emit("pe", lambda e, tile=tile, pst=pst: e.transpose(
                    pst.ap, vema[:, tile * 128:(tile + 1) * 128], idf.ap),
                    reads=[vema, idf], writes=[pst])
                emit("dve", lambda e, pst=pst: e.tensor_copy(out=kraw[:, 0:128],
                     in_=pst.ap), reads=[pst], writes=[kraw])
                dma("act", V_TM.ap_[b, tile, :, c * 128:(c + 1) * 128], kraw[:, 0:128],
                    out_buf=V_TM, in_buf=kraw)


def build_p1(nbatch=BQ, sections=('a','T','r','k','q','v','e')):
    nc = bass.Bass("TRN2", target_bir_lowering=False, debug=False)
    io = _make_io(nc, _IO_P1)
    p = Prog(nc)
    with p.phase():
        emit_p1(p, io, nbatch, sections)
    p.finish()
    return nc



_IO_PDELTA = [
    ("Q_FM", [BQ, NCHUNK, 128, T], BF16, "ExternalInput"),
    ("K_FM", [BQ, NCHUNK, 128, T], BF16, "ExternalInput"),
    ("K_TM", [BQ, NTILE, 128, D], BF16, "ExternalInput"),
    ("V_TM", [BQ, NTILE, 128, D], F32, "ExternalInput"),
    ("BETA", [BQ, T], F32, "ExternalInput"),
    ("RSTD2", [BQ, T], F32, "ExternalInput"),
    ("ident_bf16", [128, 128], BF16, "ExternalInput"),
    ("mask_su", [128, 128], F32, "ExternalInput"),
    ("mask_sl", [128, 128], F32, "ExternalInput"),
    ("mask_ui", [128, 128], F32, "ExternalInput"),
    ("O_FM", [BQ, NCHUNK, 128, T], BF16, "ExternalOutput"),
]

def emit_pdelta(p, io, nbatch=BQ, nchunks=NTILE):
    nc = p.nc
    QF = io["Q_FM"]
    KF = io["K_FM"]
    KT = io["K_TM"]
    VT = io["V_TM"]
    BETA = io["BETA"]
    RSTD2 = io["RSTD2"]
    idbf_in = io["ident_bf16"]
    msu_in = io["mask_su"]
    msl_in = io["mask_sl"]
    mui_in = io["mask_ui"]
    OF = io["O_FM"]
    emit, dma = p.emit, p.dma

    idb = p.sbuf("idb", [128, 128], BF16)
    msu = p.sbuf("msu", [128, 128], F32)
    msl = p.sbuf("msl", [128, 128], F32)
    mui = p.sbuf("mui", [128, 128], F32)
    ones1 = p.sbuf("ones1", [1, 128], F32)
    dma("sp", idb.ap, idbf_in.ap, out_buf=idb, in_buf=idbf_in)
    dma("sp", msu.ap, msu_in.ap, out_buf=msu, in_buf=msu_in)
    dma("sp", msl.ap, msl_in.ap, out_buf=msl, in_buf=msl_in)
    dma("sp", mui.ap, mui_in.ap, out_buf=mui, in_buf=mui_in)
    emit("dve", lambda e: e.memset(ones1.ap, 1.0), writes=[ones1])

    STB = [p.sbuf(f"STB{b}", [128, NCHUNK * D], F32) for b in range(nbatch)]
    for b in range(nbatch):
        emit("dve", lambda e, b=b: e.memset(STB[b].ap, 0.0), writes=[STB[b]])

    qf = p.sbuf("qf", [128, 1024], BF16)
    kf = p.sbuf("kf", [128, 1024], BF16)
    kt = p.sbuf("kt", [128, 1024], BF16)
    vt = p.sbuf("vt", [128, 1024], F32)
    bcol = p.sbuf("bcol", [128, 1], F32)
    nbcol = p.sbuf("nbcol", [128, 1], F32)
    brow = p.sbuf("brow", [1, 128], F32)
    rcol = p.sbuf("rcol", [128, 1], F32)
    nbc = p.sbuf("nbc", [128, 128], F32)
    Sbf = p.sbuf("Sbf", [128, 1024], BF16)
    Wt_ = p.sbuf("Wt_", [128, 128], BF16)
    W_ = p.sbuf("W_", [128, 128], BF16)
    Pt = p.sbuf("Pt", [128, 128], BF16)
    MqT = p.sbuf("MqT", [128, 128], BF16)
    Bb = p.sbuf("Bb", [128, 1024], BF16)
    Ub = p.sbuf("Ub", [128, 1024], BF16)
    otm = p.sbuf("otm", [128, 1024], BF16)
    ofm = p.sbuf("ofm", [128, 128], BF16)
    tmp = p.sbuf("tmpd", [128, 128], F32)

    psG = p.psum("psG", [128, 128], F32)
    psGq = p.psum("psGq", [128, 128], F32)
    psM = p.psum("psM", [128, 128], F32)
    psX = p.psum("psX", [128, 128], BF16)
    psW = [p.psum(f"psW{h}", [128, 512], F32) for h in range(2)]
    psO = [p.psum(f"psO{h}", [128, 512], F32) for h in range(2)]

    for ci in range(nchunks):
        for b in range(nbatch):
            S = STB[b]
            t0 = ci * 128
            for c in range(NCHUNK):
                dma("sp", qf[:, c * 128:(c + 1) * 128],
                    QF.ap_[b, c, :, t0:t0 + 128], out_buf=qf, in_buf=QF,
                    no_waw=(c > 0))
                dma("sp", kf[:, c * 128:(c + 1) * 128],
                    KF.ap_[b, c, :, t0:t0 + 128], out_buf=kf, in_buf=KF,
                    no_waw=(c > 0))
            dma("act", kt.ap, KT.ap_[b, ci, :, :], out_buf=kt, in_buf=KT)
            dma("act", vt.ap, VT.ap_[b, ci, :, :], out_buf=vt, in_buf=VT)
            dma("pool", bcol.ap, BETA.ap_[b, t0:t0 + 128, None], out_buf=bcol, in_buf=BETA)
            dma("pool", brow.ap, BETA.ap_[b, None, t0:t0 + 128], out_buf=brow, in_buf=BETA)
            dma("pool", rcol.ap, RSTD2.ap_[b, t0:t0 + 128, None], out_buf=rcol, in_buf=RSTD2)

            for c in range(NCHUNK):
                emit("pe", lambda e, c=c: e.matmul(psG.ap, kf[:, c*128:(c+1)*128],
                     kf[:, c*128:(c+1)*128], start=(c == 0), stop=(c == NCHUNK-1)),
                     reads=[kf], writes=[psG])
            for c in range(NCHUNK):
                emit("pe", lambda e, c=c: e.matmul(psGq.ap, kf[:, c*128:(c+1)*128],
                     qf[:, c*128:(c+1)*128], start=(c == 0), stop=(c == NCHUNK-1)),
                     reads=[kf, qf], writes=[psGq])
            emit("pe", lambda e: e.matmul(psM.ap, ones1.ap, brow.ap, start=True, stop=True),
                 reads=[ones1, brow], writes=[psM])
            emit("dve", lambda e: e.tensor_scalar(out=nbc.ap, in0=psM.ap,
                 scalar1=-1.0, scalar2=None, op0=OP.mult), reads=[psM], writes=[nbc])
            emit("dve", lambda e: e.tensor_scalar(out=nbcol.ap, in0=bcol.ap,
                 scalar1=-1.0, scalar2=None, op0=OP.mult), reads=[bcol], writes=[nbcol])

            emit("dve", lambda e: e.tensor_tensor(out=MqT.ap, in0=psGq.ap,
                 in1=mui.ap, op=OP.mult), reads=[psGq, mui], writes=[MqT])
            emit("dve", lambda e: e.tensor_tensor(out=tmp.ap, in0=psG.ap,
                 in1=msu.ap, op=OP.mult), reads=[psG, msu], writes=[tmp])
            emit("dve", lambda e: e.tensor_tensor(out=W_.ap, in0=tmp.ap,
                 in1=nbc.ap, op=OP.mult), reads=[tmp, nbc], writes=[W_])
            emit("dve", lambda e: e.tensor_tensor(out=tmp.ap, in0=psG.ap,
                 in1=msl.ap, op=OP.mult), reads=[psG, msl], writes=[tmp])
            emit("dve", lambda e: e.tensor_scalar(out=Wt_.ap, in0=tmp.ap,
                 scalar1=nbcol.ap, scalar2=None, op0=OP.mult),
                 reads=[tmp, nbcol], writes=[Wt_])
            emit("dve", lambda e: e.tensor_tensor(out=Pt.ap, in0=W_.ap,
                 in1=idb.ap, op=OP.add), reads=[W_, idb], writes=[Pt])

            for c in range(NCHUNK):
                emit("dve" if c % 2 == 0 else "act",
                     (lambda e, c=c, S=S: e.tensor_copy(out=Sbf.ap, in_=S[:, c*1024:(c+1)*1024]))
                     if c % 2 == 0 else
                     (lambda e, c=c, S=S: e.copy(out=Sbf.ap, in_=S[:, c*1024:(c+1)*1024])),
                     reads=[S], writes=[Sbf])
                for h in range(2):
                    emit("pe", lambda e, c=c, h=h: e.matmul(psW[h].ap,
                         kf[:, c*128:(c+1)*128], Sbf[:, h*512:(h+1)*512],
                         start=(c == 0), stop=(c == NCHUNK-1)),
                         reads=[kf, Sbf], writes=[psW[h]])
                    emit("pe", lambda e, c=c, h=h: e.matmul(psO[h].ap,
                         qf[:, c*128:(c+1)*128], Sbf[:, h*512:(h+1)*512],
                         start=(c == 0), stop=False, skip_group_check=True),
                         reads=[qf, Sbf], writes=[psO[h]])

            for k in range(1, 7):
                emit("pe", lambda e: e.matmul(psG.ap, Wt_.ap, W_.ap, start=True, stop=True),
                     reads=[Wt_, W_], writes=[psG])
                emit("pe", lambda e: e.matmul(psGq.ap, W_.ap, Wt_.ap, start=True, stop=True),
                     reads=[W_, Wt_], writes=[psGq])
                emit("dve", lambda e: e.tensor_copy(out=W_.ap, in_=psG.ap),
                     reads=[psG], writes=[W_])
                emit("dve", lambda e: e.tensor_copy(out=Wt_.ap, in_=psGq.ap),
                     reads=[psGq], writes=[Wt_])
                emit("pe", lambda e: e.matmul(psM.ap, idb.ap, Pt.ap, start=True, stop=False),
                     reads=[idb, Pt], writes=[psM])
                emit("pe", lambda e: e.matmul(psM.ap, Wt_.ap, Pt.ap, start=False, stop=True),
                     reads=[Wt_, Pt], writes=[psM])
                emit("dve", lambda e: e.tensor_copy(out=Pt.ap, in_=psM.ap),
                     reads=[psM], writes=[Pt])

            for h in range(2):
                emit("dve", lambda e, h=h: e.scalar_tensor_tensor(
                     out=vt[:, h*512:(h+1)*512], in0=psW[h].ap, scalar=-1.0,
                     in1=vt[:, h*512:(h+1)*512], op0=OP.mult, op1=OP.add),
                     reads=[psW[h], vt], writes=[vt], no_waw=True)
            emit("dve", lambda e: e.tensor_scalar(out=Bb.ap, in0=vt.ap,
                 scalar1=bcol.ap, scalar2=None, op0=OP.mult),
                 reads=[vt, bcol], writes=[Bb])

            for h in range(2):
                emit("pe", lambda e, h=h: e.matmul(psW[h].ap, Pt.ap,
                     Bb[:, h*512:(h+1)*512], start=True, stop=True),
                     reads=[Pt, Bb], writes=[psW[h]])
                emit("dve", lambda e, h=h: e.tensor_copy(
                     out=Ub[:, h*512:(h+1)*512], in_=psW[h].ap),
                     reads=[psW[h]], writes=[Ub], no_waw=True)

            for h in range(2):
                emit("pe", lambda e, h=h: e.matmul(psO[h].ap, MqT.ap,
                     Ub[:, h*512:(h+1)*512], start=False, stop=True,
                     skip_group_check=True),
                     reads=[MqT, Ub], writes=[psO[h]])
                emit("dve", lambda e, h=h: e.tensor_scalar(
                     out=otm[:, h*512:(h+1)*512], in0=psO[h].ap,
                     scalar1=rcol.ap, scalar2=None, op0=OP.mult),
                     reads=[psO[h], rcol], writes=[otm], no_waw=True)

            for c in range(NCHUNK):
                for h in range(2):
                    emit("pe", lambda e, c=c, h=h: e.matmul(psW[h].ap,
                         kt[:, c*128:(c+1)*128], Ub[:, h*512:(h+1)*512],
                         start=True, stop=True),
                         reads=[kt, Ub], writes=[psW[h]])
                    emit("dve", lambda e, c=c, h=h, S=S: e.tensor_tensor(
                         out=S[:, c*1024 + h*512: c*1024 + (h+1)*512],
                         in0=psW[h].ap,
                         in1=S[:, c*1024 + h*512: c*1024 + (h+1)*512],
                         op=OP.add),
                         reads=[psW[h], S], writes=[S], no_waw=True)

            for c in range(NCHUNK):
                emit("pe", lambda e, c=c: e.transpose(psX.ap,
                     otm[:, c*128:(c+1)*128], idb.ap),
                     reads=[otm, idb], writes=[psX])
                emit("dve", lambda e: e.tensor_copy(out=ofm.ap, in_=psX.ap),
                     reads=[psX], writes=[ofm])
                dma("act", OF.ap_[b, c, :, t0:t0 + 128], ofm.ap,
                    out_buf=OF, in_buf=ofm)


def build_pdelta(nbatch=BQ, nchunks=NTILE):
    nc = bass.Bass("TRN2", target_bir_lowering=False, debug=False)
    io = _make_io(nc, _IO_PDELTA)
    p = Prog(nc)
    with p.phase():
        emit_pdelta(p, io, nbatch, nchunks)
    p.finish()
    return nc



_IO_PC = [
    ("XN_FM", [BQ, NCHUNK, 128, T], F32, "ExternalInput"),
    ("O_FM", [BQ, NCHUNK, 128, T], BF16, "ExternalInput"),
    ("Wo", [D, D], F32, "ExternalInput"),
    ("norm2_w", [D], F32, "ExternalInput"),
    ("w_gate", [D, 2 * D], F32, "ExternalInput"),
    ("w_up", [D, 2 * D], F32, "ExternalInput"),
    ("w_down", [2 * D, D], F32, "ExternalInput"),
    ("ident_f32", [128, 128], F32, "ExternalInput"),
    ("S_FM", [BQ, NCHUNK, 128, T], F32, "ExternalOutput"),
]

def emit_pc(p, io, nbatch=BQ):
    nc = p.nc
    XN = io["XN_FM"]
    OF = io["O_FM"]
    WO = io["Wo"]
    N2 = io["norm2_w"]
    WG = io["w_gate"]
    WU = io["w_up"]
    WD = io["w_down"]
    idf_in = io["ident_f32"]
    SF = io["S_FM"]
    emit, dma = p.emit, p.dma

    idf = p.sbuf("idf", [128, 128], F32)
    dma("sp", idf.ap, idf_in.ap, out_buf=idf, in_buf=idf_in)
    onesc = p.sbuf("onesc", [128, 1], F32)
    ones1 = p.sbuf("ones1", [1, 128], F32)
    eps6 = p.sbuf("eps6", [128, 1], F32)
    emit("dve", lambda e: e.memset(onesc.ap, 1.0), writes=[onesc])
    emit("dve", lambda e: e.memset(ones1.ap, 1.0), writes=[ones1])
    emit("dve", lambda e: e.memset(eps6.ap, 1e-6), writes=[eps6])

    vrows = p.sbuf("vrows", [8, 128], F32)
    n2c = p.sbuf("n2c", [128, NCHUNK], F32)
    ps_t = [p.psum(f"ps_t{i}", [128, 128], F32) for i in range(2)]
    dma("sp", vrows.ap, N2.ap_.rearrange("(c p) -> c p", p=128), out_buf=vrows, in_buf=N2)
    emit("pe", lambda e: e.transpose(ps_t[0][:, 0:8], vrows.ap, idf[0:8, 0:8]),
         reads=[vrows, idf], writes=[ps_t[0]])
    emit("dve", lambda e: e.tensor_copy(out=n2c.ap, in_=ps_t[0][:, 0:8]),
         reads=[ps_t[0]], writes=[n2c])

    wstg = p.sbuf("wstg", [128, D], F32)
    wo = p.sbuf("wo", [128, NCHUNK * D], BF16)
    wg = p.sbuf("wg", [128, NCHUNK * 2 * D], BF16)
    wu = p.sbuf("wu", [128, NCHUNK * 2 * D], BF16)
    wd = p.sbuf("wd", [128, 2 * NCHUNK * D], BF16)
    for c in range(NCHUNK):
        dma("sp", wstg.ap, WO.ap_[c * 128:(c + 1) * 128, :], out_buf=wstg, in_buf=WO)
        emit("dve", lambda e, c=c: e.tensor_copy(out=wo[:, c * D:(c + 1) * D],
             in_=wstg.ap), reads=[wstg], writes=[wo], no_waw=True)
    for win, wout in ((WG, wg), (WU, wu)):
        for c in range(NCHUNK):
            for hh in range(2):
                dma("sp", wstg.ap, win.ap_[c * 128:(c + 1) * 128, hh * D:(hh + 1) * D],
                    out_buf=wstg, in_buf=win)
                emit("dve", lambda e, c=c, hh=hh, wout=wout: e.tensor_scalar(
                     out=wout[:, c * 2 * D + hh * D: c * 2 * D + (hh + 1) * D],
                     in0=wstg.ap, scalar1=n2c[:, c:c + 1], scalar2=None, op0=OP.mult),
                     reads=[wstg, n2c], writes=[wout], no_waw=True)
    for c in range(2 * NCHUNK):
        dma("sp", wstg.ap, WD.ap_[c * 128:(c + 1) * 128, :], out_buf=wstg, in_buf=WD)
        emit("dve", lambda e, c=c: e.tensor_copy(out=wd[:, c * D:(c + 1) * D],
             in_=wstg.ap), reads=[wstg], writes=[wd], no_waw=True)

    xnf = p.sbuf("xnf", [128, NCHUNK * 512], F32)
    of = p.sbuf("of", [128, NCHUNK * 512], BF16)
    yf = p.sbuf("yf", [128, NCHUNK * 512], F32)
    ysq = p.sbuf("ysq", [128, 512], F32)
    h2b = p.sbuf("h2b", [128, NCHUNK * 512], BF16)
    gatef = p.sbuf("gatef", [128, 4 * 512], F32)
    mlpin = p.sbuf("mlpin", [128, 16 * 512], BF16)
    sfo = p.sbuf("sfo", [128, 512], F32)
    rrow = p.sbuf("rrow", [1, 512], F32)
    rbc = p.sbuf("rbc", [128, 512], F32)
    psA = p.psum("psA", [128, 512], F32)
    psB = p.psum("psB", [128, 512], F32)
    psR = p.psum("psR", [1, 512], F32)
    psC = p.psum("psC", [128, 512], F32)

    for b in range(nbatch):
        for g in range(4):
            sl = slice(g * 512, (g + 1) * 512)
            for c in range(NCHUNK):
                dma("sp", xnf[:, c * 512:(c + 1) * 512], XN.ap_[b, c, :, sl],
                    out_buf=xnf, in_buf=XN, no_waw=(c > 0))
                dma("act", of[:, c * 512:(c + 1) * 512], OF.ap_[b, c, :, sl],
                    out_buf=of, in_buf=OF, no_waw=(c > 0))
            for c in range(NCHUNK):
                ps = psA if c % 2 == 0 else psB
                for k in range(NCHUNK):
                    emit("pe", lambda e, c=c, k=k, ps=ps: e.matmul(ps.ap,
                         wo[:, k * D + c * 128: k * D + (c + 1) * 128],
                         of[:, k * 512:(k + 1) * 512],
                         start=(k == 0), stop=(k == NCHUNK - 1)),
                         reads=[wo, of], writes=[ps])
                emit("dve", lambda e, c=c, ps=ps: e.tensor_tensor(
                     out=yf[:, c * 512:(c + 1) * 512], in0=ps.ap,
                     in1=xnf[:, c * 512:(c + 1) * 512], op=OP.add),
                     reads=[ps, xnf], writes=[yf], no_waw=True)
            for c in range(NCHUNK):
                emit("act", lambda e, c=c: e.square(out=ysq.ap,
                     in_=yf[:, c * 512:(c + 1) * 512]), reads=[yf], writes=[ysq])
                emit("pe", lambda e, c=c: e.matmul(psR.ap, onesc.ap, ysq.ap,
                     start=(c == 0), stop=(c == NCHUNK - 1)),
                     reads=[onesc, ysq], writes=[psR])
            emit("dve", lambda e: e.tensor_scalar(out=rrow.ap, in0=psR.ap,
                 scalar1=1.0 / D, scalar2=None, op0=OP.mult),
                 reads=[psR], writes=[rrow])
            emit("act", lambda e: e.activation(out=rrow.ap, in_=rrow.ap,
                 func=AF.Sqrt, bias=eps6[0:1, :]), reads=[rrow, eps6], writes=[rrow])
            emit("dve", lambda e: e.reciprocal(out=rrow.ap, in_=rrow.ap),
                 reads=[rrow], writes=[rrow])
            emit("pe", lambda e: e.matmul(psC.ap, ones1.ap, rrow.ap,
                 start=True, stop=True), reads=[ones1, rrow], writes=[psC])
            emit("dve", lambda e: e.tensor_copy(out=rbc.ap, in_=psC.ap),
                 reads=[psC], writes=[rbc])
            for c in range(NCHUNK):
                emit("dve", lambda e, c=c: e.tensor_tensor(
                     out=h2b[:, c * 512:(c + 1) * 512],
                     in0=yf[:, c * 512:(c + 1) * 512], in1=rbc.ap, op=OP.mult),
                     reads=[yf, rbc], writes=[h2b], no_waw=True)
            for fh in range(4):
                for fc in range(4):
                    f = fh * 4 + fc
                    ps = psA if fc % 2 == 0 else psB
                    for k in range(NCHUNK):
                        emit("pe", lambda e, f=f, k=k, ps=ps: e.matmul(ps.ap,
                             wg[:, k * 2 * D + f * 128: k * 2 * D + (f + 1) * 128],
                             h2b[:, k * 512:(k + 1) * 512],
                             start=(k == 0), stop=(k == NCHUNK - 1)),
                             reads=[wg, h2b], writes=[ps])
                    emit("act", lambda e, fc=fc, ps=ps: e.activation(
                         out=gatef[:, fc * 512:(fc + 1) * 512], in_=ps.ap,
                         func=AF.Silu), reads=[ps], writes=[gatef], no_waw=True)
                    for k in range(NCHUNK):
                        emit("pe", lambda e, f=f, k=k, ps=ps: e.matmul(ps.ap,
                             wu[:, k * 2 * D + f * 128: k * 2 * D + (f + 1) * 128],
                             h2b[:, k * 512:(k + 1) * 512],
                             start=(k == 0), stop=(k == NCHUNK - 1)),
                             reads=[wu, h2b], writes=[ps])
                    emit("dve", lambda e, f=f, fc=fc, ps=ps: e.tensor_tensor(
                         out=mlpin[:, f * 512:(f + 1) * 512], in0=ps.ap,
                         in1=gatef[:, fc * 512:(fc + 1) * 512], op=OP.mult),
                         reads=[ps, gatef], writes=[mlpin], no_waw=True)
            for c in range(NCHUNK):
                ps = psA if c % 2 == 0 else psB
                for k in range(16):
                    emit("pe", lambda e, c=c, k=k, ps=ps: e.matmul(ps.ap,
                         wd[:, k * D + c * 128: k * D + (c + 1) * 128],
                         mlpin[:, k * 512:(k + 1) * 512],
                         start=(k == 0), stop=(k == 15)),
                         reads=[wd, mlpin], writes=[ps])
                emit("dve", lambda e, c=c, ps=ps: e.tensor_tensor(
                     out=sfo.ap, in0=ps.ap,
                     in1=yf[:, c * 512:(c + 1) * 512], op=OP.add),
                     reads=[ps, yf], writes=[sfo])
                dma("act", SF.ap_[b, c, :, sl], sfo.ap, out_buf=SF, in_buf=sfo)


def build_pc(nbatch=BQ):
    nc = bass.Bass("TRN2", target_bir_lowering=False, debug=False)
    io = _make_io(nc, _IO_PC)
    p = Prog(nc)
    with p.phase():
        emit_pc(p, io, nbatch)
    p.finish()
    return nc



_P0_SPECS = [
    ("w1", [14 * 1024, 2048]), ("w2", [14 * 2048, 1024]),
    ("gug", [8 * 1024, 1024]), ("guu", [8 * 1024, 1024]),
    ("dn", [8 * 1024, 1024]),
]


def emit_p0(p, io):
    nc = p.nc
    stg = [p.sbuf(f"stg{i}", [128, 1024], F32) for i in range(3)]
    stb = [p.sbuf(f"stb{i}", [128, 1024], BF16) for i in range(3)]
    i = 0
    for name, (R, C) in _P0_SPECS:
        fin = io[name]
        fout = io[name + "b"]
        for r0 in range(0, R, 128):
            for c0 in range(0, C, 1024):
                j = i % 3
                i += 1
                p.dma("pool", stg[j].ap, fin.ap_[r0:r0 + 128, c0:c0 + 1024],
                      out_buf=stg[j], in_buf=fin)
                if j % 2 == 0:
                    p.emit("dve", lambda e, j=j: e.tensor_copy(
                        out=stb[j].ap, in_=stg[j].ap),
                        reads=[stg[j]], writes=[stb[j]])
                else:
                    p.emit("act", lambda e, j=j: e.copy(
                        out=stb[j].ap, in_=stg[j].ap),
                        reads=[stg[j]], writes=[stb[j]])
                p.dma("pool", fout.ap_[r0:r0 + 128, c0:c0 + 1024], stb[j].ap,
                      out_buf=fout, in_buf=stb[j])


def _p0_io(nc, kind_in="ExternalInput", kind_out="ExternalOutput"):
    io = {}
    for name, (R, C) in _P0_SPECS:
        io[name] = Buf(nc.dram_tensor(name, [R, C], F32, kind=kind_in), name)
        io[name + "b"] = Buf(nc.dram_tensor(name + "b", [R, C], BF16, kind=kind_out), name + "b")
    return io


def build_p0():
    nc = bass.Bass("TRN2", target_bir_lowering=False, debug=False)
    io = _p0_io(nc)
    p = Prog(nc)
    with p.phase():
        emit_p0(p, io)
    p.finish()
    return nc


def _old_build_p0_unused():
    nc = bass.Bass("TRN2", target_bir_lowering=False, debug=False)
    specs = [
        ("w1", [14 * 1024, 2048]), ("w2", [14 * 2048, 1024]),
        ("gug", [8 * 1024, 1024]), ("guu", [8 * 1024, 1024]),
        ("dn", [8 * 1024, 1024]),
    ]
    p = Prog(nc)
    stg = [p.sbuf(f"stg{i}", [128, 2048], F32) for i in range(3)]
    stb = [p.sbuf(f"stb{i}", [128, 2048], BF16) for i in range(3)]
    qs = ["sp", "act", "pool"]
    i = 0
    for name, (R, C) in specs:
        fin = Buf(nc.dram_tensor(name, [R, C], F32, kind="ExternalInput"), name)
        fout = Buf(nc.dram_tensor(name + "b", [R, C], BF16, kind="ExternalOutput"), name + "b")
        for r0 in range(0, R, 128):
            j = i % 3
            i += 1
            p.dma(qs[j], stg[j].ap[:, 0:C] if C < 2048 else stg[j].ap,
                  fin.ap_[r0:r0 + 128, :], out_buf=stg[j], in_buf=fin)
            eng = "dve" if j % 2 == 0 else "act"
            if eng == "dve":
                p.emit(eng, lambda e, j=j, C=C: e.tensor_copy(
                    out=stb[j][:, 0:C], in_=stg[j][:, 0:C]),
                    reads=[stg[j]], writes=[stb[j]])
            else:
                p.emit(eng, lambda e, j=j, C=C: e.copy(
                    out=stb[j][:, 0:C], in_=stg[j][:, 0:C]),
                    reads=[stg[j]], writes=[stb[j]])
            p.dma(qs[(j + 1) % 3], fout.ap_[r0:r0 + 128, :], stb[j][:, 0:C],
                  out_buf=fout, in_buf=stb[j])
    p.finish()
    return nc



_IO_PD = [
    ("XN_FM", [BQ, NCHUNK, 128, T], F32, "ExternalInput"),
    ("S_FM", [BQ, NCHUNK, 128, T], F32, "ExternalInput"),
    ("x", [BQ, T, D], F32, "ExternalInput"),
    ("w1b", [14 * 1024, 2048], BF16, "ExternalInput"),
    ("w2b", [14 * 2048, 1024], BF16, "ExternalInput"),
    ("b1", [14, 2048], F32, "ExternalInput"),
    ("b2", [14, 1024], F32, "ExternalInput"),
    ("router_w", [D, 14], F32, "ExternalInput"),
    ("ident_f32", [128, 128], F32, "ExternalInput"),
    ("iota14", [128, 14], F32, "ExternalInput"),
    ("X1", [BQ, T, D], F32, "ExternalOutput"),
]

def emit_pd(p, io, nbatch=BQ, nexp=14, nsg=None):
    nc = p.nc
    E = 14
    XN = io["XN_FM"]
    SFM = io["S_FM"]
    X0 = io["x"]
    W1B = io["w1b"]
    W2B = io["w2b"]
    B1 = io["b1"]
    B2 = io["b2"]
    RW = io["router_w"]
    idf_in = io["ident_f32"]
    iota_in = io["iota14"]
    X1 = io["X1"]
    emit, dma = p.emit, p.dma

    idf = p.sbuf("idf", [128, 128], F32)
    iot = p.sbuf("iot", [128, E], F32)
    dma("sp", idf.ap, idf_in.ap, out_buf=idf, in_buf=idf_in)
    dma("sp", iot.ap, iota_in.ap, out_buf=iot, in_buf=iota_in)

    rwc = p.sbuf("rwc", [128, NCHUNK * E], F32)
    for c in range(NCHUNK):
        dma("sp", rwc[:, c * E:(c + 1) * E],
            RW.ap_.rearrange("(c p) e -> c p e", p=128)[c], out_buf=rwc,
            in_buf=RW, no_waw=(c > 0))
    xrt = [p.sbuf(f"xrt{i}", [128, 1024], F32) for i in range(2)]
    xstg = xrt[0]
    b1rows = p.sbuf("b1rows", [E, 1024], F32)
    b2rows = p.sbuf("b2rows", [E, 1024], BF16)
    b1c = p.sbuf("b1c", [128, 16 * E], F32)
    ps_t = p.psum("ps_t", [128, 128], F32)
    dma("sp", xstg[0:E, :], B2.ap, out_buf=xstg, in_buf=B2)
    emit("dve", lambda e: e.tensor_copy(out=b2rows.ap, in_=xstg[0:E, :]),
         reads=[xstg], writes=[b2rows])
    for bh in range(2):
        dma("sp", b1rows.ap, B1.ap_[:, bh * 1024:(bh + 1) * 1024],
            out_buf=b1rows, in_buf=B1)
        for fc8 in range(8):
            fc = bh * 8 + fc8
            emit("pe", lambda e, fc8=fc8: e.transpose(ps_t[:, 0:E],
                 b1rows[:, fc8 * 128:(fc8 + 1) * 128], idf[0:E, 0:E]),
                 reads=[b1rows, idf], writes=[ps_t])
            emit("dve", lambda e, fc=fc: e.tensor_copy(
                 out=b1c[:, fc * E:(fc + 1) * E], in_=ps_t[:, 0:E]),
                 reads=[ps_t], writes=[b1c], no_waw=True)

    xnb = p.sbuf("xnb", [128, NCHUNK * 1024], BF16)
    wbuf = [p.sbuf(f"wbuf{i}", [128, 16384], BF16) for i in range(3)]
    hid = p.sbuf("hid", [128, 16 * 1024], BF16)
    acc = p.sbuf("acc", [128, 8 * 1024], F32)
    lg = p.sbuf("lg", [128, E], F32)
    mx8 = p.sbuf("mx8", [128, 8], F32)
    idxu = p.sbuf("idxu", [128, 8], U32)
    idxf = p.sbuf("idxf", [128, 1], F32)
    nmx = p.sbuf("nmx", [128, 1], F32)
    ssum = p.sbuf("ssum", [128, 1], F32)
    meq = p.sbuf("meq", [128, E], F32)
    CC = p.sbuf("CC", [128, 8 * E], F32)
    CTT = p.sbuf("CTT", [E, 8 * 128], BF16)
    gt = p.sbuf("gt", [128, 1024], F32)
    stm = p.sbuf("stm", [128, 1024], F32)
    x0t = p.sbuf("x0t", [128, 1024], F32)
    sstg = p.sbuf("sstg", [128, 128], F32)
    rot6 = [p.psum(f"rot{i}", [128, 512], F32) for i in range(6)]
    rotctr = [0]

    def nextps():
        ps = rot6[rotctr[0] % 6]
        rotctr[0] += 1
        return ps
    psR = p.psum("psR", [128, E], F32)

    NSG = 2 * nbatch if nsg is None else nsg
    for sg in range(NSG):
        b, hh = sg // 2, sg % 2
        tok0 = hh * 1024
        for ti in range(8):
            t0 = tok0 + ti * 128
            xr = xrt[ti % 2]
            dma("sp", xr.ap_.rearrange("p (c t) -> p c t", t=128),
                XN.ap_[b, :, :, t0:t0 + 128].transpose([1, 0, 2]),
                out_buf=xr, in_buf=XN)
            for c in range(NCHUNK):
                emit("pe", lambda e, c=c, xr=xr: e.matmul(psR.ap,
                     xr[:, c * 128:(c + 1) * 128],
                     rwc[:, c * E:(c + 1) * E], start=(c == 0),
                     stop=(c == NCHUNK - 1)), reads=[xr, rwc], writes=[psR])
            emit("dve", lambda e: e.tensor_copy(out=lg.ap, in_=psR.ap),
                 reads=[psR], writes=[lg])
            emit("dve", lambda e: e.max(out=mx8.ap, in_=lg.ap),
                 reads=[lg], writes=[mx8])
            emit("dve", lambda e: e.max_index(out=idxu.ap, in_max=mx8.ap,
                 in_values=lg.ap), reads=[mx8, lg], writes=[idxu])
            emit("dve", lambda e: e.tensor_copy(out=idxf.ap, in_=idxu[:, 0:1]),
                 reads=[idxu], writes=[idxf])
            emit("dve", lambda e: e.reduce_max(out=nmx.ap, in_=lg.ap, axis=AX.X,
                 negate=True), reads=[lg], writes=[nmx])
            emit("act", lambda e: e.activation(out=meq.ap, in_=lg.ap, func=AF.Exp,
                 bias=nmx.ap), reads=[lg, nmx], writes=[meq])
            emit("dve", lambda e: e.reduce_sum(out=ssum.ap, in_=meq.ap, axis=AX.X),
                 reads=[meq], writes=[ssum])
            emit("dve", lambda e: e.reciprocal(out=ssum.ap, in_=ssum.ap),
                 reads=[ssum], writes=[ssum])
            emit("dve", lambda e: e.tensor_scalar(out=meq.ap, in0=iot.ap,
                 scalar1=idxf.ap, scalar2=None, op0=OP.is_equal),
                 reads=[iot, idxf], writes=[meq])
            emit("dve", lambda e, ti=ti: e.tensor_scalar(
                 out=CC[:, ti * E:(ti + 1) * E], in0=meq.ap, scalar1=ssum.ap,
                 scalar2=None, op0=OP.mult), reads=[meq, ssum], writes=[CC],
                 no_waw=True)
            emit("pe", lambda e, ti=ti: e.transpose(ps_t[0:E, 0:128],
                 CC[:, ti * E:(ti + 1) * E], idf.ap),
                 reads=[CC, idf], writes=[ps_t])
            emit("dve", lambda e, ti=ti: e.tensor_copy(
                 out=CTT[:, ti * 128:(ti + 1) * 128], in_=ps_t[0:E, 0:128]),
                 reads=[ps_t], writes=[CTT], no_waw=True)
        for c in range(NCHUNK):
            xr = xrt[c % 2]
            dma("act", xr.ap, XN.ap_[b, c, :, tok0:tok0 + 1024],
                out_buf=xr, in_buf=XN)
            emit("dve", lambda e, c=c, xr=xr: e.tensor_copy(
                 out=xnb[:, c * 1024:(c + 1) * 1024], in_=xr.ap),
                 reads=[xr], writes=[xnb], no_waw=(c > 0))
        emit("dve", lambda e: e.memset(acc.ap, 0.0), writes=[acc])
        for ei in range(nexp):
            w1t = wbuf[(2 * ei) % 3]
            w2t = wbuf[(2 * ei + 1) % 3]
            for c in range(NCHUNK):
                dma("sp", w1t[:, c * 2048:(c + 1) * 2048],
                    W1B.ap_[ei * 1024 + c * 128: ei * 1024 + (c + 1) * 128, :],
                    out_buf=w1t, in_buf=W1B, no_waw=(c > 0))
            for fc in range(16):
                dma("pool", w2t[:, fc * 1024:(fc + 1) * 1024],
                    W2B.ap_[ei * 2048 + fc * 128: ei * 2048 + (fc + 1) * 128, :],
                    out_buf=w2t, in_buf=W2B, no_waw=(fc > 0))
            for fc in range(16):
                for th in range(2):
                    ps = nextps()
                    for k in range(NCHUNK):
                        emit("pe", lambda e, fc=fc, th=th, k=k, ps=ps, w1t=w1t:
                             e.matmul(ps.ap,
                             w1t[:, k * 2048 + fc * 128: k * 2048 + (fc + 1) * 128],
                             xnb[:, k * 1024 + th * 512: k * 1024 + (th + 1) * 512],
                             start=(k == 0), stop=(k == NCHUNK - 1)),
                             reads=[w1t, xnb], writes=[ps])
                    emit("act", lambda e, fc=fc, th=th, ps=ps, ei=ei:
                         e.activation(out=hid[:, fc * 1024 + th * 512:
                                              fc * 1024 + (th + 1) * 512],
                         in_=ps.ap, func=AF.Gelu,
                         bias=b1c[:, fc * E + ei: fc * E + ei + 1]),
                         reads=[ps, b1c], writes=[hid], no_waw=True)
            for ti in range(8):
                for h in range(2):
                    ps = nextps()
                    for fc in range(16):
                        emit("pe", lambda e, ti=ti, h=h, fc=fc, ps=ps, w2t=w2t:
                             e.matmul(ps.ap,
                             hid[:, fc * 1024 + ti * 128: fc * 1024 + (ti + 1) * 128],
                             w2t[:, fc * 1024 + h * 512: fc * 1024 + (h + 1) * 512],
                             start=(fc == 0), stop=(fc == 15)),
                             reads=[hid, w2t], writes=[ps])
                    emit("dve", lambda e, ti=ti, h=h, ps=ps, ei=ei:
                         e.scalar_tensor_tensor(
                         out=acc[:, ti * 1024 + h * 512: ti * 1024 + (h + 1) * 512],
                         in0=ps.ap, scalar=CC[:, ti * E + ei: ti * E + ei + 1],
                         in1=acc[:, ti * 1024 + h * 512: ti * 1024 + (h + 1) * 512],
                         op0=OP.mult, op1=OP.add),
                         reads=[ps, CC, acc], writes=[acc], no_waw=True)
        for ti in range(8):
            t0 = tok0 + ti * 128
            for h in range(2):
                ps = nextps()
                emit("pe", lambda e, ti=ti, h=h, ps=ps: e.matmul(ps.ap,
                     CTT[:, ti * 128:(ti + 1) * 128],
                     b2rows[:, h * 512:(h + 1) * 512], start=True, stop=True),
                     reads=[CTT, b2rows], writes=[ps])
                emit("dve", lambda e, ti=ti, h=h, ps=ps: e.tensor_tensor(
                     out=acc[:, ti * 1024 + h * 512: ti * 1024 + (h + 1) * 512],
                     in0=ps.ap,
                     in1=acc[:, ti * 1024 + h * 512: ti * 1024 + (h + 1) * 512],
                     op=OP.add), reads=[ps, acc], writes=[acc], no_waw=True)
            emit("act", lambda e, ti=ti: e.activation(out=gt.ap,
                 in_=acc[:, ti * 1024:(ti + 1) * 1024], func=AF.Silu),
                 reads=[acc], writes=[gt])
            for c in range(NCHUNK):
                dma("pool", sstg.ap, SFM.ap_[b, c, :, t0:t0 + 128],
                    out_buf=sstg, in_buf=SFM)
                emit("pe", lambda e, c=c: e.transpose(ps_t.ap, sstg.ap, idf.ap),
                     reads=[sstg, idf], writes=[ps_t])
                emit("dve", lambda e, c=c: e.tensor_copy(
                     out=stm[:, c * 128:(c + 1) * 128], in_=ps_t.ap),
                     reads=[ps_t], writes=[stm], no_waw=True)
            dma("sp", x0t.ap, X0.ap_[b, t0:t0 + 128, :], out_buf=x0t, in_buf=X0)
            emit("dve", lambda e: e.tensor_tensor(out=gt.ap, in0=gt.ap,
                 in1=stm.ap, op=OP.mult), reads=[gt, stm], writes=[gt])
            emit("dve", lambda e: e.tensor_tensor(out=gt.ap, in0=gt.ap,
                 in1=x0t.ap, op=OP.add), reads=[gt, x0t], writes=[gt])
            dma("act", X1.ap_[b, t0:t0 + 128, :], gt.ap, out_buf=X1, in_buf=gt)


def build_pd(nbatch=BQ, nexp=14, nsg=None):
    nc = bass.Bass("TRN2", target_bir_lowering=False, debug=False)
    io = _make_io(nc, _IO_PD)
    p = Prog(nc)
    with p.phase():
        emit_pd(p, io, nbatch, nexp, nsg)
    p.finish()
    return nc



_IO_PE = [
    ("X1", [BQ, T, D], F32, "ExternalInput"),
    ("gugb", [8 * 1024, 1024], BF16, "ExternalInput"),
    ("guub", [8 * 1024, 1024], BF16, "ExternalInput"),
    ("dnb", [8 * 1024, 1024], BF16, "ExternalInput"),
    ("gb_g", [8, 1024], F32, "ExternalInput"),
    ("gb_u", [8, 1024], F32, "ExternalInput"),
    ("db", [8, 1024], F32, "ExternalInput"),
    ("router_indices", [T, 2], I32, "ExternalInput"),
    ("routing_weights", [T, 8], F32, "ExternalInput"),
    ("flagcol", [128, 1], F32, "ExternalInput"),
    ("ident_f32", [128, 128], F32, "ExternalInput"),
    ("ident_bf16", [128, 128], BF16, "ExternalInput"),
    ("mask_li", [128, 128], F32, "ExternalInput"),
    ("OUT", [BQ, T, D], F32, "ExternalOutput"),
]

def emit_pe(p, io, nquarter=4, nexp=8, nheads=16):
    nc = p.nc
    E = 8
    X1 = io["X1"]
    GG = io["gugb"]
    GU = io["guub"]
    DN = io["dnb"]
    GBG = io["gb_g"]
    GBU = io["gb_u"]
    DB = io["db"]
    RI = io["router_indices"]
    RWT = io["routing_weights"]
    FLAG = io["flagcol"]
    idf_in = io["ident_f32"]
    idb_in = io["ident_bf16"]
    mli_in = io["mask_li"]
    OUT = io["OUT"]
    emit, dma = p.emit, p.dma

    idf = p.sbuf("idf", [128, 128], F32)
    idb = p.sbuf("idb", [128, 128], BF16)
    mli = p.sbuf("mli", [128, 128], F32)
    flagc = p.sbuf("flagc", [128, 1], F32)
    eighth = p.sbuf("eighth", [128, 1], F32)
    c1702 = p.sbuf("c1702", [128, 1], F32)
    eps5 = p.sbuf("eps5", [128, 1], F32)
    dma("sp", idf.ap, idf_in.ap, out_buf=idf, in_buf=idf_in)
    dma("sp", idb.ap, idb_in.ap, out_buf=idb, in_buf=idb_in)
    dma("sp", mli.ap, mli_in.ap, out_buf=mli, in_buf=mli_in)
    dma("sp", flagc.ap, FLAG.ap, out_buf=flagc, in_buf=FLAG)
    emit("dve", lambda e: e.memset(eighth.ap, 0.125), writes=[eighth])
    emit("dve", lambda e: e.memset(c1702.ap, 1.702), writes=[c1702])
    emit("dve", lambda e: e.memset(eps5.ap, 1e-5), writes=[eps5])

    xstg = p.sbuf("xstg", [128, D], F32)
    for b in range(1, BQ):
        for tile in range(NTILE):
            t0 = tile * 128
            dma("pool", xstg.ap, X1.ap_[b, t0:t0 + 128, :], out_buf=xstg, in_buf=X1)
            dma("pool", OUT.ap_[b, t0:t0 + 128, :], xstg.ap, out_buf=OUT, in_buf=xstg)

    gbgc = p.sbuf("gbgc", [128, 8 * E], F32)
    gbuc = p.sbuf("gbuc", [128, 8 * E], F32)
    dbr = p.sbuf("dbr", [E, 1024], BF16)
    ps_t = p.psum("ps_t", [128, 128], F32)
    for src_b, dst in ((GBG, gbgc), (GBU, gbuc)):
        dma("sp", xstg[0:E, :], src_b.ap, out_buf=xstg, in_buf=src_b)
        for fc in range(8):
            emit("pe", lambda e, fc=fc: e.transpose(ps_t[:, 0:E],
                 xstg[0:E, fc * 128:(fc + 1) * 128], idf[0:E, 0:E]),
                 reads=[xstg, idf], writes=[ps_t])
            emit("dve", lambda e, fc=fc, dst=dst: e.tensor_copy(
                 out=dst[:, fc * E:(fc + 1) * E], in_=ps_t[:, 0:E]),
                 reads=[ps_t], writes=[dst], no_waw=True)
    dma("sp", xstg[0:E, :], DB.ap, out_buf=xstg, in_buf=DB)
    emit("dve", lambda e: e.tensor_copy(out=dbr.ap, in_=xstg[0:E, :]),
         reads=[xstg], writes=[dbr])

    ri2 = p.sbuf("ri2", [128, 2], I32)
    rif = p.sbuf("rif", [128, 2], F32)
    rwt = p.sbuf("rwt", [128, E], F32)
    eqt = p.sbuf("eqt", [128, E], F32)
    C1 = p.sbuf("C1", [128, NTILE * E], F32)
    C1T = p.sbuf("C1T", [E, NTILE * 128], BF16)
    iota8 = p.sbuf("iota8", [128, E], F32)
    for e8 in range(E):
        emit("dve", lambda e, e8=e8: e.memset(iota8[:, e8:e8 + 1], float(e8)),
             writes=[iota8], no_waw=(e8 > 0))
    for tile in range(NTILE):
        t0 = tile * 128
        dma("sp", ri2.ap, RI.ap_[t0:t0 + 128, :], out_buf=ri2, in_buf=RI)
        dma("sp", rwt.ap, RWT.ap_[t0:t0 + 128, :], out_buf=rwt, in_buf=RWT)
        emit("dve", lambda e: e.tensor_copy(out=rif.ap, in_=ri2.ap),
             reads=[ri2], writes=[rif])
        emit("dve", lambda e: e.tensor_scalar(out=eqt.ap, in0=iota8.ap,
             scalar1=rif[:, 0:1], scalar2=None, op0=OP.is_equal),
             reads=[iota8, rif], writes=[eqt])
        emit("dve", lambda e: e.tensor_scalar(out=rwt.ap, in0=iota8.ap,
             scalar1=rif[:, 1:2], scalar2=rwt.ap, op0=OP.is_equal, op1=OP.bypass),
             reads=[iota8, rif], writes=[]) if False else None
        emit("dve", lambda e: e.tensor_scalar(out=rif.ap, in0=iota8[:, 0:2],
             scalar1=rif[:, 1:2], scalar2=None, op0=OP.is_equal),
             reads=[iota8, rif], writes=[rif]) if False else None
        emit("dve", lambda e, tile=tile: e.tensor_scalar(
             out=C1[:, tile * E:(tile + 1) * E],
             in0=iota8.ap, scalar1=rif[:, 1:2], scalar2=None, op0=OP.is_equal),
             reads=[iota8, rif], writes=[C1], no_waw=True)
        emit("dve", lambda e, tile=tile: e.tensor_tensor(
             out=eqt.ap, in0=eqt.ap, in1=C1[:, tile * E:(tile + 1) * E],
             op=OP.add), reads=[eqt, C1], writes=[eqt])
        emit("dve", lambda e, tile=tile: e.tensor_tensor(
             out=C1[:, tile * E:(tile + 1) * E], in0=eqt.ap, in1=rwt.ap,
             op=OP.mult), reads=[eqt, rwt], writes=[C1], no_waw=True)
        emit("pe", lambda e, tile=tile: e.transpose(ps_t[0:E, 0:128],
             C1[:, tile * E:(tile + 1) * E], idf.ap),
             reads=[C1, idf], writes=[ps_t])
        emit("dve", lambda e, tile=tile: e.tensor_copy(
             out=C1T[:, tile * 128:(tile + 1) * 128], in_=ps_t[0:E, 0:128]),
             reads=[ps_t], writes=[C1T], no_waw=True)

    XN1B = p.sbuf("XN1B", [128, NTILE * D], BF16)
    XN1FM = p.sbuf("XN1FM", [128, NCHUNK * T], BF16)
    xsq = p.sbuf("xsq", [128, D], F32)
    s1c = p.sbuf("s1c", [128, 1], F32)
    s2c = p.sbuf("s2c", [128, 1], F32)
    mcol = p.sbuf("mcol", [128, 1], F32)
    vcol = p.sbuf("vcol", [128, 1], F32)
    rcol = p.sbuf("rcol", [128, 1], F32)
    xnb_t = p.sbuf("xnb_t", [128, D], BF16)
    ps_tb = p.psum("ps_tb", [128, 128], BF16)
    tstg = p.sbuf("tstg", [128, 128], BF16)
    inv_d = 1.0 / D
    for tile in range(NTILE):
        t0 = tile * 128
        dma("sp", xstg.ap, X1.ap_[0, t0:t0 + 128, :], out_buf=xstg, in_buf=X1)
        emit("act", lambda e: e.square(out=xsq.ap, in_=xstg.ap),
             reads=[xstg], writes=[xsq])
        emit("dve", lambda e: e.reduce_sum(out=s1c.ap, in_=xstg.ap, axis=AX.X),
             reads=[xstg], writes=[s1c])
        emit("dve", lambda e: e.reduce_sum(out=s2c.ap, in_=xsq.ap, axis=AX.X),
             reads=[xsq], writes=[s2c])
        emit("dve", lambda e: e.tensor_scalar(out=mcol.ap, in0=s1c.ap,
             scalar1=inv_d, scalar2=None, op0=OP.mult), reads=[s1c], writes=[mcol])
        emit("dve", lambda e: e.tensor_scalar(out=vcol.ap, in0=s2c.ap,
             scalar1=inv_d, scalar2=None, op0=OP.mult), reads=[s2c], writes=[vcol])
        emit("dve", lambda e: e.tensor_tensor(out=rcol.ap, in0=mcol.ap,
             in1=mcol.ap, op=OP.mult), reads=[mcol], writes=[rcol])
        emit("dve", lambda e: e.tensor_tensor(out=vcol.ap, in0=vcol.ap,
             in1=rcol.ap, op=OP.subtract), reads=[vcol, rcol], writes=[vcol])
        emit("act", lambda e: e.activation(out=rcol.ap, in_=vcol.ap,
             func=AF.Sqrt, bias=eps5.ap), reads=[vcol, eps5], writes=[rcol])
        emit("dve", lambda e: e.reciprocal(out=rcol.ap, in_=rcol.ap),
             reads=[rcol], writes=[rcol])
        emit("dve", lambda e, tile=tile: e.tensor_scalar(
             out=XN1B[:, tile * D:(tile + 1) * D], in0=xstg.ap,
             scalar1=mcol.ap, scalar2=rcol.ap, op0=OP.subtract, op1=OP.mult),
             reads=[xstg, mcol, rcol], writes=[XN1B], no_waw=True)
        emit("dve", lambda e, tile=tile: e.tensor_copy(out=xnb_t.ap,
             in_=XN1B[:, tile * D:(tile + 1) * D]), reads=[XN1B], writes=[xnb_t])
        for c in range(NCHUNK):
            emit("pe", lambda e, c=c: e.transpose(ps_tb.ap,
                 xnb_t[:, c * 128:(c + 1) * 128], idb.ap),
                 reads=[xnb_t, idb], writes=[ps_tb])
            emit("dve", lambda e: e.tensor_copy(out=tstg.ap, in_=ps_tb.ap),
                 reads=[ps_tb], writes=[tstg])
            emit("act", lambda e, c=c, tile=tile: e.copy(
                 out=XN1FM[:, c * T + tile * 128: c * T + (tile + 1) * 128],
                 in_=tstg.ap), reads=[tstg], writes=[XN1FM], no_waw=True)

    sc = p.sbuf("sc", [128, T], F32)
    pexp = p.sbuf("pexp", [128, T], F32)
    rmx = p.sbuf("rmx", [128, 1], F32)
    nm8 = p.sbuf("nm8", [128, 1], F32)
    rsm = p.sbuf("rsm", [128, 1], F32)
    pb = p.sbuf("pb", [128, 128], BF16)
    pT = p.sbuf("pT", [128, 128], BF16)
    OATT = p.sbuf("OATT", [128, 4 * D], F32)
    acc1 = p.sbuf("acc1", [128, 4 * D], F32)
    wL = [p.sbuf(f"wL{i}", [128, 8 * 1024], BF16) for i in range(3)]
    ggt = p.sbuf("ggt", [128, 512], F32)
    sgt = p.sbuf("sgt", [128, 512], F32)
    gut = p.sbuf("gut", [128, 512], F32)
    inter = p.sbuf("inter", [128, 8 * 512], BF16)
    gt1 = p.sbuf("gt1", [128, D], F32)
    x1t = p.sbuf("x1t", [128, D], F32)
    psS = p.psum("psS", [128, 512], F32)
    psS2 = p.psum("psS2", [128, 512], F32)
    psV = p.psum("psV", [128, 64], F32)
    psM1 = p.psum("psM1", [128, 512], F32)
    psM2 = p.psum("psM2", [128, 512], F32)

    for q4 in range(nquarter):
        for h in range(nheads):
            ch, pb0 = h // 2, (h % 2) * 64
            for qi in range(4):
                qb = q4 * 4 + qi
                nk = (qb + 1) * 128
                nkg = (nk + 511) // 512
                for kg in range(nkg):
                    wdt = min(512, nk - kg * 512)
                    ps = psS if kg % 2 == 0 else psS2
                    emit("pe", lambda e, ch=ch, pb0=pb0, qb=qb, kg=kg, wdt=wdt, ps=ps:
                         e.matmul(ps[:, 0:wdt],
                         XN1FM[pb0:pb0 + 64, ch * T + qb * 128: ch * T + (qb + 1) * 128],
                         XN1FM[pb0:pb0 + 64, ch * T + kg * 512: ch * T + kg * 512 + wdt],
                         start=True, stop=True),
                         reads=[XN1FM], writes=[ps])
                    emit("dve", lambda e, kg=kg, wdt=wdt, ps=ps: e.tensor_copy(
                         out=sc[:, kg * 512: kg * 512 + wdt], in_=ps[:, 0:wdt]),
                         reads=[ps], writes=[sc], no_waw=(kg > 0))
                emit("dve", lambda e, nk=nk: e.reduce_max(out=rmx.ap,
                     in_=sc[:, 0:nk], axis=AX.X), reads=[sc], writes=[rmx])
                emit("dve", lambda e: e.tensor_scalar(out=nm8.ap, in0=rmx.ap,
                     scalar1=-0.125, scalar2=None, op0=OP.mult),
                     reads=[rmx], writes=[nm8])
                emit("act", lambda e, nk=nk: e.activation(out=pexp[:, 0:nk],
                     in_=sc[:, 0:nk], func=AF.Exp, bias=nm8.ap, scale=eighth.ap),
                     reads=[sc, nm8, eighth], writes=[pexp])
                emit("dve", lambda e, qb=qb: e.tensor_tensor(
                     out=pexp[:, qb * 128:(qb + 1) * 128],
                     in0=pexp[:, qb * 128:(qb + 1) * 128], in1=mli.ap, op=OP.mult),
                     reads=[pexp, mli], writes=[pexp], no_waw=True)
                emit("dve", lambda e, nk=nk: e.reduce_sum(out=rsm.ap,
                     in_=pexp[:, 0:nk], axis=AX.X), reads=[pexp], writes=[rsm])
                emit("dve", lambda e: e.reciprocal(out=rsm.ap, in_=rsm.ap),
                     reads=[rsm], writes=[rsm])
                for kb in range(qb + 1):
                    emit("act", lambda e, kb=kb: e.copy(out=pb.ap,
                         in_=pexp[:, kb * 128:(kb + 1) * 128]),
                         reads=[pexp], writes=[pb])
                    emit("pe", lambda e: e.transpose(ps_tb.ap, pb.ap, idb.ap),
                         reads=[pb, idb], writes=[ps_tb])
                    emit("dve", lambda e: e.tensor_copy(out=pT.ap, in_=ps_tb.ap),
                         reads=[ps_tb], writes=[pT])
                    emit("pe", lambda e, kb=kb, h=h, qb=qb: e.matmul(psV.ap,
                         pT.ap, XN1B[:, kb * D + h * 64: kb * D + (h + 1) * 64],
                         start=(kb == 0), stop=(kb == qb)),
                         reads=[pT, XN1B], writes=[psV])
                emit("dve", lambda e, qi=qi, h=h: e.tensor_scalar(
                     out=OATT[:, qi * D + h * 64: qi * D + (h + 1) * 64],
                     in0=psV.ap, scalar1=rsm.ap, scalar2=None, op0=OP.mult),
                     reads=[psV, rsm], writes=[OATT], no_waw=True)

        emit("dve", lambda e: e.memset(acc1.ap, 0.0), writes=[acc1])
        for ei in range(nexp):
            bgg = wL[(3 * ei) % 3]
            bgu = wL[(3 * ei + 1) % 3]
            bdn = wL[(3 * ei + 2) % 3]
            for c in range(NCHUNK):
                dma("sp", bgg[:, c * 1024:(c + 1) * 1024],
                    GG.ap_[ei * 1024 + c * 128: ei * 1024 + (c + 1) * 128, :],
                    out_buf=bgg, in_buf=GG, no_waw=(c > 0))
                dma("act", bgu[:, c * 1024:(c + 1) * 1024],
                    GU.ap_[ei * 1024 + c * 128: ei * 1024 + (c + 1) * 128, :],
                    out_buf=bgu, in_buf=GU, no_waw=(c > 0))
                dma("pool", bdn[:, c * 1024:(c + 1) * 1024],
                    DN.ap_[ei * 1024 + c * 128: ei * 1024 + (c + 1) * 128, :],
                    out_buf=bdn, in_buf=DN, no_waw=(c > 0))
            for fc in range(8):
                for k in range(NCHUNK):
                    emit("pe", lambda e, fc=fc, k=k, q4=q4, bgg=bgg: e.matmul(
                         psM1.ap, bgg[:, k * 1024 + fc * 128: k * 1024 + (fc + 1) * 128],
                         XN1FM[:, k * T + q4 * 512: k * T + (q4 + 1) * 512],
                         start=(k == 0), stop=(k == NCHUNK - 1)),
                         reads=[bgg, XN1FM], writes=[psM1])
                emit("dve", lambda e, fc=fc, ei=ei: e.tensor_scalar(out=ggt.ap,
                     in0=psM1.ap, scalar1=gbgc[:, fc * E + ei: fc * E + ei + 1],
                     scalar2=None, op0=OP.add),
                     reads=[psM1, gbgc], writes=[ggt])
                emit("dve", lambda e: e.tensor_scalar(out=ggt.ap, in0=ggt.ap,
                     scalar1=7.0, scalar2=None, op0=OP.min),
                     reads=[ggt], writes=[ggt])
                emit("act", lambda e: e.activation(out=sgt.ap, in_=ggt.ap,
                     func=AF.Sigmoid, scale=c1702.ap), reads=[ggt, c1702], writes=[sgt])
                emit("dve", lambda e: e.tensor_tensor(out=ggt.ap, in0=ggt.ap,
                     in1=sgt.ap, op=OP.mult), reads=[ggt, sgt], writes=[ggt])
                for k in range(NCHUNK):
                    emit("pe", lambda e, fc=fc, k=k, q4=q4, bgu=bgu: e.matmul(
                         psM2.ap, bgu[:, k * 1024 + fc * 128: k * 1024 + (fc + 1) * 128],
                         XN1FM[:, k * T + q4 * 512: k * T + (q4 + 1) * 512],
                         start=(k == 0), stop=(k == NCHUNK - 1)),
                         reads=[bgu, XN1FM], writes=[psM2])
                emit("dve", lambda e, fc=fc, ei=ei: e.tensor_scalar(out=gut.ap,
                     in0=psM2.ap, scalar1=gbuc[:, fc * E + ei: fc * E + ei + 1],
                     scalar2=None, op0=OP.add),
                     reads=[psM2, gbuc], writes=[gut])
                emit("dve", lambda e: e.tensor_scalar(out=gut.ap, in0=gut.ap,
                     scalar1=-7.0, scalar2=7.0, op0=OP.max, op1=OP.min),
                     reads=[gut], writes=[gut])
                emit("dve", lambda e: e.tensor_scalar(out=gut.ap, in0=gut.ap,
                     scalar1=1.0, scalar2=None, op0=OP.add),
                     reads=[gut], writes=[gut])
                emit("dve", lambda e, fc=fc: e.tensor_tensor(
                     out=inter[:, fc * 512:(fc + 1) * 512], in0=gut.ap,
                     in1=ggt.ap, op=OP.mult), reads=[gut, ggt], writes=[inter],
                     no_waw=True)
            for ti in range(4):
                for hh2 in range(2):
                    ps = psM1 if hh2 == 0 else psM2
                    for fc in range(8):
                        emit("pe", lambda e, ti=ti, hh2=hh2, fc=fc, ps=ps, bdn=bdn:
                             e.matmul(ps.ap,
                             inter[:, fc * 512 + ti * 128: fc * 512 + (ti + 1) * 128],
                             bdn[:, fc * 1024 + hh2 * 512: fc * 1024 + (hh2 + 1) * 512],
                             start=(fc == 0), stop=(fc == 7)),
                             reads=[inter, bdn], writes=[ps])
                    emit("dve", lambda e, ti=ti, hh2=hh2, ps=ps, ei=ei, q4=q4:
                         e.scalar_tensor_tensor(
                         out=acc1[:, ti * D + hh2 * 512: ti * D + (hh2 + 1) * 512],
                         in0=ps.ap,
                         scalar=C1[:, (q4 * 4 + ti) * E + ei: (q4 * 4 + ti) * E + ei + 1],
                         in1=acc1[:, ti * D + hh2 * 512: ti * D + (hh2 + 1) * 512],
                         op0=OP.mult, op1=OP.add),
                         reads=[ps, C1, acc1], writes=[acc1], no_waw=True)
        for ti in range(4):
            tile = q4 * 4 + ti
            t0 = tile * 128
            for hh2 in range(2):
                ps = psM1 if hh2 == 0 else psM2
                emit("pe", lambda e, tile=tile, hh2=hh2, ps=ps: e.matmul(ps.ap,
                     C1T[:, tile * 128:(tile + 1) * 128],
                     dbr[:, hh2 * 512:(hh2 + 1) * 512], start=True, stop=True),
                     reads=[C1T, dbr], writes=[ps])
                emit("dve", lambda e, ti=ti, hh2=hh2, ps=ps: e.tensor_tensor(
                     out=acc1[:, ti * D + hh2 * 512: ti * D + (hh2 + 1) * 512],
                     in0=ps.ap,
                     in1=acc1[:, ti * D + hh2 * 512: ti * D + (hh2 + 1) * 512],
                     op=OP.add), reads=[ps, acc1], writes=[acc1], no_waw=True)
            emit("act", lambda e, ti=ti: e.activation(out=gt1.ap,
                 in_=acc1[:, ti * D:(ti + 1) * D], func=AF.Silu),
                 reads=[acc1], writes=[gt1])
            dma("sp", x1t.ap, X1.ap_[0, t0:t0 + 128, :], out_buf=x1t, in_buf=X1)
            emit("dve", lambda e, ti=ti: e.tensor_tensor(out=gt1.ap, in0=gt1.ap,
                 in1=OATT[:, ti * D:(ti + 1) * D], op=OP.mult),
                 reads=[gt1, OATT], writes=[gt1])
            emit("dve", lambda e: e.tensor_scalar(out=gt1.ap, in0=gt1.ap,
                 scalar1=flagc.ap, scalar2=None, op0=OP.mult),
                 reads=[gt1, flagc], writes=[gt1])
            emit("dve", lambda e: e.tensor_tensor(out=gt1.ap, in0=gt1.ap,
                 in1=x1t.ap, op=OP.add), reads=[gt1, x1t], writes=[gt1])
            dma("act", OUT.ap_[0, t0:t0 + 128, :], gt1.ap, out_buf=OUT, in_buf=gt1)


def build_pe(nquarter=4, nexp=8, nheads=16):
    nc = bass.Bass("TRN2", target_bir_lowering=False, debug=False)
    io = _make_io(nc, _IO_PE)
    p = Prog(nc)
    with p.phase():
        emit_pe(p, io, nquarter, nexp, nheads)
    p.finish()
    return nc



_FUSED_EXTERNAL_IN = {
    "x", "Wq", "Wk", "Wv", "beta_w", "beta_b", "norm1_w", "alpha_logit",
    "ident_f32", "ident_bf16", "mask_su", "mask_sl", "mask_ui", "mask_li",
    "Wo", "norm2_w", "w_gate", "w_up", "w_down", "b1", "b2", "router_w",
    "iota14", "gb_g", "gb_u", "db", "router_indices", "routing_weights",
    "flagcol", "w1", "w2", "gug", "guu", "dn",
}


def build_fused():
    nc = bass.Bass("TRN2", target_bir_lowering=False, debug=False)
    specs = {}
    for name, (R, C) in _P0_SPECS:
        specs[name] = ([R, C], F32)
        specs[name + "b"] = ([R, C], BF16)
    for spec in (_IO_P1, _IO_PDELTA, _IO_PC, _IO_PD, _IO_PE):
        for name, shape, dt, kind in spec:
            specs.setdefault(name, (shape, dt))
    io = {}
    for name, (shape, dt) in specs.items():
        if name == "OUT":
            kind = "ExternalOutput"
        elif name in _FUSED_EXTERNAL_IN:
            kind = "ExternalInput"
        else:
            kind = "Internal"
        io[name] = Buf(nc.dram_tensor(name, shape, dt, kind=kind), name)
    p = Prog(nc)
    with p.phase():
        emit_p0(p, io)
        emit_p1(p, io)
    with p.phase():
        emit_pdelta(p, io)
    with p.phase():
        emit_pc(p, io)
    with p.phase():
        emit_pd(p, io)
    with p.phase():
        emit_pe(p, io)
    p.finish()
    return nc


def kernel(**inputs):
    inp = {k: np.asarray(v) for k, v in inputs.items()}
    consts = _host_consts()
    asc = np.ascontiguousarray
    cores = list(range(_NCORES))
    nc = build_fused()
    maps = []
    for c in cores:
        maps.append({
            "x": inp["L0_attn_x"][4 * c:4 * c + 4],
            "Wq": inp["L0_attn_Wq"], "Wk": inp["L0_attn_Wk"], "Wv": inp["L0_attn_Wv"],
            "beta_w": inp["L0_attn_beta_w"], "beta_b": inp["L0_attn_beta_b"],
            "norm1_w": inp["L0_attn_norm1_w"], "alpha_logit": inp["L0_attn_alpha_logit"],
            "Wo": inp["L0_attn_Wo"], "norm2_w": inp["L0_attn_norm2_w"],
            "w_gate": inp["L0_attn_w_gate"], "w_up": inp["L0_attn_w_up"],
            "w_down": inp["L0_attn_w_down"],
            "w1": inp["L0_ffn_w1"].reshape(14 * 1024, 2048),
            "w2": inp["L0_ffn_w2"].reshape(14 * 2048, 1024),
            "b1": inp["L0_ffn_b1"], "b2": inp["L0_ffn_b2"],
            "router_w": inp["L0_ffn_router_w"],
            "gug": asc(inp["L1_ffn_gate_up_proj"][:, :, 0::2]).reshape(8 * 1024, 1024),
            "guu": asc(inp["L1_ffn_gate_up_proj"][:, :, 1::2]).reshape(8 * 1024, 1024),
            "dn": inp["L1_ffn_down_proj"].reshape(8 * 1024, 1024),
            "gb_g": asc(inp["L1_ffn_gate_up_proj_bias"][:, 0::2]),
            "gb_u": asc(inp["L1_ffn_gate_up_proj_bias"][:, 1::2]),
            "db": inp["L1_ffn_down_proj_bias"],
            "router_indices": inp["L1_ffn_router_indices"],
            "routing_weights": inp["L1_ffn_routing_weights"],
            "flagcol": np.full((128, 1), 1.0 if c == 0 else 0.0, np.float32),
            "ident_f32": consts["ident_f32"], "ident_bf16": consts["ident_bf16"],
            "mask_su": np.triu(np.ones((128, 128), np.float32), 1),
            "mask_sl": np.tril(np.ones((128, 128), np.float32), -1),
            "mask_ui": np.triu(np.ones((128, 128), np.float32), 0),
            "mask_li": np.tril(np.ones((128, 128), np.float32)),
            "iota14": np.broadcast_to(np.arange(14, dtype=np.float32), (128, 14)).copy(),
        })
    res = run_bass_kernel_spmd(nc, maps, core_ids=cores)
    return np.concatenate([res.results[c]["OUT"] for c in cores], axis=0)




_NCORES = 8


def kernel_unfused(**inputs):
    inp = {k: np.asarray(v) for k, v in inputs.items()}
    consts = _host_consts()
    msu = np.triu(np.ones((128, 128), np.float32), 1)
    msl = np.tril(np.ones((128, 128), np.float32), -1)
    mui = np.triu(np.ones((128, 128), np.float32), 0)
    mli = np.tril(np.ones((128, 128), np.float32))
    iota14 = np.broadcast_to(np.arange(14, dtype=np.float32), (128, 14)).copy()
    cores = list(range(_NCORES))
    asc = np.ascontiguousarray

    nc = build_p0()
    m0 = {
        "w1": inp["L0_ffn_w1"].reshape(14 * 1024, 2048),
        "w2": inp["L0_ffn_w2"].reshape(14 * 2048, 1024),
        "gug": asc(inp["L1_ffn_gate_up_proj"][:, :, 0::2]).reshape(8 * 1024, 1024),
        "guu": asc(inp["L1_ffn_gate_up_proj"][:, :, 1::2]).reshape(8 * 1024, 1024),
        "dn": inp["L1_ffn_down_proj"].reshape(8 * 1024, 1024),
    }
    r0 = run_bass_kernel_spmd(nc, [m0] * _NCORES, core_ids=cores).results

    nc = build_p1()
    maps = []
    for c in cores:
        maps.append({
            "x": inp["L0_attn_x"][4 * c:4 * c + 4],
            "Wq": inp["L0_attn_Wq"], "Wk": inp["L0_attn_Wk"], "Wv": inp["L0_attn_Wv"],
            "beta_w": inp["L0_attn_beta_w"], "beta_b": inp["L0_attn_beta_b"],
            "norm1_w": inp["L0_attn_norm1_w"], "alpha_logit": inp["L0_attn_alpha_logit"],
            "ident_f32": consts["ident_f32"], "ident_bf16": consts["ident_bf16"],
        })
    r1 = run_bass_kernel_spmd(nc, maps, core_ids=cores).results

    nc = build_pdelta()
    maps = []
    for c in cores:
        maps.append({
            "Q_FM": r1[c]["Q_FM"], "K_FM": r1[c]["K_FM"], "K_TM": r1[c]["K_TM"],
            "V_TM": r1[c]["V_TM"], "BETA": r1[c]["BETA"], "RSTD2": r1[c]["RSTD2"],
            "ident_bf16": consts["ident_bf16"],
            "mask_su": msu, "mask_sl": msl, "mask_ui": mui,
        })
    r2 = run_bass_kernel_spmd(nc, maps, core_ids=cores).results

    nc = build_pc()
    maps = []
    for c in cores:
        maps.append({
            "XN_FM": r1[c]["XN_FM"], "O_FM": r2[c]["O_FM"],
            "Wo": inp["L0_attn_Wo"], "norm2_w": inp["L0_attn_norm2_w"],
            "w_gate": inp["L0_attn_w_gate"], "w_up": inp["L0_attn_w_up"],
            "w_down": inp["L0_attn_w_down"], "ident_f32": consts["ident_f32"],
        })
    r3 = run_bass_kernel_spmd(nc, maps, core_ids=cores).results

    nc = build_pd()
    maps = []
    for c in cores:
        maps.append({
            "XN_FM": r1[c]["XN_FM"], "S_FM": r3[c]["S_FM"],
            "x": inp["L0_attn_x"][4 * c:4 * c + 4],
            "w1b": r0[c]["w1b"], "w2b": r0[c]["w2b"],
            "b1": inp["L0_ffn_b1"], "b2": inp["L0_ffn_b2"],
            "router_w": inp["L0_ffn_router_w"],
            "ident_f32": consts["ident_f32"], "iota14": iota14,
        })
    r4 = run_bass_kernel_spmd(nc, maps, core_ids=cores).results

    nc = build_pe()
    maps = []
    for c in cores:
        flag = np.full((128, 1), 1.0 if c == 0 else 0.0, np.float32)
        maps.append({
            "X1": r4[c]["X1"],
            "gugb": r0[c]["gugb"], "guub": r0[c]["guub"], "dnb": r0[c]["dnb"],
            "gb_g": asc(inp["L1_ffn_gate_up_proj_bias"][:, 0::2]),
            "gb_u": asc(inp["L1_ffn_gate_up_proj_bias"][:, 1::2]),
            "db": inp["L1_ffn_down_proj_bias"],
            "router_indices": inp["L1_ffn_router_indices"],
            "routing_weights": inp["L1_ffn_routing_weights"],
            "flagcol": flag,
            "ident_f32": consts["ident_f32"], "ident_bf16": consts["ident_bf16"],
            "mask_li": mli,
        })
    r5 = run_bass_kernel_spmd(nc, maps, core_ids=cores).results

    return np.concatenate([r5[c]["OUT"] for c in cores], axis=0)


# version 06
# speedup vs baseline: 1.1481x; 1.0724x over previous
import contextlib
import math
from bisect import bisect_left

import numpy as np
import ml_dtypes

import concourse.bass as bass
import concourse.mybir as mybir
from concourse.bass_utils import run_bass_kernel_spmd

F32 = mybir.dt.float32
BF16 = mybir.dt.bfloat16
U32 = mybir.dt.uint32
I32 = mybir.dt.int32
AF = mybir.ActivationFunctionType
OP = mybir.AluOpType
AX = mybir.AxisListType

BQ = 4
T = 2048
D = 1024
NCHUNK = D // 128
NTILE = T // 128



class _Op:
    __slots__ = ("fn", "waits", "incs", "is_dma")

    def __init__(self, fn, is_dma=False):
        self.fn = fn
        self.is_dma = is_dma
        self.waits = []
        self.incs = []


class _Dom:

    __slots__ = ("name", "sem", "ops", "inc_idx", "inc_cum", "step")

    def __init__(self, name, sem, step):
        self.name = name
        self.sem = sem
        self.ops = []
        self.inc_idx = []
        self.inc_cum = []
        self.step = step


class Buf:
    __slots__ = ("ap_", "writers", "readers", "name", "is_psum")

    def __init__(self, handle, name, is_psum=False):
        self.ap_ = handle.ap() if hasattr(handle, "ap") else handle
        self.name = name
        self.is_psum = is_psum
        self.writers = {}
        self.readers = {}

    def __getitem__(self, key):
        return self.ap_[key]

    @property
    def ap(self):
        return self.ap_[:]


ENGINES = ("pe", "act", "dve", "pool", "sp")
_ENG_ATTR = {"pe": "tensor", "act": "scalar", "dve": "vector",
             "pool": "gpsimd", "sp": "sync"}
_QUEUE_OF = {"sp": "q_sp", "act": "q_act", "pool": "q_pool", "dve": "q_dve"}


class Prog:
    def __init__(self, nc):
        self.nc = nc
        self.stack = contextlib.ExitStack()
        self.dom = {}
        for e in ENGINES:
            sem = self.stack.enter_context(nc.semaphore(f"s_{e}"))
            self.dom[e] = _Dom(e, sem, 1)
        self.qpool = {}
        for q in set(_QUEUE_OF.values()):
            sem = self.stack.enter_context(nc.semaphore(f"s_{q}"))
            self.dom[q] = _Dom(q, sem, 16)
            self.qpool[q] = [sem] + [
                self.stack.enter_context(nc.semaphore(f"s_{q}_{i}"))
                for i in range(1, 4)]
        self.waited = {e: {} for e in ENGINES}
        self.bufs = []
        self._phase_stack = None
        self._pending_barrier = {}
        self._names = {}

    def _uname(self, name):
        if name in self._names:
            self._names[name] += 1
            return f"{name}__{self._names[name]}"
        self._names[name] = 0
        return name

    def sbuf(self, name, shape, dtype):
        st = self._phase_stack if self._phase_stack is not None else self.stack
        h = st.enter_context(self.nc.sbuf_tensor(self._uname(name), shape, dtype))
        b = Buf(h, name)
        self.bufs.append(b)
        return b

    def psum(self, name, shape, dtype):
        st = self._phase_stack if self._phase_stack is not None else self.stack
        h = st.enter_context(self.nc.psum_tensor(self._uname(name), shape, dtype))
        b = Buf(h, name, is_psum=True)
        self.bufs.append(b)
        return b

    def barrier(self):
        for dn, d in self.dom.items():
            if not d.ops:
                continue
            idx = len(d.ops)
            if dn in ENGINES:
                while idx > 0 and d.ops[idx - 1].is_dma:
                    idx -= 1
                if idx == 0:
                    continue
            v = self._ensure_inc(dn, idx)
            self._pending_barrier[dn] = v

    @contextlib.contextmanager
    def phase(self, name=""):
        self._phase_stack = contextlib.ExitStack()
        try:
            yield
        finally:
            self.barrier()
            self._phase_stack.close()
            self._phase_stack = None

    def dram(self, name, shape, dtype, kind="Internal"):
        h = self.nc.dram_tensor(name, shape, dtype, kind=kind)
        b = Buf(h, name)
        self.bufs.append(b)
        return b

    def _ensure_inc(self, dname, idx):
        d = self.dom[dname]
        if d.inc_idx and d.inc_idx[-1] >= idx:
            i = bisect_left(d.inc_idx, idx)
            return d.inc_cum[i]
        op = d.ops[idx - 1]
        cum = (d.inc_cum[-1] if d.inc_cum else 0) + d.step
        op.incs.append((d.sem, d.step))
        d.inc_idx.append(idx)
        d.inc_cum.append(cum)
        return cum

    def emit(self, eng, fn, reads=(), writes=(), dma_q=None, no_waw=False):
        op = _Op(fn, is_dma=bool(dma_q))
        if self._pending_barrier:
            w = self.waited[eng]
            for dn, v in self._pending_barrier.items():
                if w.get(dn, 0) < v:
                    op.waits.append((self.dom[dn].sem, v))
                    w[dn] = v
        deps = {}
        for b in reads:
            for dn, (idx, _o) in b.writers.items():
                if deps.get(dn, 0) < idx:
                    deps[dn] = idx
            if b.is_psum:
                for dn, (idx, _o) in b.readers.items():
                    if dn != eng and deps.get(dn, 0) < idx:
                        deps[dn] = idx
        for b in writes:
            for dn, (idx, _o) in b.writers.items():
                if not no_waw and deps.get(dn, 0) < idx:
                    deps[dn] = idx
            for dn, (idx, _o) in b.readers.items():
                if deps.get(dn, 0) < idx:
                    deps[dn] = idx
        mydom = _QUEUE_OF[eng] if dma_q else eng
        for dn, idx in deps.items():
            if dn == "pe" and eng == "pe":
                continue
            if dn == mydom and dn == eng:
                pass
            v = self._ensure_inc(dn, idx)
            w = self.waited[eng]
            if w.get(dn, 0) < v:
                op.waits.append((self.dom[dn].sem, v))
                w[dn] = v
        d = self.dom[mydom]
        d.ops.append(op)
        idx = len(d.ops)
        if mydom != eng:
            self.dom[eng].ops.append(op)
            self._ensure_inc(mydom, idx)
        for b in writes:
            b.writers = {mydom: (idx, op)}
            b.readers = {}
        for b in reads:
            b.readers[mydom] = (idx, op)
        return op

    def dma(self, eng, out, in_, out_buf=None, in_buf=None, no_waw=False, **kw):
        reads = [in_buf] if in_buf is not None else []
        writes = [out_buf] if out_buf is not None else []
        return self.emit(eng, lambda e: e.dma_start(out=out, in_=in_, **kw),
                         reads=reads, writes=writes, dma_q=eng, no_waw=no_waw)

    def replay(self):
        nc = self.nc
        with nc.Block() as block:
            for e in ENGINES:
                ops = self.dom[e].ops
                if not ops and e != "sp":
                    continue

                def run(engobj, ops=ops, e=e):
                    for op in ops:
                        for sem, v in op.waits:
                            engobj.wait_ge(sem, v)
                        ins = op.fn(engobj)
                        for sem, amt in op.incs:
                            ins.then_inc(sem, amt)
                    if e == "sp":
                        for sem, v in getattr(self, "_final_qwaits", []):
                            engobj.wait_ge(sem, v)
                getattr(block, _ENG_ATTR[e])(run)

    def close(self):
        self.stack.close()

    def finish(self):
        for q in set(_QUEUE_OF.values()):
            d = self.dom[q]
            if d.ops:
                self._ensure_inc(q, len(d.ops))
        self._round_dma_waits_to_bursts()
        self.replay()
        self.close()

    def _round_dma_waits_to_bursts(self):
        qnames = set(_QUEUE_OF.values())
        opq = {}
        for q in qnames:
            for i, op in enumerate(self.dom[q].ops):
                opq[id(op)] = (q, i + 1)
        burst_start = {}
        for e in ENGINES:
            boundary = True
            cur_start = {}
            for op in self.dom[e].ops:
                if op.waits:
                    boundary = True
                ent = opq.get(id(op))
                if ent is not None:
                    q, i = ent
                    if boundary or q not in cur_start:
                        cur_start[q] = i
                        boundary = False
                    burst_start[(q, i)] = cur_start[q]
        burst_end = {}
        for (q, i), s in burst_start.items():
            key = (q, s)
            if burst_end.get(key, 0) < i:
                burst_end[key] = i
        K = 4
        qinfo = {}
        for q in qnames:
            n = len(self.dom[q].ops)
            if n == 0:
                continue
            starts = sorted({burst_start[(q, i)] for i in range(1, n + 1)})
            sem_of_burst = {s: self.qpool[q][j % K] for j, s in enumerate(starts)}
            cnt = {id(sem): 0 for sem in self.qpool[q]}
            cum_at = [None] * (n + 1)
            sem_at = [None] * (n + 1)
            prev_total_of_burst = {}
            for i in range(1, n + 1):
                s = burst_start[(q, i)]
                sem = sem_of_burst[s]
                if i == s:
                    prev_total_of_burst[s] = cnt[id(sem)]
                cnt[id(sem)] += 1
                cum_at[i] = dict(cnt)
                sem_at[i] = sem
            qinfo[q] = (burst_start, burst_end, sem_of_burst, cum_at, sem_at,
                        prev_total_of_burst, n)
            oldsem = self.dom[q].sem
            for i in range(1, n + 1):
                op = self.dom[q].ops[i - 1]
                op.incs = [((sem_at[i], amt) if s_ is oldsem else (s_, amt))
                           for (s_, amt) in op.incs]
            for s in prev_total_of_burst:
                if prev_total_of_burst[s] > 0:
                    self.dom[q].ops[s - 1].waits.append(
                        (sem_of_burst[s], 16 * prev_total_of_burst[s]))
        oldsems = {id(self.dom[q].sem): q for q in qnames if self.dom[q].ops}
        for e in ENGINES:
            for op in self.dom[e].ops:
                neww = []
                for (s_, v) in op.waits:
                    q = oldsems.get(id(s_))
                    if q is None:
                        neww.append((s_, v))
                        continue
                    (bs, be, sob, cum_at, sem_at, ptb, n) = qinfo[q]
                    i = v // 16
                    if i <= 0:
                        continue
                    i = min(i, n)
                    end = be[(q, bs[(q, i)])]
                    for sem in self.qpool[q]:
                        c = cum_at[end].get(id(sem), 0)
                        if c > 0:
                            neww.append((sem, 16 * c))
                op.waits = neww
        self._final_qwaits = []
        for q in qnames:
            if q not in qinfo:
                continue
            (bs, be, sob, cum_at, sem_at, ptb, n) = qinfo[q]
            for sem in self.qpool[q]:
                c = cum_at[n].get(id(sem), 0)
                if c > 0:
                    self._final_qwaits.append((sem, 16 * c))
        for e in ENGINES:
            mx = {}
            for op in self.dom[e].ops:
                kept = []
                for (s_, v) in op.waits:
                    if mx.get(id(s_), -1) < v:
                        kept.append((s_, v))
                        mx[id(s_)] = v
                op.waits = kept


def _make_io(nc, specs):
    io = {}
    for name, shape, dt, kind in specs:
        io[name] = Buf(nc.dram_tensor(name, shape, dt, kind=kind), name)
    return io


def _host_consts():
    c = {}
    c["ident_f32"] = np.eye(128, dtype=np.float32)
    c["ident_bf16"] = np.eye(128).astype(ml_dtypes.bfloat16)
    return c



_IO_P1 = [
    ("x", [BQ, T, D], F32, "ExternalInput"),
    ("Wq", [D, D], F32, "ExternalInput"),
    ("Wk", [D, D], F32, "ExternalInput"),
    ("Wv", [D, D], F32, "ExternalInput"),
    ("beta_w", [D, 1], F32, "ExternalInput"),
    ("beta_b", [1], F32, "ExternalInput"),
    ("norm1_w", [D], F32, "ExternalInput"),
    ("alpha_logit", [D], F32, "ExternalInput"),
    ("ident_f32", [128, 128], F32, "ExternalInput"),
    ("ident_bf16", [128, 128], BF16, "ExternalInput"),
    ("XN_FM", [BQ, NCHUNK, 128, T], F32, "ExternalOutput"),
    ("Q_FM", [BQ, NCHUNK, 128, T], BF16, "ExternalOutput"),
    ("K_FM", [BQ, NCHUNK, 128, T], BF16, "ExternalOutput"),
    ("K_TM", [BQ, NTILE, 128, D], BF16, "ExternalOutput"),
    ("V_TM", [BQ, NTILE, 128, D], F32, "ExternalOutput"),
    ("BETA", [BQ, T], F32, "ExternalOutput"),
    ("RSTD2", [BQ, T], F32, "ExternalOutput"),
    ("RSTD2i", [BQ, T], F32, "Internal"),
]

def emit_p1(p, io, nbatch=BQ, sections=('a','T','r','k','q','v','e')):
    nc = p.nc
    x_in = io["x"]
    wq_in = io["Wq"]
    wk_in = io["Wk"]
    wv_in = io["Wv"]
    bw_in = io["beta_w"]
    bb_in = io["beta_b"]
    n1_in = io["norm1_w"]
    al_in = io["alpha_logit"]
    id32_in = io["ident_f32"]
    idbf_in = io["ident_bf16"]
    XN_FM = io["XN_FM"]
    Q_FM = io["Q_FM"]
    K_FM = io["K_FM"]
    K_TM = io["K_TM"]
    V_TM = io["V_TM"]
    BETA = io["BETA"]
    RSTD2 = io["RSTD2"]
    RSTD2i = io["RSTD2i"]
    emit, dma = p.emit, p.dma

    idf = p.sbuf("idf", [128, 128], F32)
    idb = p.sbuf("idb", [128, 128], BF16)
    n1c = p.sbuf("n1c", [128, NCHUNK], F32)
    alc = p.sbuf("alc", [128, NCHUNK], F32)
    bbc = p.sbuf("bbc", [128, 1], F32)
    ones1 = p.sbuf("ones1", [1, 128], F32)
    eps5 = p.sbuf("eps5", [128, 1], F32)
    eps6 = p.sbuf("eps6", [128, 1], F32)

    dma("sp", idf.ap, id32_in.ap, out_buf=idf, in_buf=id32_in)
    dma("sp", idb.ap, idbf_in.ap, out_buf=idb, in_buf=idbf_in)
    vrows = p.sbuf("vrows", [8, 128], F32)
    bwc = p.sbuf("bwc", [128, NCHUNK], F32)
    ps_t = [p.psum(f"ps_t{i}", [128, 128], F32) for i in range(2)]
    ps_v8 = ps_t[0]
    for src_in, view, dst in (
        (n1_in, n1_in.ap_.rearrange("(c p) -> c p", p=128), n1c),
        (al_in, al_in.ap_.rearrange("(c p) -> c p", p=128), alc),
        (bw_in, bw_in.ap_.rearrange("(c p) one -> c (p one)", p=128), bwc),
    ):
        dma("sp", vrows.ap, view, out_buf=vrows, in_buf=src_in)
        emit("pe", lambda e: e.transpose(ps_v8[:, 0:8], vrows.ap, idf[0:8, 0:8]),
             reads=[vrows, idf], writes=[ps_v8])
        emit("dve", lambda e, dst=dst: e.tensor_copy(out=dst.ap, in_=ps_v8[:, 0:8]),
             reads=[ps_v8], writes=[dst])
    emit("dve", lambda e: e.memset(ones1.ap, 1.0), writes=[ones1])
    emit("dve", lambda e: e.memset(eps5.ap, 1e-5), writes=[eps5])
    emit("dve", lambda e: e.memset(eps6.ap, 1e-6), writes=[eps6])
    emit("act", lambda e: e.activation(out=alc.ap, in_=alc.ap, func=AF.Sigmoid),
         reads=[alc], writes=[alc])
    a1c = p.sbuf("a1c", [128, NCHUNK], F32)
    emit("dve", lambda e: e.memset(a1c.ap, 1.0), writes=[a1c])
    emit("dve", lambda e: e.tensor_tensor(out=a1c.ap, in0=a1c.ap, in1=alc.ap,
         op=OP.subtract), reads=[a1c, alc], writes=[a1c])

    bb1 = p.sbuf("bb1", [1, 1], F32)
    ps_bb = ps_t[0]
    dma("sp", bb1.ap, bb_in.ap_[:, None], out_buf=bb1, in_buf=bb_in)
    emit("pe", lambda e: e.matmul(ps_bb[:, 0:1], ones1.ap, bb1.ap, start=True, stop=True),
         reads=[ones1, bb1], writes=[ps_bb])
    emit("dve", lambda e: e.tensor_copy(out=bbc.ap, in_=ps_bb[:, 0:1]),
         reads=[ps_bb], writes=[bbc])

    wq = p.sbuf("wq", [128, NCHUNK * D], BF16)
    wk = p.sbuf("wk", [128, NCHUNK * D], BF16)
    wv = p.sbuf("wv", [128, NCHUNK * D], BF16)
    wb = p.sbuf("wb", [128, NCHUNK], BF16)
    wstg = p.sbuf("wstg", [128, D], F32)
    for name, win, wout in (("q", wq_in, wq), ("k", wk_in, wk), ("v", wv_in, wv)):
        for c in range(NCHUNK):
            dma("sp", wstg.ap, win.ap_[c * 128:(c + 1) * 128, :], out_buf=wstg, in_buf=win)
            emit("dve", lambda e, c=c, wout=wout: e.tensor_scalar(
                out=wout[:, c * D:(c + 1) * D], in0=wstg.ap,
                scalar1=n1c[:, c:c + 1], scalar2=None, op0=OP.mult),
                reads=[wstg, n1c], writes=[wout], no_waw=True)
    emit("dve", lambda e: e.tensor_tensor(out=wb.ap, in0=bwc.ap, in1=n1c.ap,
         op=OP.mult), reads=[bwc, n1c], writes=[wb])

    xts = [p.sbuf(f"xt{i}", [128, D], F32) for i in range(2)]
    xsqs = [p.sbuf(f"xsq{i}", [128, D], F32) for i in range(2)]
    s1s = [p.sbuf(f"s1_{i}", [128, 1], F32) for i in range(2)]
    s2s = [p.sbuf(f"s2_{i}", [128, 1], F32) for i in range(2)]
    mcols = [p.sbuf(f"mcol{i}", [128, 1], F32) for i in range(2)]
    vcols = [p.sbuf(f"vcol{i}", [128, 1], F32) for i in range(2)]
    r1cols = [p.sbuf(f"r1col{i}", [128, 1], F32) for i in range(2)]
    xn_tms = [p.sbuf(f"xn_tm{i}", [128, D], F32) for i in range(2)]
    xsq = xsqs[0]
    r2grp = p.sbuf("r2grp", [128, 4], F32)
    XNF = p.sbuf("XNF", [128, NCHUNK * 512], F32)
    XNFB = p.sbuf("XNFB", [128, NCHUNK * 512], BF16)
    ps_a = p.psum("ps_a", [128, 512], F32)
    ps_b = p.psum("ps_b", [128, 512], F32)
    ps_c = p.psum("ps_c", [128, 512], F32)
    ps_s = ps_t[1]
    kraw = p.sbuf("kraw", [128, D], F32)
    kn2 = p.sbuf("kn2", [128, 1], F32)
    ksc = p.sbuf("ksc", [128, 1], F32)
    ktmb = p.sbuf("ktmb", [128, D], BF16)
    kfmb = p.sbuf("kfmb", [128, 128], BF16)
    ps_kt = p.psum("ps_kt", [128, 128], BF16)
    qfmb = p.sbuf("qfmb", [128, 512], BF16)
    rrow = p.sbuf("rrow", [1, 512], F32)
    rbc = p.sbuf("rbc", [128, 512], F32)
    VRAW = p.sbuf("VRAW", [128, NCHUNK * T], F32)
    vema = p.sbuf("vema", [128, T], F32)
    bcol = p.sbuf("bcol", [128, 1], F32)

    inv_d = 1.0 / D

    for b in range(nbatch):
        for g in range(4):
            for ti in range(4):
                tile = g * 4 + ti
                t0 = tile * 128
                par = tile % 2
                xt, xsq, s1, s2 = xts[par], xsqs[par], s1s[par], s2s[par]
                mcol, vcol, r1col, xn_tm = mcols[par], vcols[par], r1cols[par], xn_tms[par]
                dma("sp", xt.ap, x_in.ap_[b, t0:t0 + 128, :], out_buf=xt, in_buf=x_in)
                emit("act", lambda e, xsq=xsq, xt=xt: e.square(out=xsq.ap, in_=xt.ap),
                     reads=[xt], writes=[xsq])
                emit("dve", lambda e, s1=s1, xt=xt: e.reduce_sum(out=s1.ap, in_=xt.ap, axis=AX.X),
                     reads=[xt], writes=[s1])
                emit("dve", lambda e, s2=s2, xsq=xsq: e.reduce_sum(out=s2.ap, in_=xsq.ap, axis=AX.X),
                     reads=[xsq], writes=[s2])
                emit("dve", lambda e, mcol=mcol, s1=s1: e.tensor_scalar(out=mcol.ap, in0=s1.ap,
                     scalar1=inv_d, scalar2=None, op0=OP.mult),
                     reads=[s1], writes=[mcol])
                emit("dve", lambda e, s2=s2, vcol=vcol: e.tensor_scalar(out=vcol.ap, in0=s2.ap,
                     scalar1=inv_d, scalar2=None, op0=OP.mult),
                     reads=[s2], writes=[vcol])
                emit("dve", lambda e, mcol=mcol, r1col=r1col: e.tensor_tensor(out=r1col.ap, in0=mcol.ap,
                     in1=mcol.ap, op=OP.mult), reads=[mcol], writes=[r1col])
                emit("dve", lambda e, r1col=r1col, vcol=vcol: e.tensor_tensor(out=vcol.ap, in0=vcol.ap,
                     in1=r1col.ap, op=OP.subtract), reads=[vcol, r1col], writes=[vcol])
                emit("act", lambda e, r1col=r1col, vcol=vcol: e.activation(out=r1col.ap, in_=vcol.ap,
                     func=AF.Sqrt, bias=eps5.ap), reads=[vcol, eps5], writes=[r1col])
                emit("dve", lambda e, r1col=r1col: e.reciprocal(out=r1col.ap, in_=r1col.ap),
                     reads=[r1col], writes=[r1col])
                emit("dve", lambda e, mcol=mcol, r1col=r1col, xn_tm=xn_tm, xt=xt: e.tensor_scalar(out=xn_tm.ap, in0=xt.ap,
                     scalar1=mcol.ap, scalar2=r1col.ap, op0=OP.subtract, op1=OP.mult),
                     reads=[xt, mcol, r1col], writes=[xn_tm])
                emit("act", lambda e, xn_tm=xn_tm, xsq=xsq: e.square(out=xsq.ap, in_=xn_tm.ap),
                     reads=[xn_tm], writes=[xsq])
                emit("dve", lambda e, s2=s2, xsq=xsq: e.reduce_sum(out=s2.ap, in_=xsq.ap, axis=AX.X),
                     reads=[xsq], writes=[s2])
                emit("dve", lambda e, ti=ti, s2=s2: e.tensor_scalar(out=r2grp[:, ti:ti + 1],
                     in0=s2.ap, scalar1=inv_d, scalar2=None, op0=OP.mult),
                     reads=[s2], writes=[r2grp], no_waw=True)
                emit("act", lambda e, ti=ti: e.activation(out=r2grp[:, ti:ti + 1],
                     in_=r2grp[:, ti:ti + 1], func=AF.Sqrt, bias=eps6.ap),
                     reads=[r2grp, eps6], writes=[r2grp], no_waw=True)
                emit("dve", lambda e, ti=ti: e.reciprocal(out=r2grp[:, ti:ti + 1],
                     in_=r2grp[:, ti:ti + 1]), reads=[r2grp], writes=[r2grp], no_waw=True)
                if 'r' in sections:
                    dma("act", RSTD2.ap_[b, t0:t0 + 128, None], r2grp[:, ti:ti + 1],
                        out_buf=RSTD2, in_buf=r2grp)
                    dma("act", RSTD2i.ap_[b, t0:t0 + 128, None], r2grp[:, ti:ti + 1],
                        out_buf=RSTD2i, in_buf=r2grp)
                for c in (range(NCHUNK) if 'T' in sections else []):
                    pst = ps_t[c % 2]
                    emit("pe", lambda e, c=c, pst=pst, xn_tm=xn_tm: e.transpose(
                        pst.ap, xn_tm[:, c * 128:(c + 1) * 128], idf.ap),
                        reads=[xn_tm, idf], writes=[pst])
                    emit("dve", lambda e, c=c, ti=ti, pst=pst: e.tensor_copy(
                        out=XNF[:, c * 512 + ti * 128: c * 512 + (ti + 1) * 128],
                        in_=pst.ap), reads=[pst], writes=[XNF], no_waw=True)
                    emit("act", lambda e, c=c, ti=ti, pst=pst: e.copy(
                        out=XNFB[:, c * 512 + ti * 128: c * 512 + (ti + 1) * 128],
                        in_=pst.ap), reads=[pst], writes=[XNFB], no_waw=True)

            for c in (range(NCHUNK) if 'T' in sections else []):
                dma("act", XN_FM.ap_[b, c, :, g * 512:(g + 1) * 512],
                    XNF[:, c * 512:(c + 1) * 512], out_buf=XN_FM, in_buf=XNF)
            if 'r' in sections:
                dma("sp", rrow.ap, RSTD2i.ap_[b, None, g * 512:(g + 1) * 512],
                    out_buf=rrow, in_buf=RSTD2i)
                emit("pe", lambda e: e.matmul(ps_c.ap, ones1.ap, rrow.ap, start=True, stop=True),
                     reads=[ones1, rrow], writes=[ps_c])
                emit("dve", lambda e: e.tensor_copy(out=rbc.ap, in_=ps_c.ap),
                     reads=[ps_c], writes=[rbc])

            for ti in (range(4) if 'k' in sections else []):
                tile = g * 4 + ti
                for h in range(2):
                    ps = ps_a if h == 0 else ps_b
                    for c in range(NCHUNK):
                        emit("pe", lambda e, c=c, ti=ti, h=h, ps=ps: e.matmul(
                            ps.ap,
                            XNFB[:, c * 512 + ti * 128: c * 512 + (ti + 1) * 128],
                            wk[:, c * D + h * 512: c * D + (h + 1) * 512],
                            start=(c == 0), stop=(c == NCHUNK - 1)),
                            reads=[XNFB, wk], writes=[ps])
                    emit("dve", lambda e, h=h, ps=ps: e.tensor_copy(
                        out=kraw[:, h * 512:(h + 1) * 512], in_=ps.ap),
                        reads=[ps], writes=[kraw], no_waw=True)
                ksq = xsqs[ti % 2]
                emit("act", lambda e, ksq=ksq: e.square(out=ksq.ap, in_=kraw.ap),
                     reads=[kraw], writes=[ksq])
                emit("dve", lambda e, ksq=ksq: e.reduce_sum(out=kn2.ap, in_=ksq.ap, axis=AX.X),
                     reads=[ksq], writes=[kn2])
                emit("act", lambda e: e.sqrt(out=kn2.ap, in_=kn2.ap),
                     reads=[kn2], writes=[kn2])
                emit("dve", lambda e, ti=ti: e.tensor_tensor(out=kn2.ap, in0=kn2.ap,
                     in1=r2grp[:, ti:ti + 1], op=OP.mult),
                     reads=[kn2, r2grp], writes=[kn2])
                emit("dve", lambda e: e.tensor_scalar(out=kn2.ap, in0=kn2.ap,
                     scalar1=1e-12, scalar2=None, op0=OP.max),
                     reads=[kn2], writes=[kn2])
                emit("dve", lambda e: e.reciprocal(out=kn2.ap, in_=kn2.ap),
                     reads=[kn2], writes=[kn2])
                emit("dve", lambda e, ti=ti: e.tensor_tensor(out=ksc.ap,
                     in0=r2grp[:, ti:ti + 1], in1=kn2.ap, op=OP.mult),
                     reads=[r2grp, kn2], writes=[ksc])
                emit("dve", lambda e: e.tensor_scalar(out=ktmb.ap, in0=kraw.ap,
                     scalar1=ksc.ap, scalar2=None, op0=OP.mult),
                     reads=[kraw, ksc], writes=[ktmb])
                dma("act", K_TM.ap_[b, tile, :, :], ktmb.ap, out_buf=K_TM, in_buf=ktmb)
                for c in range(NCHUNK):
                    emit("pe", lambda e, c=c: e.transpose(
                        ps_kt.ap, ktmb[:, c * 128:(c + 1) * 128], idb.ap),
                        reads=[ktmb, idb], writes=[ps_kt])
                    emit("dve", lambda e: e.tensor_copy(out=kfmb.ap, in_=ps_kt.ap),
                         reads=[ps_kt], writes=[kfmb])
                    dma("act", K_FM.ap_[b, c, :, tile * 128:(tile + 1) * 128],
                        kfmb.ap, out_buf=K_FM, in_buf=kfmb)
                for c in range(NCHUNK):
                    emit("pe", lambda e, c=c, ti=ti: e.matmul(
                        ps_s[:, 0:1],
                        XNFB[:, c * 512 + ti * 128: c * 512 + (ti + 1) * 128],
                        wb[:, c:c + 1], start=(c == 0), stop=(c == NCHUNK - 1)),
                        reads=[XNFB, wb], writes=[ps_s])
                emit("dve", lambda e, ti=ti: e.tensor_scalar(out=bcol.ap,
                     in0=ps_s[:, 0:1], scalar1=r2grp[:, ti:ti + 1], scalar2=None,
                     op0=OP.mult), reads=[ps_s, r2grp], writes=[bcol])
                emit("act", lambda e: e.activation(out=bcol.ap, in_=bcol.ap,
                     func=AF.Sigmoid, bias=bbc.ap), reads=[bcol, bbc], writes=[bcol])
                dma("act", BETA.ap_[b, tile * 128:(tile + 1) * 128, None], bcol.ap,
                    out_buf=BETA, in_buf=bcol)

            for c in (range(NCHUNK) if 'q' in sections else []):
                ps = ps_a if c % 2 == 0 else ps_b
                for k in range(NCHUNK):
                    emit("pe", lambda e, c=c, k=k, ps=ps: e.matmul(
                        ps.ap, wq[:, k * D + c * 128: k * D + (c + 1) * 128],
                        XNFB[:, k * 512:(k + 1) * 512],
                        start=(k == 0), stop=(k == NCHUNK - 1)),
                        reads=[wq, XNFB], writes=[ps])
                emit("dve", lambda e, ps=ps: e.tensor_copy(out=qfmb.ap, in_=ps.ap),
                     reads=[ps], writes=[qfmb])
                dma("act", Q_FM.ap_[b, c, :, g * 512:(g + 1) * 512], qfmb.ap,
                    out_buf=Q_FM, in_buf=qfmb)

            for c in (range(NCHUNK) if 'v' in sections else []):
                ps = ps_a if c % 2 == 0 else ps_b
                for k in range(NCHUNK):
                    emit("pe", lambda e, c=c, k=k, ps=ps: e.matmul(
                        ps.ap, wv[:, k * D + c * 128: k * D + (c + 1) * 128],
                        XNFB[:, k * 512:(k + 1) * 512],
                        start=(k == 0), stop=(k == NCHUNK - 1)),
                        reads=[wv, XNFB], writes=[ps])
                emit("dve", lambda e, c=c, g=g, ps=ps: e.tensor_tensor(
                    out=VRAW[:, c * T + g * 512: c * T + (g + 1) * 512],
                    in0=ps.ap, in1=rbc.ap, op=OP.mult),
                    reads=[ps, rbc], writes=[VRAW], no_waw=True)

        for c in (range(NCHUNK) if 'e' in sections else []):
            emit("dve", lambda e, c=c: e.tensor_scalar(
                 out=VRAW[:, c * T:(c + 1) * T],
                 in0=VRAW[:, c * T:(c + 1) * T], scalar1=alc[:, c:c + 1],
                 scalar2=None, op0=OP.mult), reads=[VRAW, alc], writes=[VRAW],
                 no_waw=True)
            emit("dve", lambda e, c=c: e.tensor_tensor_scan(out=vema.ap,
                 data0=a1c[:, c:c + 1].broadcast_to([128, T]),
                 data1=VRAW[:, c * T:(c + 1) * T], initial=0.0,
                 op0=OP.mult, op1=OP.add),
                 reads=[a1c, VRAW], writes=[vema])
            for tile in range(NTILE):
                pst = ps_t[tile % 2]
                emit("pe", lambda e, tile=tile, pst=pst: e.transpose(
                    pst.ap, vema[:, tile * 128:(tile + 1) * 128], idf.ap),
                    reads=[vema, idf], writes=[pst])
                emit("dve", lambda e, pst=pst: e.tensor_copy(out=kraw[:, 0:128],
                     in_=pst.ap), reads=[pst], writes=[kraw])
                dma("act", V_TM.ap_[b, tile, :, c * 128:(c + 1) * 128], kraw[:, 0:128],
                    out_buf=V_TM, in_buf=kraw)


def build_p1(nbatch=BQ, sections=('a','T','r','k','q','v','e')):
    nc = bass.Bass("TRN2", target_bir_lowering=False, debug=False)
    io = _make_io(nc, _IO_P1)
    p = Prog(nc)
    with p.phase():
        emit_p1(p, io, nbatch, sections)
    p.finish()
    return nc



_IO_PDELTA = [
    ("Q_FM", [BQ, NCHUNK, 128, T], BF16, "ExternalInput"),
    ("K_FM", [BQ, NCHUNK, 128, T], BF16, "ExternalInput"),
    ("K_TM", [BQ, NTILE, 128, D], BF16, "ExternalInput"),
    ("V_TM", [BQ, NTILE, 128, D], F32, "ExternalInput"),
    ("BETA", [BQ, T], F32, "ExternalInput"),
    ("RSTD2", [BQ, T], F32, "ExternalInput"),
    ("ident_bf16", [128, 128], BF16, "ExternalInput"),
    ("mask_su", [128, 128], F32, "ExternalInput"),
    ("mask_sl", [128, 128], F32, "ExternalInput"),
    ("mask_ui", [128, 128], F32, "ExternalInput"),
    ("O_FM", [BQ, NCHUNK, 128, T], BF16, "ExternalOutput"),
]

def emit_pdelta(p, io, nbatch=BQ, nchunks=NTILE):
    nc = p.nc
    QF = io["Q_FM"]
    KF = io["K_FM"]
    KT = io["K_TM"]
    VT = io["V_TM"]
    BETA = io["BETA"]
    RSTD2 = io["RSTD2"]
    idbf_in = io["ident_bf16"]
    msu_in = io["mask_su"]
    msl_in = io["mask_sl"]
    mui_in = io["mask_ui"]
    OF = io["O_FM"]
    emit, dma = p.emit, p.dma

    idb = p.sbuf("idb", [128, 128], BF16)
    msu = p.sbuf("msu", [128, 128], F32)
    msl = p.sbuf("msl", [128, 128], F32)
    mui = p.sbuf("mui", [128, 128], F32)
    ones1 = p.sbuf("ones1", [1, 128], F32)
    dma("sp", idb.ap, idbf_in.ap, out_buf=idb, in_buf=idbf_in)
    dma("sp", msu.ap, msu_in.ap, out_buf=msu, in_buf=msu_in)
    dma("sp", msl.ap, msl_in.ap, out_buf=msl, in_buf=msl_in)
    dma("sp", mui.ap, mui_in.ap, out_buf=mui, in_buf=mui_in)
    emit("dve", lambda e: e.memset(ones1.ap, 1.0), writes=[ones1])

    STB = [p.sbuf(f"STB{b}", [128, NCHUNK * D], F32) for b in range(nbatch)]
    for b in range(nbatch):
        emit("dve", lambda e, b=b: e.memset(STB[b].ap, 0.0), writes=[STB[b]])

    qfs = [p.sbuf(f"qf{i}", [128, 1024], BF16) for i in range(2)]
    kfs = [p.sbuf(f"kf{i}", [128, 1024], BF16) for i in range(2)]
    kts = [p.sbuf(f"kt{i}", [128, 1024], BF16) for i in range(2)]
    vts = [p.sbuf(f"vt{i}", [128, 1024], F32) for i in range(2)]
    bcols = [p.sbuf(f"bcol{i}", [128, 1], F32) for i in range(2)]
    nbcols = [p.sbuf(f"nbcol{i}", [128, 1], F32) for i in range(2)]
    brows = [p.sbuf(f"brow{i}", [1, 128], F32) for i in range(2)]
    rcols = [p.sbuf(f"rcol{i}", [128, 1], F32) for i in range(2)]
    nbcs = [p.sbuf(f"nbc{i}", [128, 128], F32) for i in range(2)]
    Sbfs = [p.sbuf(f"Sbf{i}", [128, 1024], BF16) for i in range(2)]
    Wt_s = [p.sbuf(f"Wt_{i}", [128, 128], BF16) for i in range(2)]
    W_s = [p.sbuf(f"W_{i}", [128, 128], BF16) for i in range(2)]
    Pts = [p.sbuf(f"Pt{i}", [128, 128], BF16) for i in range(2)]
    MqTs = [p.sbuf(f"MqT{i}", [128, 128], BF16) for i in range(2)]
    Bbs = [p.sbuf(f"Bb{i}", [128, 1024], BF16) for i in range(2)]
    Ubs = [p.sbuf(f"Ub{i}", [128, 1024], BF16) for i in range(2)]
    otms = [p.sbuf(f"otm{i}", [128, 1024], BF16) for i in range(2)]
    ofms = [p.sbuf(f"ofm{i}", [128, 128], BF16) for i in range(2)]
    tmps = [p.sbuf(f"tmpd{i}", [128, 128], F32) for i in range(2)]

    psG = p.psum("psG", [128, 128], F32)
    psGq = p.psum("psGq", [128, 128], F32)
    psM = p.psum("psM", [128, 128], F32)
    psX = p.psum("psX", [128, 128], BF16)
    psW = [p.psum(f"psW{h}", [128, 512], F32) for h in range(2)]
    psO = [p.psum(f"psO{h}", [128, 512], F32) for h in range(2)]

    for ci in range(nchunks):
        for b in range(nbatch):
            S = STB[b]
            par = b % 2
            qf, kf, kt, vt = qfs[par], kfs[par], kts[par], vts[par]
            bcol, nbcol, brow, rcol = bcols[par], nbcols[par], brows[par], rcols[par]
            nbc, Sbf, Wt_, W_, Pt = nbcs[par], Sbfs[par], Wt_s[par], W_s[par], Pts[par]
            MqT, Bb, Ub, otm, ofm, tmp = MqTs[par], Bbs[par], Ubs[par], otms[par], ofms[par], tmps[par]
            t0 = ci * 128
            for c in range(NCHUNK):
                dma("sp", qf[:, c * 128:(c + 1) * 128],
                    QF.ap_[b, c, :, t0:t0 + 128], out_buf=qf, in_buf=QF,
                    no_waw=(c > 0))
                dma("sp", kf[:, c * 128:(c + 1) * 128],
                    KF.ap_[b, c, :, t0:t0 + 128], out_buf=kf, in_buf=KF,
                    no_waw=(c > 0))
            dma("act", kt.ap, KT.ap_[b, ci, :, :], out_buf=kt, in_buf=KT)
            dma("act", vt.ap, VT.ap_[b, ci, :, :], out_buf=vt, in_buf=VT)
            dma("pool", bcol.ap, BETA.ap_[b, t0:t0 + 128, None], out_buf=bcol, in_buf=BETA)
            dma("pool", brow.ap, BETA.ap_[b, None, t0:t0 + 128], out_buf=brow, in_buf=BETA)
            dma("pool", rcol.ap, RSTD2.ap_[b, t0:t0 + 128, None], out_buf=rcol, in_buf=RSTD2)

            for c in range(NCHUNK):
                emit("pe", lambda e, c=c, kf=kf: e.matmul(psG.ap, kf[:, c*128:(c+1)*128],
                     kf[:, c*128:(c+1)*128], start=(c == 0), stop=(c == NCHUNK-1)),
                     reads=[kf], writes=[psG])
            for c in range(NCHUNK):
                emit("pe", lambda e, c=c, kf=kf, qf=qf: e.matmul(psGq.ap, kf[:, c*128:(c+1)*128],
                     qf[:, c*128:(c+1)*128], start=(c == 0), stop=(c == NCHUNK-1)),
                     reads=[kf, qf], writes=[psGq])
            emit("pe", lambda e, brow=brow: e.matmul(psM.ap, ones1.ap, brow.ap, start=True, stop=True),
                 reads=[ones1, brow], writes=[psM])
            emit("dve", lambda e, nbc=nbc: e.tensor_scalar(out=nbc.ap, in0=psM.ap,
                 scalar1=-1.0, scalar2=None, op0=OP.mult), reads=[psM], writes=[nbc])
            emit("dve", lambda e, bcol=bcol, nbcol=nbcol: e.tensor_scalar(out=nbcol.ap, in0=bcol.ap,
                 scalar1=-1.0, scalar2=None, op0=OP.mult), reads=[bcol], writes=[nbcol])

            emit("dve", lambda e, MqT=MqT: e.tensor_tensor(out=MqT.ap, in0=psGq.ap,
                 in1=mui.ap, op=OP.mult), reads=[psGq, mui], writes=[MqT])
            emit("dve", lambda e, tmp=tmp: e.tensor_tensor(out=tmp.ap, in0=psG.ap,
                 in1=msu.ap, op=OP.mult), reads=[psG, msu], writes=[tmp])
            emit("dve", lambda e, W_=W_, nbc=nbc, tmp=tmp: e.tensor_tensor(out=W_.ap, in0=tmp.ap,
                 in1=nbc.ap, op=OP.mult), reads=[tmp, nbc], writes=[W_])
            emit("dve", lambda e, tmp=tmp: e.tensor_tensor(out=tmp.ap, in0=psG.ap,
                 in1=msl.ap, op=OP.mult), reads=[psG, msl], writes=[tmp])
            emit("dve", lambda e, Wt_=Wt_, nbcol=nbcol, tmp=tmp: e.tensor_scalar(out=Wt_.ap, in0=tmp.ap,
                 scalar1=nbcol.ap, scalar2=None, op0=OP.mult),
                 reads=[tmp, nbcol], writes=[Wt_])
            emit("dve", lambda e, Pt=Pt, W_=W_: e.tensor_tensor(out=Pt.ap, in0=W_.ap,
                 in1=idb.ap, op=OP.add), reads=[W_, idb], writes=[Pt])

            for c in range(NCHUNK):
                emit("dve" if c % 2 == 0 else "act",
                     (lambda e, c=c, S=S, Sbf=Sbf: e.tensor_copy(out=Sbf.ap, in_=S[:, c*1024:(c+1)*1024]))
                     if c % 2 == 0 else
                     (lambda e, c=c, S=S, Sbf=Sbf: e.copy(out=Sbf.ap, in_=S[:, c*1024:(c+1)*1024])),
                     reads=[S], writes=[Sbf])
                for h in range(2):
                    emit("pe", lambda e, c=c, h=h, Sbf=Sbf, kf=kf: e.matmul(psW[h].ap,
                         kf[:, c*128:(c+1)*128], Sbf[:, h*512:(h+1)*512],
                         start=(c == 0), stop=(c == NCHUNK-1)),
                         reads=[kf, Sbf], writes=[psW[h]])
                    emit("pe", lambda e, c=c, h=h, Sbf=Sbf, qf=qf: e.matmul(psO[h].ap,
                         qf[:, c*128:(c+1)*128], Sbf[:, h*512:(h+1)*512],
                         start=(c == 0), stop=False, skip_group_check=True),
                         reads=[qf, Sbf], writes=[psO[h]])

            for k in range(1, 7):
                emit("pe", lambda e, W_=W_, Wt_=Wt_: e.matmul(psG.ap, Wt_.ap, W_.ap, start=True, stop=True),
                     reads=[Wt_, W_], writes=[psG])
                emit("pe", lambda e, W_=W_, Wt_=Wt_: e.matmul(psGq.ap, W_.ap, Wt_.ap, start=True, stop=True),
                     reads=[W_, Wt_], writes=[psGq])
                emit("dve", lambda e, W_=W_: e.tensor_copy(out=W_.ap, in_=psG.ap),
                     reads=[psG], writes=[W_])
                emit("dve", lambda e, Wt_=Wt_: e.tensor_copy(out=Wt_.ap, in_=psGq.ap),
                     reads=[psGq], writes=[Wt_])
                emit("pe", lambda e, Pt=Pt: e.matmul(psM.ap, idb.ap, Pt.ap, start=True, stop=False),
                     reads=[idb, Pt], writes=[psM])
                emit("pe", lambda e, Pt=Pt, Wt_=Wt_: e.matmul(psM.ap, Wt_.ap, Pt.ap, start=False, stop=True),
                     reads=[Wt_, Pt], writes=[psM])
                emit("dve", lambda e, Pt=Pt: e.tensor_copy(out=Pt.ap, in_=psM.ap),
                     reads=[psM], writes=[Pt])

            for h in range(2):
                emit("dve", lambda e, h=h, vt=vt: e.scalar_tensor_tensor(
                     out=vt[:, h*512:(h+1)*512], in0=psW[h].ap, scalar=-1.0,
                     in1=vt[:, h*512:(h+1)*512], op0=OP.mult, op1=OP.add),
                     reads=[psW[h], vt], writes=[vt], no_waw=True)
            emit("dve", lambda e, Bb=Bb, bcol=bcol, vt=vt: e.tensor_scalar(out=Bb.ap, in0=vt.ap,
                 scalar1=bcol.ap, scalar2=None, op0=OP.mult),
                 reads=[vt, bcol], writes=[Bb])

            for h in range(2):
                emit("pe", lambda e, h=h, Bb=Bb, Pt=Pt: e.matmul(psW[h].ap, Pt.ap,
                     Bb[:, h*512:(h+1)*512], start=True, stop=True),
                     reads=[Pt, Bb], writes=[psW[h]])
                emit("dve", lambda e, h=h, Ub=Ub: e.tensor_copy(
                     out=Ub[:, h*512:(h+1)*512], in_=psW[h].ap),
                     reads=[psW[h]], writes=[Ub], no_waw=True)

            for h in range(2):
                emit("pe", lambda e, h=h, MqT=MqT, Ub=Ub: e.matmul(psO[h].ap, MqT.ap,
                     Ub[:, h*512:(h+1)*512], start=False, stop=True,
                     skip_group_check=True),
                     reads=[MqT, Ub], writes=[psO[h]])
                emit("dve", lambda e, h=h, otm=otm, rcol=rcol: e.tensor_scalar(
                     out=otm[:, h*512:(h+1)*512], in0=psO[h].ap,
                     scalar1=rcol.ap, scalar2=None, op0=OP.mult),
                     reads=[psO[h], rcol], writes=[otm], no_waw=True)

            for c in range(NCHUNK):
                for h in range(2):
                    emit("pe", lambda e, c=c, h=h, Ub=Ub, kt=kt: e.matmul(psW[h].ap,
                         kt[:, c*128:(c+1)*128], Ub[:, h*512:(h+1)*512],
                         start=True, stop=True),
                         reads=[kt, Ub], writes=[psW[h]])
                    emit("dve", lambda e, c=c, h=h, S=S: e.tensor_tensor(
                         out=S[:, c*1024 + h*512: c*1024 + (h+1)*512],
                         in0=psW[h].ap,
                         in1=S[:, c*1024 + h*512: c*1024 + (h+1)*512],
                         op=OP.add),
                         reads=[psW[h], S], writes=[S], no_waw=True)

            for c in range(NCHUNK):
                emit("pe", lambda e, c=c, otm=otm: e.transpose(psX.ap,
                     otm[:, c*128:(c+1)*128], idb.ap),
                     reads=[otm, idb], writes=[psX])
                emit("dve", lambda e, ofm=ofm: e.tensor_copy(out=ofm.ap, in_=psX.ap),
                     reads=[psX], writes=[ofm])
                dma("act", OF.ap_[b, c, :, t0:t0 + 128], ofm.ap,
                    out_buf=OF, in_buf=ofm)


def build_pdelta(nbatch=BQ, nchunks=NTILE):
    nc = bass.Bass("TRN2", target_bir_lowering=False, debug=False)
    io = _make_io(nc, _IO_PDELTA)
    p = Prog(nc)
    with p.phase():
        emit_pdelta(p, io, nbatch, nchunks)
    p.finish()
    return nc



_IO_PC = [
    ("XN_FM", [BQ, NCHUNK, 128, T], F32, "ExternalInput"),
    ("O_FM", [BQ, NCHUNK, 128, T], BF16, "ExternalInput"),
    ("Wo", [D, D], F32, "ExternalInput"),
    ("norm2_w", [D], F32, "ExternalInput"),
    ("w_gate", [D, 2 * D], F32, "ExternalInput"),
    ("w_up", [D, 2 * D], F32, "ExternalInput"),
    ("w_down", [2 * D, D], F32, "ExternalInput"),
    ("ident_f32", [128, 128], F32, "ExternalInput"),
    ("S_FM", [BQ, NCHUNK, 128, T], F32, "ExternalOutput"),
]

def emit_pc(p, io, nbatch=BQ):
    nc = p.nc
    XN = io["XN_FM"]
    OF = io["O_FM"]
    WO = io["Wo"]
    N2 = io["norm2_w"]
    WG = io["w_gate"]
    WU = io["w_up"]
    WD = io["w_down"]
    idf_in = io["ident_f32"]
    SF = io["S_FM"]
    emit, dma = p.emit, p.dma

    idf = p.sbuf("idf", [128, 128], F32)
    dma("sp", idf.ap, idf_in.ap, out_buf=idf, in_buf=idf_in)
    onesc = p.sbuf("onesc", [128, 1], F32)
    ones1 = p.sbuf("ones1", [1, 128], F32)
    eps6 = p.sbuf("eps6", [128, 1], F32)
    emit("dve", lambda e: e.memset(onesc.ap, 1.0), writes=[onesc])
    emit("dve", lambda e: e.memset(ones1.ap, 1.0), writes=[ones1])
    emit("dve", lambda e: e.memset(eps6.ap, 1e-6), writes=[eps6])

    vrows = p.sbuf("vrows", [8, 128], F32)
    n2c = p.sbuf("n2c", [128, NCHUNK], F32)
    ps_t = [p.psum(f"ps_t{i}", [128, 128], F32) for i in range(2)]
    dma("sp", vrows.ap, N2.ap_.rearrange("(c p) -> c p", p=128), out_buf=vrows, in_buf=N2)
    emit("pe", lambda e: e.transpose(ps_t[0][:, 0:8], vrows.ap, idf[0:8, 0:8]),
         reads=[vrows, idf], writes=[ps_t[0]])
    emit("dve", lambda e: e.tensor_copy(out=n2c.ap, in_=ps_t[0][:, 0:8]),
         reads=[ps_t[0]], writes=[n2c])

    wstg = p.sbuf("wstg", [128, D], F32)
    wo = p.sbuf("wo", [128, NCHUNK * D], BF16)
    wg = p.sbuf("wg", [128, NCHUNK * 2 * D], BF16)
    wu = p.sbuf("wu", [128, NCHUNK * 2 * D], BF16)
    wd = p.sbuf("wd", [128, 2 * NCHUNK * D], BF16)
    for c in range(NCHUNK):
        dma("sp", wstg.ap, WO.ap_[c * 128:(c + 1) * 128, :], out_buf=wstg, in_buf=WO)
        emit("dve", lambda e, c=c: e.tensor_copy(out=wo[:, c * D:(c + 1) * D],
             in_=wstg.ap), reads=[wstg], writes=[wo], no_waw=True)
    for win, wout in ((WG, wg), (WU, wu)):
        for c in range(NCHUNK):
            for hh in range(2):
                dma("sp", wstg.ap, win.ap_[c * 128:(c + 1) * 128, hh * D:(hh + 1) * D],
                    out_buf=wstg, in_buf=win)
                emit("dve", lambda e, c=c, hh=hh, wout=wout: e.tensor_scalar(
                     out=wout[:, c * 2 * D + hh * D: c * 2 * D + (hh + 1) * D],
                     in0=wstg.ap, scalar1=n2c[:, c:c + 1], scalar2=None, op0=OP.mult),
                     reads=[wstg, n2c], writes=[wout], no_waw=True)
    for c in range(2 * NCHUNK):
        dma("sp", wstg.ap, WD.ap_[c * 128:(c + 1) * 128, :], out_buf=wstg, in_buf=WD)
        emit("dve", lambda e, c=c: e.tensor_copy(out=wd[:, c * D:(c + 1) * D],
             in_=wstg.ap), reads=[wstg], writes=[wd], no_waw=True)

    xnf = p.sbuf("xnf", [128, NCHUNK * 512], F32)
    of = p.sbuf("of", [128, NCHUNK * 512], BF16)
    yf = p.sbuf("yf", [128, NCHUNK * 512], F32)
    ysq = p.sbuf("ysq", [128, 512], F32)
    h2b = p.sbuf("h2b", [128, NCHUNK * 512], BF16)
    gatef = p.sbuf("gatef", [128, 4 * 512], F32)
    mlpin = p.sbuf("mlpin", [128, 16 * 512], BF16)
    sfo = p.sbuf("sfo", [128, 512], F32)
    rrow = p.sbuf("rrow", [1, 512], F32)
    rbc = p.sbuf("rbc", [128, 512], F32)
    rot4 = [p.psum(f"rot{i}", [128, 512], F32) for i in range(4)]
    rc4 = [0]

    def nextps():
        ps = rot4[rc4[0] % 4]
        rc4[0] += 1
        return ps
    psR = p.psum("psR", [1, 512], F32)
    psC = p.psum("psC", [128, 512], F32)

    for b in range(nbatch):
        for g in range(4):
            sl = slice(g * 512, (g + 1) * 512)
            for c in range(NCHUNK):
                dma("sp", xnf[:, c * 512:(c + 1) * 512], XN.ap_[b, c, :, sl],
                    out_buf=xnf, in_buf=XN, no_waw=(c > 0))
                dma("act", of[:, c * 512:(c + 1) * 512], OF.ap_[b, c, :, sl],
                    out_buf=of, in_buf=OF, no_waw=(c > 0))
            for c in range(NCHUNK):
                ps = nextps()
                for k in range(NCHUNK):
                    emit("pe", lambda e, c=c, k=k, ps=ps: e.matmul(ps.ap,
                         wo[:, k * D + c * 128: k * D + (c + 1) * 128],
                         of[:, k * 512:(k + 1) * 512],
                         start=(k == 0), stop=(k == NCHUNK - 1)),
                         reads=[wo, of], writes=[ps])
                emit("dve", lambda e, c=c, ps=ps: e.tensor_tensor(
                     out=yf[:, c * 512:(c + 1) * 512], in0=ps.ap,
                     in1=xnf[:, c * 512:(c + 1) * 512], op=OP.add),
                     reads=[ps, xnf], writes=[yf], no_waw=True)
            for c in range(NCHUNK):
                emit("act", lambda e, c=c: e.square(out=ysq.ap,
                     in_=yf[:, c * 512:(c + 1) * 512]), reads=[yf], writes=[ysq])
                emit("pe", lambda e, c=c: e.matmul(psR.ap, onesc.ap, ysq.ap,
                     start=(c == 0), stop=(c == NCHUNK - 1)),
                     reads=[onesc, ysq], writes=[psR])
            emit("dve", lambda e: e.tensor_scalar(out=rrow.ap, in0=psR.ap,
                 scalar1=1.0 / D, scalar2=None, op0=OP.mult),
                 reads=[psR], writes=[rrow])
            emit("act", lambda e: e.activation(out=rrow.ap, in_=rrow.ap,
                 func=AF.Sqrt, bias=eps6[0:1, :]), reads=[rrow, eps6], writes=[rrow])
            emit("dve", lambda e: e.reciprocal(out=rrow.ap, in_=rrow.ap),
                 reads=[rrow], writes=[rrow])
            emit("pe", lambda e: e.matmul(psC.ap, ones1.ap, rrow.ap,
                 start=True, stop=True), reads=[ones1, rrow], writes=[psC])
            emit("dve", lambda e: e.tensor_copy(out=rbc.ap, in_=psC.ap),
                 reads=[psC], writes=[rbc])
            for c in range(NCHUNK):
                emit("dve", lambda e, c=c: e.tensor_tensor(
                     out=h2b[:, c * 512:(c + 1) * 512],
                     in0=yf[:, c * 512:(c + 1) * 512], in1=rbc.ap, op=OP.mult),
                     reads=[yf, rbc], writes=[h2b], no_waw=True)
            for fh in range(4):
                for fc in range(4):
                    f = fh * 4 + fc
                    ps = nextps()
                    for k in range(NCHUNK):
                        emit("pe", lambda e, f=f, k=k, ps=ps: e.matmul(ps.ap,
                             wg[:, k * 2 * D + f * 128: k * 2 * D + (f + 1) * 128],
                             h2b[:, k * 512:(k + 1) * 512],
                             start=(k == 0), stop=(k == NCHUNK - 1)),
                             reads=[wg, h2b], writes=[ps])
                    emit("act", lambda e, fc=fc, ps=ps: e.activation(
                         out=gatef[:, fc * 512:(fc + 1) * 512], in_=ps.ap,
                         func=AF.Silu), reads=[ps], writes=[gatef], no_waw=True)
                    for k in range(NCHUNK):
                        emit("pe", lambda e, f=f, k=k, ps=ps: e.matmul(ps.ap,
                             wu[:, k * 2 * D + f * 128: k * 2 * D + (f + 1) * 128],
                             h2b[:, k * 512:(k + 1) * 512],
                             start=(k == 0), stop=(k == NCHUNK - 1)),
                             reads=[wu, h2b], writes=[ps])
                    emit("dve", lambda e, f=f, fc=fc, ps=ps: e.tensor_tensor(
                         out=mlpin[:, f * 512:(f + 1) * 512], in0=ps.ap,
                         in1=gatef[:, fc * 512:(fc + 1) * 512], op=OP.mult),
                         reads=[ps, gatef], writes=[mlpin], no_waw=True)
            for c in range(NCHUNK):
                ps = nextps()
                for k in range(16):
                    emit("pe", lambda e, c=c, k=k, ps=ps: e.matmul(ps.ap,
                         wd[:, k * D + c * 128: k * D + (c + 1) * 128],
                         mlpin[:, k * 512:(k + 1) * 512],
                         start=(k == 0), stop=(k == 15)),
                         reads=[wd, mlpin], writes=[ps])
                emit("dve", lambda e, c=c, ps=ps: e.tensor_tensor(
                     out=sfo.ap, in0=ps.ap,
                     in1=yf[:, c * 512:(c + 1) * 512], op=OP.add),
                     reads=[ps, yf], writes=[sfo])
                dma("act", SF.ap_[b, c, :, sl], sfo.ap, out_buf=SF, in_buf=sfo)


def build_pc(nbatch=BQ):
    nc = bass.Bass("TRN2", target_bir_lowering=False, debug=False)
    io = _make_io(nc, _IO_PC)
    p = Prog(nc)
    with p.phase():
        emit_pc(p, io, nbatch)
    p.finish()
    return nc



_P0_SPECS = [
    ("w1", [14 * 1024, 2048]), ("w2", [14 * 2048, 1024]),
    ("gug", [8 * 1024, 1024]), ("guu", [8 * 1024, 1024]),
    ("dn", [8 * 1024, 1024]),
]


def emit_p0(p, io):
    nc = p.nc
    stg = [p.sbuf(f"stg{i}", [128, 1024], F32) for i in range(3)]
    stb = [p.sbuf(f"stb{i}", [128, 1024], BF16) for i in range(3)]
    i = 0
    for name, (R, C) in _P0_SPECS:
        fin = io[name]
        fout = io[name + "b"]
        for r0 in range(0, R, 128):
            for c0 in range(0, C, 1024):
                j = i % 3
                i += 1
                p.dma("pool", stg[j].ap, fin.ap_[r0:r0 + 128, c0:c0 + 1024],
                      out_buf=stg[j], in_buf=fin)
                if j % 2 == 0:
                    p.emit("dve", lambda e, j=j: e.tensor_copy(
                        out=stb[j].ap, in_=stg[j].ap),
                        reads=[stg[j]], writes=[stb[j]])
                else:
                    p.emit("act", lambda e, j=j: e.copy(
                        out=stb[j].ap, in_=stg[j].ap),
                        reads=[stg[j]], writes=[stb[j]])
                p.dma("act", fout.ap_[r0:r0 + 128, c0:c0 + 1024], stb[j].ap,
                      out_buf=fout, in_buf=stb[j])


def _p0_io(nc, kind_in="ExternalInput", kind_out="ExternalOutput"):
    io = {}
    for name, (R, C) in _P0_SPECS:
        io[name] = Buf(nc.dram_tensor(name, [R, C], F32, kind=kind_in), name)
        io[name + "b"] = Buf(nc.dram_tensor(name + "b", [R, C], BF16, kind=kind_out), name + "b")
    return io


def build_p0():
    nc = bass.Bass("TRN2", target_bir_lowering=False, debug=False)
    io = _p0_io(nc)
    p = Prog(nc)
    with p.phase():
        emit_p0(p, io)
    p.finish()
    return nc


def _old_build_p0_unused():
    nc = bass.Bass("TRN2", target_bir_lowering=False, debug=False)
    specs = [
        ("w1", [14 * 1024, 2048]), ("w2", [14 * 2048, 1024]),
        ("gug", [8 * 1024, 1024]), ("guu", [8 * 1024, 1024]),
        ("dn", [8 * 1024, 1024]),
    ]
    p = Prog(nc)
    stg = [p.sbuf(f"stg{i}", [128, 2048], F32) for i in range(3)]
    stb = [p.sbuf(f"stb{i}", [128, 2048], BF16) for i in range(3)]
    qs = ["sp", "act", "pool"]
    i = 0
    for name, (R, C) in specs:
        fin = Buf(nc.dram_tensor(name, [R, C], F32, kind="ExternalInput"), name)
        fout = Buf(nc.dram_tensor(name + "b", [R, C], BF16, kind="ExternalOutput"), name + "b")
        for r0 in range(0, R, 128):
            j = i % 3
            i += 1
            p.dma(qs[j], stg[j].ap[:, 0:C] if C < 2048 else stg[j].ap,
                  fin.ap_[r0:r0 + 128, :], out_buf=stg[j], in_buf=fin)
            eng = "dve" if j % 2 == 0 else "act"
            if eng == "dve":
                p.emit(eng, lambda e, j=j, C=C: e.tensor_copy(
                    out=stb[j][:, 0:C], in_=stg[j][:, 0:C]),
                    reads=[stg[j]], writes=[stb[j]])
            else:
                p.emit(eng, lambda e, j=j, C=C: e.copy(
                    out=stb[j][:, 0:C], in_=stg[j][:, 0:C]),
                    reads=[stg[j]], writes=[stb[j]])
            p.dma(qs[(j + 1) % 3], fout.ap_[r0:r0 + 128, :], stb[j][:, 0:C],
                  out_buf=fout, in_buf=stb[j])
    p.finish()
    return nc



_IO_PD = [
    ("XN_FM", [BQ, NCHUNK, 128, T], F32, "ExternalInput"),
    ("S_FM", [BQ, NCHUNK, 128, T], F32, "ExternalInput"),
    ("x", [BQ, T, D], F32, "ExternalInput"),
    ("w1b", [14 * 1024, 2048], BF16, "ExternalInput"),
    ("w2b", [14 * 2048, 1024], BF16, "ExternalInput"),
    ("b1", [14, 2048], F32, "ExternalInput"),
    ("b2", [14, 1024], F32, "ExternalInput"),
    ("router_w", [D, 14], F32, "ExternalInput"),
    ("ident_f32", [128, 128], F32, "ExternalInput"),
    ("iota14", [128, 14], F32, "ExternalInput"),
    ("X1", [BQ, T, D], F32, "ExternalOutput"),
]

def emit_pd(p, io, nbatch=BQ, nexp=14, nsg=None):
    nc = p.nc
    E = 14
    XN = io["XN_FM"]
    SFM = io["S_FM"]
    X0 = io["x"]
    W1B = io["w1b"]
    W2B = io["w2b"]
    B1 = io["b1"]
    B2 = io["b2"]
    RW = io["router_w"]
    idf_in = io["ident_f32"]
    iota_in = io["iota14"]
    X1 = io["X1"]
    emit, dma = p.emit, p.dma

    idf = p.sbuf("idf", [128, 128], F32)
    iot = p.sbuf("iot", [128, E], F32)
    dma("sp", idf.ap, idf_in.ap, out_buf=idf, in_buf=idf_in)
    dma("sp", iot.ap, iota_in.ap, out_buf=iot, in_buf=iota_in)

    rwc = p.sbuf("rwc", [128, NCHUNK * E], F32)
    for c in range(NCHUNK):
        dma("sp", rwc[:, c * E:(c + 1) * E],
            RW.ap_.rearrange("(c p) e -> c p e", p=128)[c], out_buf=rwc,
            in_buf=RW, no_waw=(c > 0))
    xrt = [p.sbuf(f"xrt{i}", [128, 1024], F32) for i in range(2)]
    xstg = xrt[0]
    b1rows = p.sbuf("b1rows", [E, 1024], F32)
    b2rows = p.sbuf("b2rows", [E, 1024], BF16)
    b1c = p.sbuf("b1c", [128, 16 * E], F32)
    ps_t = p.psum("ps_t", [128, 128], F32)
    dma("sp", xstg[0:E, :], B2.ap, out_buf=xstg, in_buf=B2)
    emit("dve", lambda e: e.tensor_copy(out=b2rows.ap, in_=xstg[0:E, :]),
         reads=[xstg], writes=[b2rows])
    for bh in range(2):
        dma("sp", b1rows.ap, B1.ap_[:, bh * 1024:(bh + 1) * 1024],
            out_buf=b1rows, in_buf=B1)
        for fc8 in range(8):
            fc = bh * 8 + fc8
            emit("pe", lambda e, fc8=fc8: e.transpose(ps_t[:, 0:E],
                 b1rows[:, fc8 * 128:(fc8 + 1) * 128], idf[0:E, 0:E]),
                 reads=[b1rows, idf], writes=[ps_t])
            emit("dve", lambda e, fc=fc: e.tensor_copy(
                 out=b1c[:, fc * E:(fc + 1) * E], in_=ps_t[:, 0:E]),
                 reads=[ps_t], writes=[b1c], no_waw=True)

    xnb = p.sbuf("xnb", [128, NCHUNK * 1024], BF16)
    wbuf = [p.sbuf(f"wbuf{i}", [128, 16384], BF16) for i in range(3)]
    hid = p.sbuf("hid", [128, 16 * 1024], BF16)
    acc = p.sbuf("acc", [128, 8 * 1024], F32)
    lg = p.sbuf("lg", [128, E], F32)
    mx8 = p.sbuf("mx8", [128, 8], F32)
    idxu = p.sbuf("idxu", [128, 8], U32)
    idxf = p.sbuf("idxf", [128, 1], F32)
    nmx = p.sbuf("nmx", [128, 1], F32)
    ssum = p.sbuf("ssum", [128, 1], F32)
    meq = p.sbuf("meq", [128, E], F32)
    CC = p.sbuf("CC", [128, 8 * E], F32)
    CTT = p.sbuf("CTT", [E, 8 * 128], BF16)
    gt = p.sbuf("gt", [128, 1024], F32)
    stm = p.sbuf("stm", [128, 1024], F32)
    x0t = p.sbuf("x0t", [128, 1024], F32)
    sstg = p.sbuf("sstg", [128, 128], F32)
    rot6 = [p.psum(f"rot{i}", [128, 512], F32) for i in range(6)]
    rotctr = [0]

    def nextps():
        ps = rot6[rotctr[0] % 6]
        rotctr[0] += 1
        return ps
    psR = p.psum("psR", [128, E], F32)

    NSG = 2 * nbatch if nsg is None else nsg
    for sg in range(NSG):
        b, hh = sg // 2, sg % 2
        tok0 = hh * 1024
        for ti in range(8):
            t0 = tok0 + ti * 128
            xr = xrt[ti % 2]
            dma("sp", xr.ap_.rearrange("p (c t) -> p c t", t=128),
                XN.ap_[b, :, :, t0:t0 + 128].transpose([1, 0, 2]),
                out_buf=xr, in_buf=XN)
            for c in range(NCHUNK):
                emit("pe", lambda e, c=c, xr=xr: e.matmul(psR.ap,
                     xr[:, c * 128:(c + 1) * 128],
                     rwc[:, c * E:(c + 1) * E], start=(c == 0),
                     stop=(c == NCHUNK - 1)), reads=[xr, rwc], writes=[psR])
            emit("dve", lambda e: e.tensor_copy(out=lg.ap, in_=psR.ap),
                 reads=[psR], writes=[lg])
            emit("dve", lambda e: e.max(out=mx8.ap, in_=lg.ap),
                 reads=[lg], writes=[mx8])
            emit("dve", lambda e: e.max_index(out=idxu.ap, in_max=mx8.ap,
                 in_values=lg.ap), reads=[mx8, lg], writes=[idxu])
            emit("dve", lambda e: e.tensor_copy(out=idxf.ap, in_=idxu[:, 0:1]),
                 reads=[idxu], writes=[idxf])
            emit("dve", lambda e: e.reduce_max(out=nmx.ap, in_=lg.ap, axis=AX.X,
                 negate=True), reads=[lg], writes=[nmx])
            emit("act", lambda e: e.activation(out=meq.ap, in_=lg.ap, func=AF.Exp,
                 bias=nmx.ap), reads=[lg, nmx], writes=[meq])
            emit("dve", lambda e: e.reduce_sum(out=ssum.ap, in_=meq.ap, axis=AX.X),
                 reads=[meq], writes=[ssum])
            emit("dve", lambda e: e.reciprocal(out=ssum.ap, in_=ssum.ap),
                 reads=[ssum], writes=[ssum])
            emit("dve", lambda e: e.tensor_scalar(out=meq.ap, in0=iot.ap,
                 scalar1=idxf.ap, scalar2=None, op0=OP.is_equal),
                 reads=[iot, idxf], writes=[meq])
            emit("dve", lambda e, ti=ti: e.tensor_scalar(
                 out=CC[:, ti * E:(ti + 1) * E], in0=meq.ap, scalar1=ssum.ap,
                 scalar2=None, op0=OP.mult), reads=[meq, ssum], writes=[CC],
                 no_waw=True)
            emit("pe", lambda e, ti=ti: e.transpose(ps_t[0:E, 0:128],
                 CC[:, ti * E:(ti + 1) * E], idf.ap),
                 reads=[CC, idf], writes=[ps_t])
            emit("dve", lambda e, ti=ti: e.tensor_copy(
                 out=CTT[:, ti * 128:(ti + 1) * 128], in_=ps_t[0:E, 0:128]),
                 reads=[ps_t], writes=[CTT], no_waw=True)
        for c in range(NCHUNK):
            xr = xrt[c % 2]
            dma("act", xr.ap, XN.ap_[b, c, :, tok0:tok0 + 1024],
                out_buf=xr, in_buf=XN)
            emit("dve", lambda e, c=c, xr=xr: e.tensor_copy(
                 out=xnb[:, c * 1024:(c + 1) * 1024], in_=xr.ap),
                 reads=[xr], writes=[xnb], no_waw=(c > 0))
        emit("dve", lambda e: e.memset(acc.ap, 0.0), writes=[acc])
        for ei in range(nexp):
            w1t = wbuf[(2 * ei) % 3]
            w2t = wbuf[(2 * ei + 1) % 3]
            for c in range(NCHUNK):
                dma("sp", w1t[:, c * 2048:(c + 1) * 2048],
                    W1B.ap_[ei * 1024 + c * 128: ei * 1024 + (c + 1) * 128, :],
                    out_buf=w1t, in_buf=W1B, no_waw=(c > 0))
            for fc in range(16):
                dma("pool", w2t[:, fc * 1024:(fc + 1) * 1024],
                    W2B.ap_[ei * 2048 + fc * 128: ei * 2048 + (fc + 1) * 128, :],
                    out_buf=w2t, in_buf=W2B, no_waw=(fc > 0))
            for fc in range(16):
                for th in range(2):
                    ps = nextps()
                    for k in range(NCHUNK):
                        emit("pe", lambda e, fc=fc, th=th, k=k, ps=ps, w1t=w1t:
                             e.matmul(ps.ap,
                             w1t[:, k * 2048 + fc * 128: k * 2048 + (fc + 1) * 128],
                             xnb[:, k * 1024 + th * 512: k * 1024 + (th + 1) * 512],
                             start=(k == 0), stop=(k == NCHUNK - 1)),
                             reads=[w1t, xnb], writes=[ps])
                    emit("act", lambda e, fc=fc, th=th, ps=ps, ei=ei:
                         e.activation(out=hid[:, fc * 1024 + th * 512:
                                              fc * 1024 + (th + 1) * 512],
                         in_=ps.ap, func=AF.Gelu,
                         bias=b1c[:, fc * E + ei: fc * E + ei + 1]),
                         reads=[ps, b1c], writes=[hid], no_waw=True)
            for ti in range(8):
                for h in range(2):
                    ps = nextps()
                    for fc in range(16):
                        emit("pe", lambda e, ti=ti, h=h, fc=fc, ps=ps, w2t=w2t:
                             e.matmul(ps.ap,
                             hid[:, fc * 1024 + ti * 128: fc * 1024 + (ti + 1) * 128],
                             w2t[:, fc * 1024 + h * 512: fc * 1024 + (h + 1) * 512],
                             start=(fc == 0), stop=(fc == 15)),
                             reads=[hid, w2t], writes=[ps])
                    emit("dve", lambda e, ti=ti, h=h, ps=ps, ei=ei:
                         e.scalar_tensor_tensor(
                         out=acc[:, ti * 1024 + h * 512: ti * 1024 + (h + 1) * 512],
                         in0=ps.ap, scalar=CC[:, ti * E + ei: ti * E + ei + 1],
                         in1=acc[:, ti * 1024 + h * 512: ti * 1024 + (h + 1) * 512],
                         op0=OP.mult, op1=OP.add),
                         reads=[ps, CC, acc], writes=[acc], no_waw=True)
        for ti in range(8):
            t0 = tok0 + ti * 128
            for h in range(2):
                ps = nextps()
                emit("pe", lambda e, ti=ti, h=h, ps=ps: e.matmul(ps.ap,
                     CTT[:, ti * 128:(ti + 1) * 128],
                     b2rows[:, h * 512:(h + 1) * 512], start=True, stop=True),
                     reads=[CTT, b2rows], writes=[ps])
                emit("dve", lambda e, ti=ti, h=h, ps=ps: e.tensor_tensor(
                     out=acc[:, ti * 1024 + h * 512: ti * 1024 + (h + 1) * 512],
                     in0=ps.ap,
                     in1=acc[:, ti * 1024 + h * 512: ti * 1024 + (h + 1) * 512],
                     op=OP.add), reads=[ps, acc], writes=[acc], no_waw=True)
            emit("act", lambda e, ti=ti: e.activation(out=gt.ap,
                 in_=acc[:, ti * 1024:(ti + 1) * 1024], func=AF.Silu),
                 reads=[acc], writes=[gt])
            for c in range(NCHUNK):
                dma("pool", sstg.ap, SFM.ap_[b, c, :, t0:t0 + 128],
                    out_buf=sstg, in_buf=SFM)
                emit("pe", lambda e, c=c: e.transpose(ps_t.ap, sstg.ap, idf.ap),
                     reads=[sstg, idf], writes=[ps_t])
                emit("dve", lambda e, c=c: e.tensor_copy(
                     out=stm[:, c * 128:(c + 1) * 128], in_=ps_t.ap),
                     reads=[ps_t], writes=[stm], no_waw=True)
            dma("sp", x0t.ap, X0.ap_[b, t0:t0 + 128, :], out_buf=x0t, in_buf=X0)
            emit("dve", lambda e: e.tensor_tensor(out=gt.ap, in0=gt.ap,
                 in1=stm.ap, op=OP.mult), reads=[gt, stm], writes=[gt])
            emit("dve", lambda e: e.tensor_tensor(out=gt.ap, in0=gt.ap,
                 in1=x0t.ap, op=OP.add), reads=[gt, x0t], writes=[gt])
            dma("act", X1.ap_[b, t0:t0 + 128, :], gt.ap, out_buf=X1, in_buf=gt)


def build_pd(nbatch=BQ, nexp=14, nsg=None):
    nc = bass.Bass("TRN2", target_bir_lowering=False, debug=False)
    io = _make_io(nc, _IO_PD)
    p = Prog(nc)
    with p.phase():
        emit_pd(p, io, nbatch, nexp, nsg)
    p.finish()
    return nc



_IO_PE = [
    ("X1", [BQ, T, D], F32, "ExternalInput"),
    ("gugb", [8 * 1024, 1024], BF16, "ExternalInput"),
    ("guub", [8 * 1024, 1024], BF16, "ExternalInput"),
    ("dnb", [8 * 1024, 1024], BF16, "ExternalInput"),
    ("gb_g", [8, 1024], F32, "ExternalInput"),
    ("gb_u", [8, 1024], F32, "ExternalInput"),
    ("db", [8, 1024], F32, "ExternalInput"),
    ("router_indices", [T, 2], I32, "ExternalInput"),
    ("routing_weights", [T, 8], F32, "ExternalInput"),
    ("flagcol", [128, 1], F32, "ExternalInput"),
    ("ident_f32", [128, 128], F32, "ExternalInput"),
    ("ident_bf16", [128, 128], BF16, "ExternalInput"),
    ("mask_li", [128, 128], F32, "ExternalInput"),
    ("OUT", [BQ, T, D], F32, "ExternalOutput"),
]

def emit_pe(p, io, nquarter=4, nexp=8, nheads=16):
    nc = p.nc
    E = 8
    X1 = io["X1"]
    GG = io["gugb"]
    GU = io["guub"]
    DN = io["dnb"]
    GBG = io["gb_g"]
    GBU = io["gb_u"]
    DB = io["db"]
    RI = io["router_indices"]
    RWT = io["routing_weights"]
    FLAG = io["flagcol"]
    idf_in = io["ident_f32"]
    idb_in = io["ident_bf16"]
    mli_in = io["mask_li"]
    OUT = io["OUT"]
    emit, dma = p.emit, p.dma

    idf = p.sbuf("idf", [128, 128], F32)
    idb = p.sbuf("idb", [128, 128], BF16)
    mli = p.sbuf("mli", [128, 128], F32)
    flagc = p.sbuf("flagc", [128, 1], F32)
    eighth = p.sbuf("eighth", [128, 1], F32)
    c1702 = p.sbuf("c1702", [128, 1], F32)
    eps5 = p.sbuf("eps5", [128, 1], F32)
    dma("sp", idf.ap, idf_in.ap, out_buf=idf, in_buf=idf_in)
    dma("sp", idb.ap, idb_in.ap, out_buf=idb, in_buf=idb_in)
    dma("sp", mli.ap, mli_in.ap, out_buf=mli, in_buf=mli_in)
    dma("sp", flagc.ap, FLAG.ap, out_buf=flagc, in_buf=FLAG)
    emit("dve", lambda e: e.memset(eighth.ap, 0.125), writes=[eighth])
    emit("dve", lambda e: e.memset(c1702.ap, 1.702), writes=[c1702])
    emit("dve", lambda e: e.memset(eps5.ap, 1e-5), writes=[eps5])

    xstg = p.sbuf("xstg", [128, D], F32)
    for b in range(1, BQ):
        for tile in range(NTILE):
            t0 = tile * 128
            dma("pool", xstg.ap, X1.ap_[b, t0:t0 + 128, :], out_buf=xstg, in_buf=X1)
            dma("pool", OUT.ap_[b, t0:t0 + 128, :], xstg.ap, out_buf=OUT, in_buf=xstg)

    gbgc = p.sbuf("gbgc", [128, 8 * E], F32)
    gbuc = p.sbuf("gbuc", [128, 8 * E], F32)
    dbr = p.sbuf("dbr", [E, 1024], BF16)
    ps_t = p.psum("ps_t", [128, 128], F32)
    for src_b, dst in ((GBG, gbgc), (GBU, gbuc)):
        dma("sp", xstg[0:E, :], src_b.ap, out_buf=xstg, in_buf=src_b)
        for fc in range(8):
            emit("pe", lambda e, fc=fc: e.transpose(ps_t[:, 0:E],
                 xstg[0:E, fc * 128:(fc + 1) * 128], idf[0:E, 0:E]),
                 reads=[xstg, idf], writes=[ps_t])
            emit("dve", lambda e, fc=fc, dst=dst: e.tensor_copy(
                 out=dst[:, fc * E:(fc + 1) * E], in_=ps_t[:, 0:E]),
                 reads=[ps_t], writes=[dst], no_waw=True)
    dma("sp", xstg[0:E, :], DB.ap, out_buf=xstg, in_buf=DB)
    emit("dve", lambda e: e.tensor_copy(out=dbr.ap, in_=xstg[0:E, :]),
         reads=[xstg], writes=[dbr])

    ri2 = p.sbuf("ri2", [128, 2], I32)
    rif = p.sbuf("rif", [128, 2], F32)
    rwt = p.sbuf("rwt", [128, E], F32)
    eqt = p.sbuf("eqt", [128, E], F32)
    C1 = p.sbuf("C1", [128, NTILE * E], F32)
    C1T = p.sbuf("C1T", [E, NTILE * 128], BF16)
    iota8 = p.sbuf("iota8", [128, E], F32)
    for e8 in range(E):
        emit("dve", lambda e, e8=e8: e.memset(iota8[:, e8:e8 + 1], float(e8)),
             writes=[iota8], no_waw=(e8 > 0))
    for tile in range(NTILE):
        t0 = tile * 128
        dma("sp", ri2.ap, RI.ap_[t0:t0 + 128, :], out_buf=ri2, in_buf=RI)
        dma("sp", rwt.ap, RWT.ap_[t0:t0 + 128, :], out_buf=rwt, in_buf=RWT)
        emit("dve", lambda e: e.tensor_copy(out=rif.ap, in_=ri2.ap),
             reads=[ri2], writes=[rif])
        emit("dve", lambda e: e.tensor_scalar(out=eqt.ap, in0=iota8.ap,
             scalar1=rif[:, 0:1], scalar2=None, op0=OP.is_equal),
             reads=[iota8, rif], writes=[eqt])
        emit("dve", lambda e: e.tensor_scalar(out=rwt.ap, in0=iota8.ap,
             scalar1=rif[:, 1:2], scalar2=rwt.ap, op0=OP.is_equal, op1=OP.bypass),
             reads=[iota8, rif], writes=[]) if False else None
        emit("dve", lambda e: e.tensor_scalar(out=rif.ap, in0=iota8[:, 0:2],
             scalar1=rif[:, 1:2], scalar2=None, op0=OP.is_equal),
             reads=[iota8, rif], writes=[rif]) if False else None
        emit("dve", lambda e, tile=tile: e.tensor_scalar(
             out=C1[:, tile * E:(tile + 1) * E],
             in0=iota8.ap, scalar1=rif[:, 1:2], scalar2=None, op0=OP.is_equal),
             reads=[iota8, rif], writes=[C1], no_waw=True)
        emit("dve", lambda e, tile=tile: e.tensor_tensor(
             out=eqt.ap, in0=eqt.ap, in1=C1[:, tile * E:(tile + 1) * E],
             op=OP.add), reads=[eqt, C1], writes=[eqt])
        emit("dve", lambda e, tile=tile: e.tensor_tensor(
             out=C1[:, tile * E:(tile + 1) * E], in0=eqt.ap, in1=rwt.ap,
             op=OP.mult), reads=[eqt, rwt], writes=[C1], no_waw=True)
        emit("pe", lambda e, tile=tile: e.transpose(ps_t[0:E, 0:128],
             C1[:, tile * E:(tile + 1) * E], idf.ap),
             reads=[C1, idf], writes=[ps_t])
        emit("dve", lambda e, tile=tile: e.tensor_copy(
             out=C1T[:, tile * 128:(tile + 1) * 128], in_=ps_t[0:E, 0:128]),
             reads=[ps_t], writes=[C1T], no_waw=True)

    XN1B = p.sbuf("XN1B", [128, NTILE * D], BF16)
    XN1FM = p.sbuf("XN1FM", [128, NCHUNK * T], BF16)
    xsq = p.sbuf("xsq", [128, D], F32)
    s1c = p.sbuf("s1c", [128, 1], F32)
    s2c = p.sbuf("s2c", [128, 1], F32)
    mcol = p.sbuf("mcol", [128, 1], F32)
    vcol = p.sbuf("vcol", [128, 1], F32)
    rcol = p.sbuf("rcol", [128, 1], F32)
    xnb_t = p.sbuf("xnb_t", [128, D], BF16)
    ps_tb = p.psum("ps_tb", [128, 128], BF16)
    tstg = p.sbuf("tstg", [128, 128], BF16)
    inv_d = 1.0 / D
    for tile in range(NTILE):
        t0 = tile * 128
        dma("sp", xstg.ap, X1.ap_[0, t0:t0 + 128, :], out_buf=xstg, in_buf=X1)
        emit("act", lambda e: e.square(out=xsq.ap, in_=xstg.ap),
             reads=[xstg], writes=[xsq])
        emit("dve", lambda e: e.reduce_sum(out=s1c.ap, in_=xstg.ap, axis=AX.X),
             reads=[xstg], writes=[s1c])
        emit("dve", lambda e: e.reduce_sum(out=s2c.ap, in_=xsq.ap, axis=AX.X),
             reads=[xsq], writes=[s2c])
        emit("dve", lambda e: e.tensor_scalar(out=mcol.ap, in0=s1c.ap,
             scalar1=inv_d, scalar2=None, op0=OP.mult), reads=[s1c], writes=[mcol])
        emit("dve", lambda e: e.tensor_scalar(out=vcol.ap, in0=s2c.ap,
             scalar1=inv_d, scalar2=None, op0=OP.mult), reads=[s2c], writes=[vcol])
        emit("dve", lambda e: e.tensor_tensor(out=rcol.ap, in0=mcol.ap,
             in1=mcol.ap, op=OP.mult), reads=[mcol], writes=[rcol])
        emit("dve", lambda e: e.tensor_tensor(out=vcol.ap, in0=vcol.ap,
             in1=rcol.ap, op=OP.subtract), reads=[vcol, rcol], writes=[vcol])
        emit("act", lambda e: e.activation(out=rcol.ap, in_=vcol.ap,
             func=AF.Sqrt, bias=eps5.ap), reads=[vcol, eps5], writes=[rcol])
        emit("dve", lambda e: e.reciprocal(out=rcol.ap, in_=rcol.ap),
             reads=[rcol], writes=[rcol])
        emit("dve", lambda e, tile=tile: e.tensor_scalar(
             out=XN1B[:, tile * D:(tile + 1) * D], in0=xstg.ap,
             scalar1=mcol.ap, scalar2=rcol.ap, op0=OP.subtract, op1=OP.mult),
             reads=[xstg, mcol, rcol], writes=[XN1B], no_waw=True)
        emit("dve", lambda e, tile=tile: e.tensor_copy(out=xnb_t.ap,
             in_=XN1B[:, tile * D:(tile + 1) * D]), reads=[XN1B], writes=[xnb_t])
        for c in range(NCHUNK):
            emit("pe", lambda e, c=c: e.transpose(ps_tb.ap,
                 xnb_t[:, c * 128:(c + 1) * 128], idb.ap),
                 reads=[xnb_t, idb], writes=[ps_tb])
            emit("dve", lambda e: e.tensor_copy(out=tstg.ap, in_=ps_tb.ap),
                 reads=[ps_tb], writes=[tstg])
            emit("act", lambda e, c=c, tile=tile: e.copy(
                 out=XN1FM[:, c * T + tile * 128: c * T + (tile + 1) * 128],
                 in_=tstg.ap), reads=[tstg], writes=[XN1FM], no_waw=True)

    sc = p.sbuf("sc", [128, T], F32)
    pexp = p.sbuf("pexp", [128, T], F32)
    rmx = p.sbuf("rmx", [128, 1], F32)
    nm8 = p.sbuf("nm8", [128, 1], F32)
    rsm = p.sbuf("rsm", [128, 1], F32)
    pb = p.sbuf("pb", [128, 128], BF16)
    pT = p.sbuf("pT", [128, 128], BF16)
    OATT = p.sbuf("OATT", [128, 4 * D], F32)
    acc1 = p.sbuf("acc1", [128, 4 * D], F32)
    wL = [p.sbuf(f"wL{i}", [128, 8 * 1024], BF16) for i in range(3)]
    ggt = p.sbuf("ggt", [128, 512], F32)
    sgt = p.sbuf("sgt", [128, 512], F32)
    gut = p.sbuf("gut", [128, 512], F32)
    inter = p.sbuf("inter", [128, 8 * 512], BF16)
    gt1 = p.sbuf("gt1", [128, D], F32)
    x1t = p.sbuf("x1t", [128, D], F32)
    psS = p.psum("psS", [128, 512], F32)
    psS2 = p.psum("psS2", [128, 512], F32)
    psV = p.psum("psV", [128, 64], F32)
    psM1 = p.psum("psM1", [128, 512], F32)
    psM2 = p.psum("psM2", [128, 512], F32)

    for q4 in range(nquarter):
        for h in range(nheads):
            ch, pb0 = h // 2, (h % 2) * 64
            for qi in range(4):
                qb = q4 * 4 + qi
                nk = (qb + 1) * 128
                nkg = (nk + 511) // 512
                for kg in range(nkg):
                    wdt = min(512, nk - kg * 512)
                    ps = psS if kg % 2 == 0 else psS2
                    emit("pe", lambda e, ch=ch, pb0=pb0, qb=qb, kg=kg, wdt=wdt, ps=ps:
                         e.matmul(ps[:, 0:wdt],
                         XN1FM[pb0:pb0 + 64, ch * T + qb * 128: ch * T + (qb + 1) * 128],
                         XN1FM[pb0:pb0 + 64, ch * T + kg * 512: ch * T + kg * 512 + wdt],
                         start=True, stop=True),
                         reads=[XN1FM], writes=[ps])
                    emit("dve", lambda e, kg=kg, wdt=wdt, ps=ps: e.tensor_copy(
                         out=sc[:, kg * 512: kg * 512 + wdt], in_=ps[:, 0:wdt]),
                         reads=[ps], writes=[sc], no_waw=(kg > 0))
                emit("dve", lambda e, nk=nk: e.reduce_max(out=rmx.ap,
                     in_=sc[:, 0:nk], axis=AX.X), reads=[sc], writes=[rmx])
                emit("dve", lambda e: e.tensor_scalar(out=nm8.ap, in0=rmx.ap,
                     scalar1=-0.125, scalar2=None, op0=OP.mult),
                     reads=[rmx], writes=[nm8])
                emit("act", lambda e, nk=nk: e.activation(out=pexp[:, 0:nk],
                     in_=sc[:, 0:nk], func=AF.Exp, bias=nm8.ap, scale=eighth.ap),
                     reads=[sc, nm8, eighth], writes=[pexp])
                emit("dve", lambda e, qb=qb: e.tensor_tensor(
                     out=pexp[:, qb * 128:(qb + 1) * 128],
                     in0=pexp[:, qb * 128:(qb + 1) * 128], in1=mli.ap, op=OP.mult),
                     reads=[pexp, mli], writes=[pexp], no_waw=True)
                emit("dve", lambda e, nk=nk: e.reduce_sum(out=rsm.ap,
                     in_=pexp[:, 0:nk], axis=AX.X), reads=[pexp], writes=[rsm])
                emit("dve", lambda e: e.reciprocal(out=rsm.ap, in_=rsm.ap),
                     reads=[rsm], writes=[rsm])
                for kb in range(qb + 1):
                    emit("act", lambda e, kb=kb: e.copy(out=pb.ap,
                         in_=pexp[:, kb * 128:(kb + 1) * 128]),
                         reads=[pexp], writes=[pb])
                    emit("pe", lambda e: e.transpose(ps_tb.ap, pb.ap, idb.ap),
                         reads=[pb, idb], writes=[ps_tb])
                    emit("dve", lambda e: e.tensor_copy(out=pT.ap, in_=ps_tb.ap),
                         reads=[ps_tb], writes=[pT])
                    emit("pe", lambda e, kb=kb, h=h, qb=qb: e.matmul(psV.ap,
                         pT.ap, XN1B[:, kb * D + h * 64: kb * D + (h + 1) * 64],
                         start=(kb == 0), stop=(kb == qb)),
                         reads=[pT, XN1B], writes=[psV])
                emit("dve", lambda e, qi=qi, h=h: e.tensor_scalar(
                     out=OATT[:, qi * D + h * 64: qi * D + (h + 1) * 64],
                     in0=psV.ap, scalar1=rsm.ap, scalar2=None, op0=OP.mult),
                     reads=[psV, rsm], writes=[OATT], no_waw=True)

        emit("dve", lambda e: e.memset(acc1.ap, 0.0), writes=[acc1])
        for ei in range(nexp):
            bgg = wL[(3 * ei) % 3]
            bgu = wL[(3 * ei + 1) % 3]
            bdn = wL[(3 * ei + 2) % 3]
            for c in range(NCHUNK):
                dma("sp", bgg[:, c * 1024:(c + 1) * 1024],
                    GG.ap_[ei * 1024 + c * 128: ei * 1024 + (c + 1) * 128, :],
                    out_buf=bgg, in_buf=GG, no_waw=(c > 0))
                dma("act", bgu[:, c * 1024:(c + 1) * 1024],
                    GU.ap_[ei * 1024 + c * 128: ei * 1024 + (c + 1) * 128, :],
                    out_buf=bgu, in_buf=GU, no_waw=(c > 0))
                dma("pool", bdn[:, c * 1024:(c + 1) * 1024],
                    DN.ap_[ei * 1024 + c * 128: ei * 1024 + (c + 1) * 128, :],
                    out_buf=bdn, in_buf=DN, no_waw=(c > 0))
            for fc in range(8):
                for k in range(NCHUNK):
                    emit("pe", lambda e, fc=fc, k=k, q4=q4, bgg=bgg: e.matmul(
                         psM1.ap, bgg[:, k * 1024 + fc * 128: k * 1024 + (fc + 1) * 128],
                         XN1FM[:, k * T + q4 * 512: k * T + (q4 + 1) * 512],
                         start=(k == 0), stop=(k == NCHUNK - 1)),
                         reads=[bgg, XN1FM], writes=[psM1])
                emit("dve", lambda e, fc=fc, ei=ei: e.tensor_scalar(out=ggt.ap,
                     in0=psM1.ap, scalar1=gbgc[:, fc * E + ei: fc * E + ei + 1],
                     scalar2=None, op0=OP.add),
                     reads=[psM1, gbgc], writes=[ggt])
                emit("dve", lambda e: e.tensor_scalar(out=ggt.ap, in0=ggt.ap,
                     scalar1=7.0, scalar2=None, op0=OP.min),
                     reads=[ggt], writes=[ggt])
                emit("act", lambda e: e.activation(out=sgt.ap, in_=ggt.ap,
                     func=AF.Sigmoid, scale=c1702.ap), reads=[ggt, c1702], writes=[sgt])
                emit("dve", lambda e: e.tensor_tensor(out=ggt.ap, in0=ggt.ap,
                     in1=sgt.ap, op=OP.mult), reads=[ggt, sgt], writes=[ggt])
                for k in range(NCHUNK):
                    emit("pe", lambda e, fc=fc, k=k, q4=q4, bgu=bgu: e.matmul(
                         psM2.ap, bgu[:, k * 1024 + fc * 128: k * 1024 + (fc + 1) * 128],
                         XN1FM[:, k * T + q4 * 512: k * T + (q4 + 1) * 512],
                         start=(k == 0), stop=(k == NCHUNK - 1)),
                         reads=[bgu, XN1FM], writes=[psM2])
                emit("dve", lambda e, fc=fc, ei=ei: e.tensor_scalar(out=gut.ap,
                     in0=psM2.ap, scalar1=gbuc[:, fc * E + ei: fc * E + ei + 1],
                     scalar2=None, op0=OP.add),
                     reads=[psM2, gbuc], writes=[gut])
                emit("dve", lambda e: e.tensor_scalar(out=gut.ap, in0=gut.ap,
                     scalar1=-7.0, scalar2=7.0, op0=OP.max, op1=OP.min),
                     reads=[gut], writes=[gut])
                emit("dve", lambda e: e.tensor_scalar(out=gut.ap, in0=gut.ap,
                     scalar1=1.0, scalar2=None, op0=OP.add),
                     reads=[gut], writes=[gut])
                emit("dve", lambda e, fc=fc: e.tensor_tensor(
                     out=inter[:, fc * 512:(fc + 1) * 512], in0=gut.ap,
                     in1=ggt.ap, op=OP.mult), reads=[gut, ggt], writes=[inter],
                     no_waw=True)
            for ti in range(4):
                for hh2 in range(2):
                    ps = psM1 if hh2 == 0 else psM2
                    for fc in range(8):
                        emit("pe", lambda e, ti=ti, hh2=hh2, fc=fc, ps=ps, bdn=bdn:
                             e.matmul(ps.ap,
                             inter[:, fc * 512 + ti * 128: fc * 512 + (ti + 1) * 128],
                             bdn[:, fc * 1024 + hh2 * 512: fc * 1024 + (hh2 + 1) * 512],
                             start=(fc == 0), stop=(fc == 7)),
                             reads=[inter, bdn], writes=[ps])
                    emit("dve", lambda e, ti=ti, hh2=hh2, ps=ps, ei=ei, q4=q4:
                         e.scalar_tensor_tensor(
                         out=acc1[:, ti * D + hh2 * 512: ti * D + (hh2 + 1) * 512],
                         in0=ps.ap,
                         scalar=C1[:, (q4 * 4 + ti) * E + ei: (q4 * 4 + ti) * E + ei + 1],
                         in1=acc1[:, ti * D + hh2 * 512: ti * D + (hh2 + 1) * 512],
                         op0=OP.mult, op1=OP.add),
                         reads=[ps, C1, acc1], writes=[acc1], no_waw=True)
        for ti in range(4):
            tile = q4 * 4 + ti
            t0 = tile * 128
            for hh2 in range(2):
                ps = psM1 if hh2 == 0 else psM2
                emit("pe", lambda e, tile=tile, hh2=hh2, ps=ps: e.matmul(ps.ap,
                     C1T[:, tile * 128:(tile + 1) * 128],
                     dbr[:, hh2 * 512:(hh2 + 1) * 512], start=True, stop=True),
                     reads=[C1T, dbr], writes=[ps])
                emit("dve", lambda e, ti=ti, hh2=hh2, ps=ps: e.tensor_tensor(
                     out=acc1[:, ti * D + hh2 * 512: ti * D + (hh2 + 1) * 512],
                     in0=ps.ap,
                     in1=acc1[:, ti * D + hh2 * 512: ti * D + (hh2 + 1) * 512],
                     op=OP.add), reads=[ps, acc1], writes=[acc1], no_waw=True)
            emit("act", lambda e, ti=ti: e.activation(out=gt1.ap,
                 in_=acc1[:, ti * D:(ti + 1) * D], func=AF.Silu),
                 reads=[acc1], writes=[gt1])
            dma("sp", x1t.ap, X1.ap_[0, t0:t0 + 128, :], out_buf=x1t, in_buf=X1)
            emit("dve", lambda e, ti=ti: e.tensor_tensor(out=gt1.ap, in0=gt1.ap,
                 in1=OATT[:, ti * D:(ti + 1) * D], op=OP.mult),
                 reads=[gt1, OATT], writes=[gt1])
            emit("dve", lambda e: e.tensor_scalar(out=gt1.ap, in0=gt1.ap,
                 scalar1=flagc.ap, scalar2=None, op0=OP.mult),
                 reads=[gt1, flagc], writes=[gt1])
            emit("dve", lambda e: e.tensor_tensor(out=gt1.ap, in0=gt1.ap,
                 in1=x1t.ap, op=OP.add), reads=[gt1, x1t], writes=[gt1])
            dma("act", OUT.ap_[0, t0:t0 + 128, :], gt1.ap, out_buf=OUT, in_buf=gt1)


def build_pe(nquarter=4, nexp=8, nheads=16):
    nc = bass.Bass("TRN2", target_bir_lowering=False, debug=False)
    io = _make_io(nc, _IO_PE)
    p = Prog(nc)
    with p.phase():
        emit_pe(p, io, nquarter, nexp, nheads)
    p.finish()
    return nc



_FUSED_EXTERNAL_IN = {
    "x", "Wq", "Wk", "Wv", "beta_w", "beta_b", "norm1_w", "alpha_logit",
    "ident_f32", "ident_bf16", "mask_su", "mask_sl", "mask_ui", "mask_li",
    "Wo", "norm2_w", "w_gate", "w_up", "w_down", "b1", "b2", "router_w",
    "iota14", "gb_g", "gb_u", "db", "router_indices", "routing_weights",
    "flagcol", "w1", "w2", "gug", "guu", "dn",
}


def build_fused():
    nc = bass.Bass("TRN2", target_bir_lowering=False, debug=False)
    specs = {}
    for name, (R, C) in _P0_SPECS:
        specs[name] = ([R, C], F32)
        specs[name + "b"] = ([R, C], BF16)
    for spec in (_IO_P1, _IO_PDELTA, _IO_PC, _IO_PD, _IO_PE):
        for name, shape, dt, kind in spec:
            specs.setdefault(name, (shape, dt))
    io = {}
    for name, (shape, dt) in specs.items():
        if name == "OUT":
            kind = "ExternalOutput"
        elif name in _FUSED_EXTERNAL_IN:
            kind = "ExternalInput"
        else:
            kind = "Internal"
        io[name] = Buf(nc.dram_tensor(name, shape, dt, kind=kind), name)
    p = Prog(nc)
    with p.phase():
        emit_p0(p, io)
        emit_p1(p, io)
    with p.phase():
        emit_pdelta(p, io)
    with p.phase():
        emit_pc(p, io)
    with p.phase():
        emit_pd(p, io)
    with p.phase():
        emit_pe(p, io)
    p.finish()
    return nc


def kernel(**inputs):
    inp = {k: np.asarray(v) for k, v in inputs.items()}
    consts = _host_consts()
    asc = np.ascontiguousarray
    cores = list(range(_NCORES))
    nc = build_fused()
    maps = []
    for c in cores:
        maps.append({
            "x": inp["L0_attn_x"][4 * c:4 * c + 4],
            "Wq": inp["L0_attn_Wq"], "Wk": inp["L0_attn_Wk"], "Wv": inp["L0_attn_Wv"],
            "beta_w": inp["L0_attn_beta_w"], "beta_b": inp["L0_attn_beta_b"],
            "norm1_w": inp["L0_attn_norm1_w"], "alpha_logit": inp["L0_attn_alpha_logit"],
            "Wo": inp["L0_attn_Wo"], "norm2_w": inp["L0_attn_norm2_w"],
            "w_gate": inp["L0_attn_w_gate"], "w_up": inp["L0_attn_w_up"],
            "w_down": inp["L0_attn_w_down"],
            "w1": inp["L0_ffn_w1"].reshape(14 * 1024, 2048),
            "w2": inp["L0_ffn_w2"].reshape(14 * 2048, 1024),
            "b1": inp["L0_ffn_b1"], "b2": inp["L0_ffn_b2"],
            "router_w": inp["L0_ffn_router_w"],
            "gug": asc(inp["L1_ffn_gate_up_proj"][:, :, 0::2]).reshape(8 * 1024, 1024),
            "guu": asc(inp["L1_ffn_gate_up_proj"][:, :, 1::2]).reshape(8 * 1024, 1024),
            "dn": inp["L1_ffn_down_proj"].reshape(8 * 1024, 1024),
            "gb_g": asc(inp["L1_ffn_gate_up_proj_bias"][:, 0::2]),
            "gb_u": asc(inp["L1_ffn_gate_up_proj_bias"][:, 1::2]),
            "db": inp["L1_ffn_down_proj_bias"],
            "router_indices": inp["L1_ffn_router_indices"],
            "routing_weights": inp["L1_ffn_routing_weights"],
            "flagcol": np.full((128, 1), 1.0 if c == 0 else 0.0, np.float32),
            "ident_f32": consts["ident_f32"], "ident_bf16": consts["ident_bf16"],
            "mask_su": np.triu(np.ones((128, 128), np.float32), 1),
            "mask_sl": np.tril(np.ones((128, 128), np.float32), -1),
            "mask_ui": np.triu(np.ones((128, 128), np.float32), 0),
            "mask_li": np.tril(np.ones((128, 128), np.float32)),
            "iota14": np.broadcast_to(np.arange(14, dtype=np.float32), (128, 14)).copy(),
        })
    res = run_bass_kernel_spmd(nc, maps, core_ids=cores)
    return np.concatenate([res.results[c]["OUT"] for c in cores], axis=0)




_NCORES = 8


def kernel_unfused(**inputs):
    inp = {k: np.asarray(v) for k, v in inputs.items()}
    consts = _host_consts()
    msu = np.triu(np.ones((128, 128), np.float32), 1)
    msl = np.tril(np.ones((128, 128), np.float32), -1)
    mui = np.triu(np.ones((128, 128), np.float32), 0)
    mli = np.tril(np.ones((128, 128), np.float32))
    iota14 = np.broadcast_to(np.arange(14, dtype=np.float32), (128, 14)).copy()
    cores = list(range(_NCORES))
    asc = np.ascontiguousarray

    nc = build_p0()
    m0 = {
        "w1": inp["L0_ffn_w1"].reshape(14 * 1024, 2048),
        "w2": inp["L0_ffn_w2"].reshape(14 * 2048, 1024),
        "gug": asc(inp["L1_ffn_gate_up_proj"][:, :, 0::2]).reshape(8 * 1024, 1024),
        "guu": asc(inp["L1_ffn_gate_up_proj"][:, :, 1::2]).reshape(8 * 1024, 1024),
        "dn": inp["L1_ffn_down_proj"].reshape(8 * 1024, 1024),
    }
    r0 = run_bass_kernel_spmd(nc, [m0] * _NCORES, core_ids=cores).results

    nc = build_p1()
    maps = []
    for c in cores:
        maps.append({
            "x": inp["L0_attn_x"][4 * c:4 * c + 4],
            "Wq": inp["L0_attn_Wq"], "Wk": inp["L0_attn_Wk"], "Wv": inp["L0_attn_Wv"],
            "beta_w": inp["L0_attn_beta_w"], "beta_b": inp["L0_attn_beta_b"],
            "norm1_w": inp["L0_attn_norm1_w"], "alpha_logit": inp["L0_attn_alpha_logit"],
            "ident_f32": consts["ident_f32"], "ident_bf16": consts["ident_bf16"],
        })
    r1 = run_bass_kernel_spmd(nc, maps, core_ids=cores).results

    nc = build_pdelta()
    maps = []
    for c in cores:
        maps.append({
            "Q_FM": r1[c]["Q_FM"], "K_FM": r1[c]["K_FM"], "K_TM": r1[c]["K_TM"],
            "V_TM": r1[c]["V_TM"], "BETA": r1[c]["BETA"], "RSTD2": r1[c]["RSTD2"],
            "ident_bf16": consts["ident_bf16"],
            "mask_su": msu, "mask_sl": msl, "mask_ui": mui,
        })
    r2 = run_bass_kernel_spmd(nc, maps, core_ids=cores).results

    nc = build_pc()
    maps = []
    for c in cores:
        maps.append({
            "XN_FM": r1[c]["XN_FM"], "O_FM": r2[c]["O_FM"],
            "Wo": inp["L0_attn_Wo"], "norm2_w": inp["L0_attn_norm2_w"],
            "w_gate": inp["L0_attn_w_gate"], "w_up": inp["L0_attn_w_up"],
            "w_down": inp["L0_attn_w_down"], "ident_f32": consts["ident_f32"],
        })
    r3 = run_bass_kernel_spmd(nc, maps, core_ids=cores).results

    nc = build_pd()
    maps = []
    for c in cores:
        maps.append({
            "XN_FM": r1[c]["XN_FM"], "S_FM": r3[c]["S_FM"],
            "x": inp["L0_attn_x"][4 * c:4 * c + 4],
            "w1b": r0[c]["w1b"], "w2b": r0[c]["w2b"],
            "b1": inp["L0_ffn_b1"], "b2": inp["L0_ffn_b2"],
            "router_w": inp["L0_ffn_router_w"],
            "ident_f32": consts["ident_f32"], "iota14": iota14,
        })
    r4 = run_bass_kernel_spmd(nc, maps, core_ids=cores).results

    nc = build_pe()
    maps = []
    for c in cores:
        flag = np.full((128, 1), 1.0 if c == 0 else 0.0, np.float32)
        maps.append({
            "X1": r4[c]["X1"],
            "gugb": r0[c]["gugb"], "guub": r0[c]["guub"], "dnb": r0[c]["dnb"],
            "gb_g": asc(inp["L1_ffn_gate_up_proj_bias"][:, 0::2]),
            "gb_u": asc(inp["L1_ffn_gate_up_proj_bias"][:, 1::2]),
            "db": inp["L1_ffn_down_proj_bias"],
            "router_indices": inp["L1_ffn_router_indices"],
            "routing_weights": inp["L1_ffn_routing_weights"],
            "flagcol": flag,
            "ident_f32": consts["ident_f32"], "ident_bf16": consts["ident_bf16"],
            "mask_li": mli,
        })
    r5 = run_bass_kernel_spmd(nc, maps, core_ids=cores).results

    return np.concatenate([r5[c]["OUT"] for c in cores], axis=0)
